# version 04
# baseline (speedup 1.0000x reference)
import numpy as np
import ml_dtypes
from contextlib import ExitStack
import concourse.bass as bass
import concourse.mybir as mybir
from concourse.bass_utils import run_bass_kernel_spmd

F32 = mybir.dt.float32
BF16 = mybir.dt.bfloat16
AF = mybir.ActivationFunctionType
ALU = mybir.AluOpType
AX = mybir.AxisListType
bfd = ml_dtypes.bfloat16

B, S, D = 4, 2048, 2048
H, KV, HD, RD = 16, 4, 128, 64
FULL = HD + RD
NQ = 1024
F2 = 4096
MROUND = float(np.float32(1.5 * 2.0 ** 23))
EPS_LN = 1e-5
EPS_Q = 1e-5
QBITS = 127.0
SCALE_ATT = 1.0 / float(np.sqrt(FULL))
C1 = 1.0 / float(np.sqrt(D))
C2 = 1.0 / float(np.sqrt(F2))


def _hadamard(n):
    h = np.array([[1.0]], dtype=np.float64)
    while h.shape[0] < n:
        h = np.block([[h, h], [h, -h]])
    return h.astype(np.float32)


def _rope_tables(positions):
    fr = (1.0 / 10000.0 ** (np.arange(0, RD, 2, dtype=np.float64) / RD))
    ang = np.outer(fr, positions.astype(np.float64))
    ang = np.tile(ang, (4, 1))
    return np.cos(ang).astype(np.float32), np.sin(ang).astype(np.float32)


def _core_qrows(stripe):
    if stripe == 0:
        return np.arange(0, 512), np.arange(1536, 2048)
    return np.arange(512, 1024), np.arange(1024, 1536)


def _core_masks(stripe):
    qb0, qb1 = _core_qrows(stripe)
    m = np.zeros((128, 16 * 512), dtype=np.float32)
    for i in range(8):
        kpos = i * 128 + np.arange(128)
        m[:, i * 512:(i + 1) * 512] = (kpos[:, None] <= qb0[None, :])
    for i in range(8):
        kpos = (8 + i) * 128 + np.arange(128)
        m[:, (8 + i) * 512:(9 + i) * 512] = (kpos[:, None] <= qb1[None, :])
    return m.astype(bfd)


def _col_perm_rope_q():
    even, odd = [], []
    for g in range(4):
        for p in range(128):
            h = (p // 32) * 4 + g
            f = p % 32
            even.append(h * RD + 2 * f)
            odd.append(h * RD + 2 * f + 1)
    return np.array(even), np.array(odd)


def _col_perm_rope_k():
    even = [(p // 32) * RD + 2 * (p % 32) for p in range(128)]
    odd = [c + 1 for c in even]
    return np.array(even), np.array(odd)


class Alloc:
    def __init__(self, nc):
        self.nc = nc
        base = (nc.sbuf_base + 31) & ~31
        self.free = [(base, nc.sbuf_top - base)]
        self.live = {}

    def alloc(self, name, shape, dt):
        esz = 2 if dt == BF16 else 4
        nbytes = (int(np.prod(shape[1:])) * esz + 31) & ~31
        cands = [(sz, off, i) for i, (off, sz) in enumerate(self.free) if sz >= nbytes]
        cands.sort()
        for sz, off, i in cands[:1]:
            if True:
                t = self.nc.alloc_sbuf_tensor_at(name, list(shape), dt, offset=off)
                self.live[name] = (off, nbytes)
                if sz == nbytes:
                    self.free.pop(i)
                else:
                    self.free[i] = (off + nbytes, sz - nbytes)
                return t
        raise MemoryError(f"SBUF alloc fail {name} {nbytes}B free={self.free}")

    def release(self, *names):
        for name in names:
            off, nbytes = self.live.pop(name)
            self.free.append((off, nbytes))
        self.free.sort()
        merged = []
        for o, s in self.free:
            if merged and merged[-1][0] + merged[-1][1] == o:
                merged[-1][1] += s
            else:
                merged.append([o, s])
        self.free = [tuple(x) for x in merged]


class Sched:
    ENGS = ("t", "v", "s", "g", "y")

    def __init__(self):
        self.q = {e: [] for e in self.ENGS}
        self.cnt = {}
        self.sems = {}
        self.dirty = {e: False for e in self.ENGS}
        self.last = {e: None for e in self.ENGS}

    def add_sem(self, name, sem):
        self.sems[name] = sem
        self.cnt[name] = 0

    def ev(self, sname):
        return (sname, self.cnt[sname])

    def op(self, eng, fn, wait=(), inc=None, dma=False):
        incs = []
        wait = tuple(w for w in wait if w is not None)
        if inc:
            for sname in ([inc] if isinstance(inc, str) else inc):
                self.cnt[sname] += 16 if dma else 1
                incs.append((sname, 16 if dma else 1))
        self.q[eng].append((wait, fn, tuple(incs)))
        if incs:
            self.dirty[eng] = False
            nm = incs[-1][0]
            self.last[eng] = (nm, self.cnt[nm])
            return (nm, self.cnt[nm])
        self.dirty[eng] = True
        return None

    def barrier(self, engs=None):
        engs = engs or [e for e in self.ENGS if self.q[e]]
        evs = []
        for e in engs:
            if self.dirty[e]:
                raise RuntimeError(f"barrier: engine {e} has trailing ops without inc")
            if self.last[e] is not None:
                evs.append(self.last[e])
        for e in engs:
            self.q[e].append((tuple(evs), None, ()))

    def wait(self, eng, *evs):
        self.q[eng].append((tuple(e for e in evs if e), None, ()))

    def replay(self, key, engine):
        for waits, fn, incs in self.q[key]:
            for (sname, val) in waits:
                engine.wait_ge(self.sems[sname], val)
            if fn is not None:
                ins = fn(engine)
                for (sname, step) in incs:
                    ins = ins.then_inc(self.sems[sname], step)


class Rot:
    def __init__(self, n):
        self.n = n
        self.ev = [None] * n

    def wait(self, i):
        e = self.ev[i % self.n]
        return (e,) if e is not None else ()

    def set(self, i, e):
        self.ev[i % self.n] = e


def build_program(dbg=None):
    nc = bass.Bass("TRN2", target_bir_lowering=False, debug=False)
    ctx = ExitStack()

    din = {}
    for name, shp, dt in [
        ("x", [S, D], F32), ("xq", [NQ, D], F32),
        ("wq", [D, H * HD], F32), ("wqe", [D, 512], F32), ("wqo", [D, 512], F32),
        ("wkvc", [D, 512], F32), ("wkdec", [512, 512], F32), ("wvdec", [512, 512], F32),
        ("wke", [512, 128], F32), ("wko", [512, 128], F32),
        ("wo", [2048, D], F32),
        ("wupt", [D, F2], F32), ("wdnt", [F2, D], F32),
        ("cosq", [128, NQ], F32), ("sinq", [128, NQ], F32),
        ("cosk", [128, S], F32), ("sink", [128, S], F32),
        ("mask", [128, 16 * 512], BF16),
    ]:
        din[name] = nc.dram_tensor(name, shp, dt, kind="ExternalInput").ap()
    dout = nc.dram_tensor("out", [NQ, D], F32, kind="ExternalOutput").ap()
    d_yln = nc.dram_tensor("yln_scratch", [NQ, D], F32, kind="Internal").ap()
    ddbg = {}

    def dbg_out(name, shp, dt=F32):
        ddbg[name] = nc.dram_tensor(name, shp, dt, kind="ExternalOutput").ap()
        return ddbg[name]

    h128_bf = nc.inline_tensor(_hadamard(128).astype(bfd), name="h128bf").ap()
    h128_f = nc.inline_tensor(_hadamard(128), name="h128f").ap()
    id_bf = nc.inline_tensor(np.eye(128, dtype=bfd), name="idbf").ap()
    id_f = nc.inline_tensor(np.eye(128, dtype=np.float32), name="idf").ap()
    ones_c_bf = nc.inline_tensor(np.ones((128, 1), dtype=bfd), name="onescbf").ap()
    ones_r_f = nc.inline_tensor(np.ones((1, 128), dtype=np.float32), name="onesrf").ap()
    sel8_np = np.zeros((8, 8 * 128), dtype=np.float32)
    for _j in range(8):
        sel8_np[_j, _j * 128:(_j + 1) * 128] = 1.0
    sel8_c = nc.inline_tensor(sel8_np, name="sel8c").ap()

    al = Alloc(nc)
    SBA = al.alloc
    pbank = [ctx.enter_context(nc.psum_tensor(f"pb{i}", [128, 512], F32)) for i in range(8)]

    S_ = Sched()
    for sname in ("ydma", "gdma", "sdma", "t", "v", "s", "g", "o"):
        S_.add_sem(sname, ctx.enter_context(nc.semaphore(sname)))

    t_cosq = SBA("m_cosq", [128, NQ], F32)
    t_sinq = SBA("m_sinq", [128, NQ], F32)
    t_h128bf = SBA("m_h128bf", [128, 128], BF16)
    t_h128f = SBA("m_h128f", [128, 128], F32)
    t_idbf = SBA("m_idbf", [128, 128], BF16)
    t_idf = SBA("m_idf", [128, 128], F32)
    t_ones_c = SBA("m_onescbf", [128, 1], BF16)
    t_ones_r = SBA("m_onesrf", [1, 128], F32)
    t_st = SBA("m_stats", [128, 1024], F32)
    t_st2 = SBA("m_stats2", [128, 128], F32)
    t_trbf = SBA("m_trbf", [128, 1024], BF16)
    t_q1m = SBA("m_q1m", [128, 16], F32)
    t_d2 = SBA("m_d2", [128, 16], F32)
    t_q2m = SBA("m_q2m", [128, 16], F32)
    t_dq2 = SBA("m_dq2", [128, 16], F32)
    t_wupb = SBA("m_wupb", [128, 1], F32)
    t_rupb = SBA("m_rupb", [128, 1], F32)
    t_wdnb = SBA("m_wdnb", [128, 1], F32)
    t_rdnb = SBA("m_rdnb", [128, 1], F32)
    t_wsc = SBA("m_wsc", [1, 8], F32)
    t_row = [SBA(f"m_row{i}", [1, 512], F32) for i in range(2)]
    t_B = SBA("m_B", [128, 1024], F32)
    t_rbs = [SBA(f"m_rbs{i}", [128, 512], F32) for i in range(2)]
    t_rl = SBA("m_rl", [128, 512], BF16)
    t_sel8 = SBA("m_sel8", [8, 8 * 128], F32)
    t_rowT = SBA("m_rowT", [8, 128], F32)
    t_gapa = SBA("m_gapa", [128, 8], F32)
    t_gapb = SBA("m_gapb", [128, 8], F32)

    def vg():
        return S_.op("v", lambda e: e.tensor_copy(t_gapa[:], t_gapb[:]), inc="v")

    def tvop(vfn, psum_ap, shadow, wait=()):
        S_.op("v", vfn, wait=wait, inc="v")
        ev = vg()
        return S_.op("s", lambda e, s=shadow, p=psum_ap: e.copy(s, p), wait=(ev,), inc="s")

    for dst, src in [(t_cosq[:], din["cosq"]), (t_sinq[:], din["sinq"]),
                     (t_h128bf[:], h128_bf), (t_h128f[:], h128_f), (t_idbf[:], id_bf),
                     (t_idf[:], id_f), (t_ones_c[:], ones_c_bf), (t_ones_r[:], ones_r_f),
                     (t_sel8[:], sel8_c)]:
        S_.op("s", lambda e, d=dst, s=src: e.dma_start(out=d, in_=s), inc="sdma", dma=True)
    ev_perm = S_.ev("sdma")
    for _e in ("t", "v", "g", "y"):
        S_.wait(_e, ev_perm)

    def ln_inplace(yw, ev_in):
        ch = [ev_in]

        def vq(fn, extra=()):
            S_.op("v", fn, wait=(ch[0],) + tuple(extra), inc="v")
            ch[0] = vg()
            return ch[0]

        for seg in range(4):
            vq(lambda e, seg=seg: e.bn_stats(
                t_st[:, seg * 6:(seg + 1) * 6], yw[:, seg * 512:(seg + 1) * 512]))
        stv = t_st[:, 0:24].rearrange("p (g c) -> p g c", c=3)
        vq(lambda e: e.reduce_sum(t_st[:, 24:25], stv[:, :, 1], axis=AX.X))
        vq(lambda e: e.tensor_scalar(t_st[:, 25:26], t_st[:, 24:25], 1.0 / 8.0, None, ALU.mult))
        vq(lambda e: e.reduce_sum(t_st[:, 26:27], stv[:, :, 2], axis=AX.X))
        vq(lambda e: e.tensor_tensor(t_st2[:, 0:8], stv[:, :, 1], stv[:, :, 1], ALU.mult))
        vq(lambda e: e.reduce_sum(t_st[:, 27:28], t_st2[:, 0:8], axis=AX.X))
        vq(lambda e: e.tensor_scalar(t_st[:, 27:28], t_st[:, 27:28], 256.0, None, ALU.mult))
        vq(lambda e: e.tensor_add(t_st[:, 26:27], t_st[:, 26:27], t_st[:, 27:28]))
        vq(lambda e: e.tensor_scalar(t_st[:, 26:27], t_st[:, 26:27], 1.0 / 2048.0, None, ALU.mult))
        vq(lambda e: e.tensor_tensor(t_st[:, 28:29], t_st[:, 25:26], t_st[:, 25:26], ALU.mult))
        vq(lambda e: e.tensor_sub(t_st[:, 26:27], t_st[:, 26:27], t_st[:, 28:29]))
        vq(lambda e: e.tensor_scalar(t_st[:, 26:27], t_st[:, 26:27], EPS_LN, None, ALU.add))
        ev_v = vq(lambda e: e.reciprocal(t_st[:, 29:30], t_st[:, 26:27]))
        ev_sq = S_.op("s", lambda e: e.activation(t_st[:, 30:31], t_st[:, 29:30], AF.Sqrt),
                      wait=(ev_v,), inc="s")
        ch[0] = ev_sq
        vq(lambda e: e.tensor_tensor(t_st[:, 31:32], t_st[:, 30:31], t_st[:, 30:31], ALU.mult))
        vq(lambda e: e.tensor_mul(t_st[:, 31:32], t_st[:, 31:32], t_st[:, 26:27]))
        vq(lambda e: e.tensor_scalar(t_st[:, 31:32], t_st[:, 31:32], -1.0, 3.0, ALU.mult, ALU.add))
        vq(lambda e: e.tensor_mul(t_st[:, 31:32], t_st[:, 31:32], t_st[:, 30:31]))
        vq(lambda e: e.tensor_scalar(t_st[:, 32:33], t_st[:, 31:32], 0.5, None, ALU.mult))
        vq(lambda e: e.tensor_mul(t_st[:, 33:34], t_st[:, 25:26], t_st[:, 32:33]))
        vq(lambda e: e.tensor_scalar(t_st[:, 33:34], t_st[:, 33:34], -1.0, None, ALU.mult))
        return vq(lambda e: e.scalar_tensor_tensor(
            yw[:], yw[:], t_st[:, 32:33], t_st[:, 33:34].broadcast_to([128, 2048]),
            ALU.mult, ALU.add))

    def bcast_col(src11, dst_col, bank):
        ev_mm = S_.op("t", lambda e: e.matmul(pbank[bank][:, 0:1], t_ones_r[:], src11,
                                              start=True, stop=True), inc="t")
        return S_.op("v", lambda e: e.tensor_copy(dst_col, pbank[bank][:, 0:1]),
                     wait=(ev_mm,), inc="v")

    t_xT = SBA("a_xT", [128, 16 * 2048], BF16)
    t_stage = [SBA(f"a_stg{i}", [128, 2048], BF16) for i in range(2)]
    t_wkvc = SBA("a_wkvc", [128, 16 * 512], BF16)
    t_cT = SBA("a_cT", [128, 4 * 2048], BF16)

    ev_wkvc = S_.op("g", lambda e: e.dma_start(
        out=t_wkvc[:].rearrange("p (dc c) -> p dc c", dc=16),
        in_=din["wkvc"].rearrange("(dc p) c -> p dc c", p=128)), inc="gdma", dma=True)

    def stream_transpose(src_dram, nrow_tiles, t_dst, chunk_stride):
        rot_stage = Rot(2)
        rot_bank = Rot(4)
        last = None
        for r in range(nrow_tiles):
            st = t_stage[r % 2]
            ev_ld = S_.op("g", lambda e, st=st, r=r: e.dma_start(
                out=st[:], in_=src_dram[r * 128:(r + 1) * 128, :]),
                wait=rot_stage.wait(r), inc="gdma", dma=True)
            for cg in range(4):
                bank = 2 * (r % 2) + (cg % 2)
                ev_t = None
                for ci in range(4):
                    c = cg * 4 + ci
                    w2 = ((ev_ld,) if (cg == 0 and ci == 0) else ())
                    if ci == 0:
                        w2 = w2 + rot_bank.wait(bank)
                    ev_t = S_.op("t", lambda e, st=st, c=c, bank=bank, ci=ci: e.transpose(
                        pbank[bank][:].bitcast(BF16)[:, ci * 128:(ci + 1) * 128],
                        st[:, c * 128:(c + 1) * 128], t_idbf[:]),
                        wait=w2, inc="t" if ci == 3 else None)
                dst = t_dst[:].rearrange("p (dc t) -> p dc t", t=chunk_stride)[
                    :, cg * 4:(cg + 1) * 4, r * 128:(r + 1) * 128]
                ev_e = S_.op("v", lambda e, dst=dst, bank=bank: e.tensor_copy(
                    dst, pbank[bank][:].bitcast(BF16)[:, 0:512]), wait=(ev_t,), inc="v")
                rot_bank.set(bank, ev_e)
                last = ev_e
            rot_stage.set(r, last)
        return last

    ev_xT = stream_transpose(din["x"], 16, t_xT, 2048)
    S_.barrier(["t", "v", "g", "s"])

    rot_c = Rot(2)
    for tcol in range(4):
        for cf in range(4):
            it = tcol * 4 + cf
            bank = 4 + it % 2
            ev_mm = None
            for dc in range(16):
                w = rot_c.wait(it) if dc == 0 else ()
                ev_mm = S_.op("t", lambda e, bank=bank, dc=dc, cf=cf, tcol=tcol: e.matmul(
                    pbank[bank][:],
                    t_wkvc[:, dc * 512 + cf * 128: dc * 512 + (cf + 1) * 128],
                    t_xT[:, dc * 2048 + tcol * 512: dc * 2048 + (tcol + 1) * 512],
                    start=(dc == 0), stop=(dc == 15)),
                    wait=w, inc="t" if dc == 15 else None)
            ev_ev = S_.op("s", lambda e, bank=bank, cf=cf, tcol=tcol: e.copy(
                t_cT[:, cf * 2048 + tcol * 512: cf * 2048 + (tcol + 1) * 512], pbank[bank][:]),
                wait=(ev_mm,), inc="s")
            rot_c.set(it, ev_ev)
    S_.barrier(["t", "s", "v", "g"])
    al.release("a_xT", "a_wkvc")

    t_kn = SBA("a_kn", [128, KV * 2048], BF16)
    t_vtm = SBA("a_vtm", [128, 16 * 512], BF16)
    t_ke = SBA("a_ke", [128, 2048], BF16)
    t_ko = SBA("a_ko", [128, 2048], BF16)
    t_wkdec = SBA("a_wkdec", [128, 4 * 512], BF16)
    t_wvdec = SBA("a_wvdec", [128, 4 * 512], BF16)
    t_wke = SBA("a_wke", [128, 4 * 128], BF16)
    t_wko = SBA("a_wko", [128, 4 * 128], BF16)
    t_cosk = SBA("a_cosk", [128, S], F32)
    t_sink = SBA("a_sink", [128, S], F32)
    t_ker = SBA("a_ker", [128, 2048], F32)
    t_kor = SBA("a_kor", [128, 2048], F32)
    t_rs1 = SBA("a_rs1", [128, 2048], F32)
    t_rs2 = SBA("a_rs2", [128, 2048], F32)

    for dst, src in [(t_cosk[:], din["cosk"]), (t_sink[:], din["sink"])]:
        S_.op("s", lambda e, d=dst, s=src: e.dma_start(out=d, in_=s), inc="sdma", dma=True)
    for t_w, dname in [(t_wkdec, "wkdec"), (t_wvdec, "wvdec"), (t_wke, "wke"), (t_wko, "wko")]:
        S_.op("g", lambda e, t_w=t_w, dname=dname: e.dma_start(
            out=t_w[:].rearrange("p (cc c) -> p cc c", cc=4),
            in_=din[dname].rearrange("(cc p) c -> p cc c", p=128)), inc="gdma", dma=True)
    ev_kw = S_.ev("gdma")
    ev_kcs = S_.ev("sdma")

    rot_k = Rot(4)
    it = 0
    for kv in range(KV):
        for kcol in range(4):
            bank = it % 4
            ev_mm = None
            for cc in range(4):
                w = rot_k.wait(it) if cc == 0 else ()
                if it == 0 and cc == 0:
                    w = w + (ev_kw,)
                ev_mm = S_.op("t", lambda e, bank=bank, cc=cc, kv=kv, kcol=kcol: e.matmul(
                    pbank[bank][:],
                    t_wkdec[:, cc * 512 + kv * 128: cc * 512 + (kv + 1) * 128],
                    t_cT[:, cc * 2048 + kcol * 512: cc * 2048 + (kcol + 1) * 512],
                    start=(cc == 0), stop=(cc == 3)),
                    wait=w, inc="t" if cc == 3 else None)
            ev_ev = S_.op("s", lambda e, bank=bank, kv=kv, kcol=kcol: e.copy(
                t_kn[:, kv * 2048 + kcol * 512: kv * 2048 + (kcol + 1) * 512], pbank[bank][:]),
                wait=(ev_mm,), inc="s")
            rot_k.set(it, ev_ev)
            it += 1
    for kt in range(16):
        bank = it % 4
        ev_mm = None
        for cc in range(4):
            w = rot_k.wait(it) if cc == 0 else ()
            ev_mm = S_.op("t", lambda e, bank=bank, cc=cc, kt=kt: e.matmul(
                pbank[bank][:],
                t_cT[:, cc * 2048 + kt * 128: cc * 2048 + kt * 128 + 128],
                t_wvdec[:, cc * 512:(cc + 1) * 512],
                start=(cc == 0), stop=(cc == 3)),
                wait=w, inc="t" if cc == 3 else None)
        ev_ev = S_.op("s", lambda e, bank=bank, kt=kt: e.copy(
            t_vtm[:, kt * 512:(kt + 1) * 512], pbank[bank][:]), wait=(ev_mm,), inc="s")
        rot_k.set(it, ev_ev)
        it += 1
    for t_dst, t_w in [(t_ker, t_wke), (t_kor, t_wko)]:
        for kcol in range(4):
            bank = it % 4
            ev_mm = None
            for cc in range(4):
                w = rot_k.wait(it) if cc == 0 else ()
                ev_mm = S_.op("t", lambda e, bank=bank, cc=cc, kcol=kcol, t_w=t_w: e.matmul(
                    pbank[bank][:],
                    t_w[:, cc * 128:(cc + 1) * 128],
                    t_cT[:, cc * 2048 + kcol * 512: cc * 2048 + (kcol + 1) * 512],
                    start=(cc == 0), stop=(cc == 3)),
                    wait=w, inc="t" if cc == 3 else None)
            ev_ev = S_.op("s", lambda e, bank=bank, kcol=kcol, t_dst=t_dst: e.copy(
                t_dst[:, kcol * 512:(kcol + 1) * 512], pbank[bank][:]), wait=(ev_mm,), inc="s")
            rot_k.set(it, ev_ev)
            it += 1
    S_.barrier(["t", "s", "v", "g"])
    S_.wait("v", ev_kcs)
    _e = S_.op("v", lambda e: e.tensor_mul(t_rs1[:], t_ker[:], t_cosk[:]), inc="v")
    _e = S_.op("v", lambda e: e.tensor_mul(t_rs2[:], t_kor[:], t_sink[:]), wait=(_e,), inc="v")
    _e = S_.op("v", lambda e: e.tensor_sub(t_ke[:], t_rs1[:], t_rs2[:]), wait=(_e,), inc="v")
    _e = S_.op("v", lambda e: e.tensor_mul(t_rs1[:], t_ker[:], t_sink[:]), wait=(_e,), inc="v")
    _e = S_.op("v", lambda e: e.tensor_mul(t_rs2[:], t_kor[:], t_cosk[:]), wait=(_e,), inc="v")
    _e = S_.op("v", lambda e: e.tensor_add(t_ko[:], t_rs1[:], t_rs2[:]), wait=(_e,), inc="v")
    S_.barrier(["t", "s", "v", "g"])
    al.release("a_wkdec", "a_wvdec", "a_wke", "a_wko", "a_cosk", "a_sink",
               "a_ker", "a_kor", "a_rs1", "a_rs2", "a_cT")

    t_xqT = SBA("a_xqT", [128, 16 * NQ], BF16)
    t_qn = SBA("a_qn", [128, H * NQ], BF16)
    t_qe = SBA("a_qe", [128, 4 * NQ], BF16)
    t_qo = SBA("a_qo", [128, 4 * NQ], BF16)
    stream_transpose(din["xq"], 8, t_xqT, NQ)
    S_.barrier(["t", "v", "g"])
    al.release("a_stg0", "a_stg1")

    t_wqs = [SBA(f"a_wqs{i}", [128, 16 * 512], BF16) for i in range(2)]
    rot_q = Rot(4)
    it = 0
    ev_slab_used = [None, None]
    for slab in range(4):
        sb = t_wqs[slab % 2]
        ev_wq = S_.op("g", lambda e, sb=sb, slab=slab: e.dma_start(
            out=sb[:].rearrange("p (dc c) -> p dc c", dc=16),
            in_=din["wq"][:, slab * 512:(slab + 1) * 512].rearrange("(dc p) c -> p dc c", p=128)),
            wait=(ev_slab_used[slab % 2],) if ev_slab_used[slab % 2] else (), inc="gdma", dma=True)
        last_mm = None
        for hh in range(4):
            h = slab * 4 + hh
            for qcol in range(2):
                bank = it % 4
                ev_mm = None
                for dc in range(16):
                    w = rot_q.wait(it) if dc == 0 else ()
                    if dc == 0 and hh == 0 and qcol == 0:
                        w = w + (ev_wq,)
                    ev_mm = S_.op("t", lambda e, bank=bank, dc=dc, hh=hh, qcol=qcol, sb=sb: e.matmul(
                        pbank[bank][:],
                        sb[:, dc * 512 + hh * 128: dc * 512 + (hh + 1) * 128],
                        t_xqT[:, dc * NQ + qcol * 512: dc * NQ + (qcol + 1) * 512],
                        start=(dc == 0), stop=(dc == 15)),
                        wait=w, inc="t" if dc == 15 else None)
                last_mm = ev_mm
                ev_ev = S_.op("s", lambda e, bank=bank, h=h, qcol=qcol: e.copy(
                    t_qn[:, h * NQ + qcol * 512: h * NQ + (qcol + 1) * 512], pbank[bank][:]),
                    wait=(ev_mm,), inc="s")
                rot_q.set(it, ev_ev)
                it += 1
        ev_slab_used[slab % 2] = last_mm
    al.release("a_wqs0", "a_wqs1")
    t_wqe = SBA("a_wqe", [128, 16 * 512], BF16)
    t_wqo = SBA("a_wqo", [128, 16 * 512], BF16)
    t_re = t_B[:, 0:512]
    t_ro = t_B[:, 512:1024]
    t_m1 = t_st[:, 0:512]
    t_m2 = t_st[:, 512:1024]
    ev_we = S_.op("g", lambda e: e.dma_start(
        out=t_wqe[:].rearrange("p (dc c) -> p dc c", dc=16),
        in_=din["wqe"].rearrange("(dc p) c -> p dc c", p=128)),
        wait=(ev_slab_used[0],), inc="gdma", dma=True)
    ev_wo_ = S_.op("g", lambda e: e.dma_start(
        out=t_wqo[:].rearrange("p (dc c) -> p dc c", dc=16),
        in_=din["wqo"].rearrange("(dc p) c -> p dc c", p=128)),
        wait=(ev_slab_used[1],), inc="gdma", dma=True)
    ev_cmb = None
    for g in range(4):
        for qcol in range(2):
            evs = []
            for side, (t_w, t_raw, evw) in enumerate([(t_wqe, t_re, ev_we), (t_wqo, t_ro, ev_wo_)]):
                bank = (it + side) % 4
                ev_mm = None
                for dc in range(16):
                    w = rot_q.wait(it + side) if dc == 0 else ()
                    if dc == 0 and g == 0 and qcol == 0:
                        w = w + (evw,)
                    ev_mm = S_.op("t", lambda e, bank=bank, dc=dc, g=g, qcol=qcol, t_w=t_w: e.matmul(
                        pbank[bank][:],
                        t_w[:, dc * 512 + g * 128: dc * 512 + (g + 1) * 128],
                        t_xqT[:, dc * NQ + qcol * 512: dc * NQ + (qcol + 1) * 512],
                        start=(dc == 0), stop=(dc == 15)),
                        wait=w, inc="t" if dc == 15 else None)
                ev_ev = S_.op("s", lambda e, bank=bank, t_raw=t_raw: e.copy(
                    t_raw, pbank[bank][:]),
                    wait=(ev_mm,) + ((ev_cmb,) if ev_cmb else ()), inc="s")
                evs.append(ev_ev)
                rot_q.set(it + side, ev_ev)
            cqs = t_cosq[:, qcol * 512:(qcol + 1) * 512]
            sqs = t_sinq[:, qcol * 512:(qcol + 1) * 512]
            qeo = t_qe[:, g * NQ + qcol * 512: g * NQ + (qcol + 1) * 512]
            qoo = t_qo[:, g * NQ + qcol * 512: g * NQ + (qcol + 1) * 512]
            _e = S_.op("v", lambda e, cqs=cqs: e.tensor_tensor(t_m1, t_re, cqs, ALU.mult),
                       wait=(evs[0], evs[1]), inc="v")
            _e = S_.op("v", lambda e, sqs=sqs: e.tensor_tensor(t_m2, t_ro, sqs, ALU.mult),
                       wait=(_e,), inc="v")
            _e = S_.op("v", lambda e, qeo=qeo: e.tensor_sub(qeo, t_m1, t_m2), wait=(_e,), inc="v")
            _e = S_.op("v", lambda e, sqs=sqs: e.tensor_tensor(t_m1, t_re, sqs, ALU.mult),
                       wait=(_e,), inc="v")
            _e = S_.op("v", lambda e, cqs=cqs: e.tensor_tensor(t_m2, t_ro, cqs, ALU.mult),
                       wait=(_e,), inc="v")
            ev_cmb = S_.op("v", lambda e, qoo=qoo: e.tensor_add(qoo, t_m1, t_m2),
                           wait=(_e,), inc="v")
            it += 2
    S_.barrier(["t", "s", "v", "g"])
    al.release("a_xqT", "a_wqe", "a_wqo")

    if dbg == "proj":
        for nm, t in [("d_kn", t_kn), ("d_vtm", t_vtm), ("d_ke", t_ke), ("d_ko", t_ko),
                      ("d_qn", t_qn), ("d_qe", t_qe), ("d_qo", t_qo)]:
            o = dbg_out(nm, list(t[:].shape))
            S_.op("g", lambda e, o=o, t=t: e.dma_start(out=o, in_=t[:]), inc="o", dma=True)
        S_.wait("g", S_.ev("o"))
        return _finish(nc, ctx, S_, ddbg)

    t_mask = SBA("m_mask", [128, 16 * 512], BF16)
    ev_mask = S_.op("s", lambda e: e.dma_start(out=t_mask[:], in_=din["mask"]),
                    inc="sdma", dma=True)
    S_.wait("v", ev_mask)
    t_pt = [SBA(f"b_pt{i}", [128, 512], BF16) for i in range(8)]
    t_ao = SBA("b_ao", [128, H * NQ], BF16)
    rot_sb = Rot(4)
    rot_pt = Rot(8)
    rot_ob = Rot(2)
    rot_lb = Rot(2)
    gs = 0
    itt = 0
    LAG = 3
    pending = []

    def flush_pending(upto):
        while pending and pending[0][0] <= upto:
            _, fn = pending.pop(0)
            fn()

    for h in range(H):
        kv = h // 4
        g = h // 4
        gq = h % 4
        for qb in range(2):
            nslots = 8 if qb == 0 else 16
            ob = 4 + itt % 2
            lb = 6 + itt % 2
            state = {"l": None, "av": None}
            for j in range(nslots):
                kt = j
                bank = gs % 4
                S_.op("t", lambda e, bank=bank, kv=kv, kt=kt, h=h, qb=qb: e.matmul(
                    pbank[bank][:],
                    t_kn[:, kv * 2048 + kt * 128: kv * 2048 + (kt + 1) * 128],
                    t_qn[:, h * NQ + qb * 512: h * NQ + (qb + 1) * 512],
                    start=True, stop=False), wait=rot_sb.wait(gs))
                S_.op("t", lambda e, bank=bank, g=g, kt=kt, gq=gq, qb=qb: e.matmul(
                    pbank[bank][:],
                    t_ke[g * 32:(g + 1) * 32, kt * 128:(kt + 1) * 128],
                    t_qe[g * 32:(g + 1) * 32, gq * NQ + qb * 512: gq * NQ + (qb + 1) * 512],
                    start=False, stop=False, tile_position=(g * 32, 0)))
                ev_mm = S_.op("t", lambda e, bank=bank, g=g, kt=kt, gq=gq, qb=qb: e.matmul(
                    pbank[bank][:],
                    t_ko[g * 32:(g + 1) * 32, kt * 128:(kt + 1) * 128],
                    t_qo[g * 32:(g + 1) * 32, gq * NQ + qb * 512: gq * NQ + (qb + 1) * 512],
                    start=False, stop=True, tile_position=(g * 32, 0)), inc="t")
                pt = t_pt[gs % 8]
                ev_exp = S_.op("s", lambda e, pt=pt, bank=bank: e.activation(
                    pt[:], pbank[bank][:], AF.Exp, scale=SCALE_ATT),
                    wait=(ev_mm,) + rot_pt.wait(gs), inc="s")
                rot_sb.set(gs, ev_exp)
                ev_dep = ev_exp
                if qb == 0 or j >= 8:
                    ev_dep = S_.op("v", lambda e, pt=pt, j=j: e.tensor_mul(
                        pt[:], pt[:], t_mask[:, j * 512:(j + 1) * 512]),
                        wait=(ev_exp,), inc="v")

                def emit_lav(j=j, pt=pt, ev_dep=ev_dep, lb=lb, ob=ob, kt=kt, kv=kv,
                             nslots=nslots, itt=itt, gs=gs, state=state):
                    w = (ev_dep,)
                    if j == 0:
                        w = w + rot_lb.wait(itt) + rot_ob.wait(itt)
                    state["l"] = S_.op("t", lambda e: e.matmul(
                        pbank[lb][0:1, :], t_ones_c[:], pt[:],
                        start=(j == 0), stop=(j == nslots - 1)),
                        wait=w, inc="t" if j == nslots - 1 else None)
                    ev_av = S_.op("t", lambda e: e.matmul(
                        pbank[ob][:], t_vtm[:, kt * 512 + kv * 128: kt * 512 + (kv + 1) * 128],
                        pt[:], start=(j == 0), stop=(j == nslots - 1)), inc="t")
                    rot_pt.set(gs, ev_av)
                    state["av"] = ev_av

                pending.append((gs, emit_lav))
                flush_pending(gs - LAG)
                gs += 1
            flush_pending(gs)
            ev_r = S_.op("v", lambda e, lb=lb, itt=itt: e.reciprocal(
                t_row[itt % 2][:], pbank[lb][0:1, :]),
                wait=(state["l"], state["av"]), inc="v")
            ev_rb = S_.op("t", lambda e, lb=lb, itt=itt: e.matmul(
                pbank[lb][:], t_ones_r[:], t_row[itt % 2][:], start=True, stop=True),
                wait=(ev_r,), inc="t")
            ev_rbs = S_.op("v", lambda e, lb=lb, itt=itt: e.tensor_copy(
                t_rbs[itt % 2][:], pbank[lb][:]), wait=(ev_rb,), inc="v")
            ev_on = S_.op("v", lambda e, ob=ob, h=h, qb=qb, itt=itt: e.tensor_mul(
                t_ao[:, h * NQ + qb * 512: h * NQ + (qb + 1) * 512],
                pbank[ob][:], t_rbs[itt % 2][:]), wait=(ev_rbs,), inc="v")
            rot_ob.set(itt, ev_on)
            rot_lb.set(itt, ev_on)
            itt += 1
    S_.barrier(["t", "s", "v", "g"])
    al.release("a_kn", "a_vtm", "a_ke", "a_ko", "a_qn", "a_qe", "a_qo",
               "b_pt0", "b_pt1", "b_pt2", "b_pt3", "b_pt4", "b_pt5", "b_pt6", "b_pt7")

    if dbg == "attn":
        o = dbg_out("d_ao", [128, H * NQ])
        for _eng in ("v", "s", "t", "y"):
            if S_.last[_eng]:
                S_.wait("g", S_.last[_eng])
        S_.op("g", lambda e, o=o: e.dma_start(out=o, in_=t_ao[:]), inc="o", dma=True)
        S_.wait("g", S_.ev("o"))
        return _finish(nc, ctx, S_, ddbg)

    t_wosb = SBA("c_wo", [128, H * 2048], BF16)
    t_ywin = [SBA(f"c_yw{i}", [128, 2048], F32) for i in range(2)]
    t_xqst = [SBA(f"c_xq{i}", [128, 2048], F32) for i in range(2)]
    ev_wo = S_.op("g", lambda e: e.dma_start(
        out=t_wosb[:].rearrange("p (h f) -> p h f", h=16),
        in_=din["wo"].rearrange("(h p) f -> p h f", p=128)), inc="gdma", dma=True)

    rot_y = Rot(2)
    rot_xs = Rot(2)
    for qt in range(8):
        ev_x = S_.op("s", lambda e, qt=qt: e.dma_start(
            out=t_xqst[qt % 2][:], in_=din["xq"][qt * 128:(qt + 1) * 128, :]),
            wait=rot_xs.wait(qt), inc="sdma", dma=True)
        bank0 = 4 * (qt % 2)
        ev_fs = [None] * 4
        for hh in range(16):
            for fs in range(4):
                w = ()
                if hh == 0:
                    w = rot_y.wait(qt)
                    if qt == 0 and fs == 0:
                        w = w + (ev_wo,)
                ev_fs[fs] = S_.op("t", lambda e, bank0=bank0, fs=fs, hh=hh, qt=qt: e.matmul(
                    pbank[bank0 + fs][:],
                    t_ao[:, hh * NQ + qt * 128: hh * NQ + qt * 128 + 128],
                    t_wosb[:, hh * 2048 + fs * 512: hh * 2048 + (fs + 1) * 512],
                    start=(hh == 0), stop=(hh == 15)),
                    wait=w, inc="t" if hh == 15 else None)
        yw = t_ywin[qt % 2]
        ev_z = None
        for fs in range(4):
            ev_z = S_.op("v", lambda e, bank0=bank0, fs=fs, yw=yw, qt=qt: e.tensor_add(
                yw[:, fs * 512:(fs + 1) * 512], pbank[bank0 + fs][:],
                t_xqst[qt % 2][:, fs * 512:(fs + 1) * 512]),
                wait=(ev_fs[fs], ev_x) if fs == 0 else (ev_fs[fs],),
                inc="v" if fs == 3 else None)
        rot_xs.set(qt, ev_z)
        ev_ln = ln_inplace(yw, ev_z)
        ev_sp = S_.op("y", lambda e, yw=yw, qt=qt: e.dma_start(
            out=d_yln[qt * 128:(qt + 1) * 128, :], in_=yw[:]), wait=(ev_ln,), inc="ydma", dma=True)
        rot_y.set(qt, ev_sp)
    S_.barrier(["t", "s", "v", "g", "y"])
    al.release("c_wo", "c_xq0", "c_xq1", "b_ao", "c_yw0", "c_yw1")

    if dbg == "wo":
        o = dbg_out("d_yln", [NQ, D])
        S_.op("g", lambda e, o=o: e.dma_start(out=o, in_=d_yln), inc="o", dma=True)
        S_.wait("g", S_.ev("o"))
        return _finish(nc, ctx, S_, ddbg)

    al.release("m_mask", "m_rbs0", "m_rbs1", "m_row0", "m_row1", "m_cosq", "m_sinq")
    t_yfm = SBA("f_yfm", [128, 16 * 512], F32)
    t_B2f = SBA("f_B2f", [128, 16 * 512], F32)
    t_xq1 = SBA("f_xq1", [128, 16 * NQ], BF16)
    t_yst = [SBA(f"f_yst{i}", [128, 2048], F32) for i in range(2)]
    rot_ys = Rot(2)
    rot_bank4b = Rot(4)
    for ch in range(2):
        last_e = None
        for rt in range(4):
            qt = ch * 4 + rt
            ys = t_yst[rt % 2]
            ev_ld = S_.op("y", lambda e, ys=ys, qt=qt: e.dma_start(
                out=ys[:], in_=d_yln[qt * 128:(qt + 1) * 128, :]),
                wait=rot_ys.wait(rt), inc="ydma", dma=True)
            for cg in range(4):
                bank = 2 * (rt % 2) + (cg % 2)
                ev_t = None
                for ci in range(4):
                    c = cg * 4 + ci
                    w2 = ((ev_ld,) if (cg == 0 and ci == 0) else ())
                    if ci == 0:
                        w2 = w2 + rot_bank4b.wait(bank)
                    ev_t = S_.op("t", lambda e, ys=ys, c=c, bank=bank, ci=ci: e.transpose(
                        pbank[bank][:, ci * 128:(ci + 1) * 128],
                        ys[:, c * 128:(c + 1) * 128], t_idf[:]),
                        wait=w2, inc="t" if ci == 3 else None)
                dst = t_yfm[:].rearrange("p (c t) -> p c t", c=16)[
                    :, cg * 4:(cg + 1) * 4, rt * 128:(rt + 1) * 128]
                last_e = S_.op("v", lambda e, dst=dst, bank=bank: e.tensor_copy(
                    dst, pbank[bank][:]), wait=(ev_t,), inc="v")
                rot_bank4b.set(bank, last_e)
            rot_ys.set(rt, last_e)
        S_.barrier(["t", "v", "y", "s"])
        if dbg == "yfm" and ch == 0:
            o = dbg_out("d_yfm", [128, 16 * 512])
            S_.wait("g", S_.last["v"])
            S_.op("g", lambda e, o=o: e.dma_start(out=o, in_=t_yfm[:]), inc="o", dma=True)
            S_.wait("g", S_.ev("o"))
            return _finish(nc, ctx, S_, ddbg)
        rot_f = Rot(2)
        for c in range(16):
            bank = c % 2
            ev_mm = S_.op("t", lambda e, c=c, bank=bank: e.matmul(
                pbank[bank][:], t_h128f[:], t_yfm[:, c * 512:(c + 1) * 512],
                start=True, stop=True), wait=rot_f.wait(c), inc="t")
            ev_e = S_.op("s", lambda e, c=c, bank=bank: e.copy(
                t_yfm[:, c * 512:(c + 1) * 512], pbank[bank][:]), wait=(ev_mm,), inc="s")
            rot_f.set(c, ev_e)
        S_.barrier(["t", "s", "v"])
        if dbg == "yfA" and ch == 0:
            o = dbg_out("d_yfm", [128, 16 * 512])
            S_.wait("g", S_.last["s"])
            S_.op("g", lambda e, o=o: e.dma_start(out=o, in_=t_yfm[:]), inc="o", dma=True)
            S_.wait("g", S_.ev("o"))
            return _finish(nc, ctx, S_, ddbg)
        bufs = [t_yfm, t_B2f]
        _e = None
        for si, stt in enumerate([8, 4, 2, 1]):
            src_v = bufs[si % 2][:].rearrange("p (g two i t) -> p g two i t", two=2, i=stt, t=512)
            dst_v = bufs[(si + 1) % 2][:].rearrange("p (g two i t) -> p g two i t", two=2, i=stt, t=512)
            S_.op("v", lambda e, s=src_v, d=dst_v: e.tensor_tensor(
                d[:, :, 0], s[:, :, 0], s[:, :, 1], ALU.add), wait=(_e,) if _e else ())
            _e = S_.op("v", lambda e, s=src_v, d=dst_v: e.tensor_tensor(
                d[:, :, 1], s[:, :, 0], s[:, :, 1], ALU.subtract), inc="v")
        if dbg == "xf1" and ch == 0:
            o = dbg_out("d_yfm", [128, 16 * 512])
            S_.wait("g", S_.last["v"])
            S_.op("g", lambda e, o=o: e.dma_start(out=o, in_=t_yfm[:]), inc="o", dma=True)
            S_.wait("g", S_.ev("o"))
            return _finish(nc, ctx, S_, ddbg)
        _e = S_.op("v", lambda e: e.tensor_tensor(t_st[:, 0:512], t_yfm[:, 0:512],
                                                  t_yfm[:, 512:1024], ALU.max), inc="v")
        for c in range(2, 16):
            _e = S_.op("v", lambda e, c=c: e.tensor_tensor(
                t_st[:, 0:512], t_st[:, 0:512], t_yfm[:, c * 512:(c + 1) * 512], ALU.max),
                wait=(_e,), inc="v")
        _e = S_.op("v", lambda e: e.tensor_tensor(t_st[:, 512:1024], t_yfm[:, 0:512],
                                                  t_yfm[:, 512:1024], ALU.min), wait=(_e,), inc="v")
        for c in range(2, 16):
            _e = S_.op("v", lambda e, c=c: e.tensor_tensor(
                t_st[:, 512:1024], t_st[:, 512:1024], t_yfm[:, c * 512:(c + 1) * 512], ALU.min),
                wait=(_e,), inc="v")
        _e = S_.op("v", lambda e: e.tensor_scalar(t_st[:, 512:1024], t_st[:, 512:1024],
                                                  -1.0, None, ALU.mult), wait=(_e,), inc="v")
        ev_tree = S_.op("v", lambda e: e.tensor_tensor(
            t_st[:, 0:512], t_st[:, 0:512], t_st[:, 512:1024], ALU.max), wait=(_e,), inc="v")
        ev_tp = None
        for jcol in range(4):
            ev_tp = S_.op("t", lambda e, jcol=jcol: e.transpose(
                pbank[6][:, jcol * 128:(jcol + 1) * 128],
                t_st[:, jcol * 128:(jcol + 1) * 128], t_idf[:]),
                wait=(ev_tree,) if jcol == 0 else (), inc="t" if jcol == 3 else None)
        ev_q1m = None
        for jcol in range(4):
            ev_q1m = S_.op("v", lambda e, jcol=jcol: e.reduce_max(
                pbank[5][:, jcol: jcol + 1],
                pbank[6][:, jcol * 128:(jcol + 1) * 128], axis=AX.X),
                wait=(ev_tp,) if jcol == 0 else (), inc="v" if jcol == 3 else None)
        ev_q1m = vg()
        ev_q1m = S_.op("s", lambda e, ch=ch: e.copy(
            t_q1m[:, ch * 4:(ch + 1) * 4], pbank[5][:, 0:4]), wait=(ev_q1m,), inc="s")
        if dbg == "q1r" and ch == 0:
            o = dbg_out("d_q1m", [128, 16])
            o2 = dbg_out("d_pb6", [128, 512])
            ev_c6 = S_.op("v", lambda e: e.tensor_copy(t_B[:, 0:512], pbank[6][:]),
                          wait=(ev_q1m,), inc="v")
            S_.wait("g", ev_c6)
            S_.op("g", lambda e, o=o: e.dma_start(out=o, in_=t_q1m[:]), inc="o", dma=True)
            S_.op("g", lambda e, o2=o2: e.dma_start(out=o2, in_=t_B[:, 0:512]), inc="o", dma=True)
            S_.wait("g", S_.ev("o"))
            return _finish(nc, ctx, S_, ddbg)
        ev = tvop(lambda e, ch=ch: e.tensor_scalar(
            pbank[5][:, 8:12], t_q1m[:, ch * 4:(ch + 1) * 4], EPS_Q / C1, None, ALU.max),
            pbank[5][:, 8:12], t_q1m[:, ch * 4:(ch + 1) * 4], wait=(ev_q1m,))
        ev = tvop(lambda e, ch=ch: e.reciprocal(pbank[5][:, 12:16], t_q1m[:, ch * 4:(ch + 1) * 4]),
                  pbank[5][:, 12:16], t_st2[:, 0:4], wait=(ev,))
        ev_s1 = tvop(lambda e: e.tensor_scalar(pbank[5][:, 8:12], t_st2[:, 0:4],
                                               QBITS, None, ALU.mult),
                     pbank[5][:, 8:12], t_st2[:, 0:4], wait=(ev,))
        ev_tp2 = S_.op("t", lambda e: e.transpose(
            pbank[7][0:4, 0:128], t_st2[:, 0:4], t_idf[:]), wait=(ev_s1,), inc="t")
        ev_rT = S_.op("v", lambda e: e.tensor_copy(t_rowT[0:4, :], pbank[7][0:4, 0:128]),
                      wait=(ev_tp2,), inc="v")
        ev_b1 = None
        for j in range(4):
            ev_b1 = S_.op("t", lambda e, j=j: e.matmul(
                pbank[4][:, j * 128:(j + 1) * 128], t_sel8[0:4, j * 128:(j + 1) * 128],
                t_rowT[0:4, :], start=True, stop=True),
                wait=(ev_rT,) if j == 0 else (), inc="t" if j == 3 else None)
        ev_B1 = S_.op("v", lambda e: e.tensor_copy(t_B[:, 0:512], pbank[4][:]),
                      wait=(ev_b1,), inc="v")
        if dbg == "q1s" and ch == 0:
            o = dbg_out("d_B", [128, 1024])
            o2 = dbg_out("d_q1m", [128, 16])
            o3 = dbg_out("d_s1", [128, 128])
            o4 = dbg_out("d_pmax", [128, 1024])
            S_.wait("g", ev_B1)
            S_.op("g", lambda e, o=o: e.dma_start(out=o, in_=t_B[:]), inc="o", dma=True)
            S_.op("g", lambda e, o2=o2: e.dma_start(out=o2, in_=t_q1m[:]), inc="o", dma=True)
            S_.op("g", lambda e, o3=o3: e.dma_start(out=o3, in_=t_st2[:]), inc="o", dma=True)
            S_.op("g", lambda e, o4=o4: e.dma_start(out=o4, in_=t_st[:]), inc="o", dma=True)
            S_.wait("g", S_.ev("o"))
            return _finish(nc, ctx, S_, ddbg)
        ev_x1 = ev_B1
        for c in range(16):
            ev_x1 = S_.op("v", lambda e, c=c: e.tensor_tensor(
                t_st[:, 0:512], t_yfm[:, c * 512:(c + 1) * 512], t_B[:, 0:512], ALU.mult),
                wait=(ev_x1,), inc="v")
            ev_x1 = S_.op("v", lambda e, c=c, ch=ch: e.tensor_scalar(
                t_xq1[:, c * NQ + ch * 512: c * NQ + (ch + 1) * 512],
                t_st[:, 0:512], MROUND, MROUND, ALU.add, ALU.subtract),
                wait=(ev_x1,), inc="v")
        S_.barrier(["t", "s", "v", "y"])
    al.release("f_yfm", "f_B2f")

    if dbg == "xq1":
        o = dbg_out("d_xq1", [128, 16 * NQ])
        o2 = dbg_out("d_q1m", [128, 16])
        for _eng in ("v", "s", "t", "y"):
            if S_.last[_eng]:
                S_.wait("g", S_.last[_eng])
        S_.op("g", lambda e, o=o: e.dma_start(out=o, in_=t_xq1[:]), inc="o", dma=True)
        S_.op("g", lambda e, o2=o2: e.dma_start(out=o2, in_=t_q1m[:]), inc="o", dma=True)
        S_.wait("g", S_.ev("o"))
        return _finish(nc, ctx, S_, ddbg)

    t_wstg = [SBA(f"w_stg{i}", [128, F2], BF16) for i in range(2)]
    t_wabs = [SBA(f"w_abs{i}", [128, F2], BF16) for i in range(2)]

    def weight_mean(dname, nrow_t, ncols, dst_w, dst_r):
        rot_s = Rot(2)
        rot_a = Rot(2)
        nmm = ncols // 512
        ev_mm = None
        for r in range(nrow_t):
            stg = t_wstg[r % 2]
            ev_ld = S_.op("g", lambda e, stg=stg, r=r: e.dma_start(
                out=stg[:, 0:ncols], in_=din[dname][r * 128:(r + 1) * 128, :]),
                wait=rot_s.wait(r), inc="gdma", dma=True)
            wab = t_wabs[r % 2]
            ev_ab = S_.op("s", lambda e, stg=stg, ncols=ncols, wab=wab: e.activation(
                wab[:, 0:ncols], stg[:, 0:ncols], AF.Abs),
                wait=(ev_ld,) + rot_a.wait(r), inc="s")
            rot_s.set(r, ev_ab)
            for n in range(nmm):
                ev_mm = S_.op("t", lambda e, n=n, r=r, nrow_t=nrow_t, nmm=nmm, wab=wab: e.matmul(
                    pbank[2][0:1, :], t_ones_c[:], wab[:, n * 512:(n + 1) * 512],
                    start=(r == 0 and n == 0), stop=(r == nrow_t - 1 and n == nmm - 1)),
                    wait=(ev_ab,) if n == 0 else (), inc="t" if n == nmm - 1 else None)
            rot_a.set(r, ev_mm)
        ev = tvop(lambda e: e.reduce_sum(pbank[5][0:1, 40:41], pbank[2][0:1, :], axis=AX.X),
                  pbank[5][0:1, 40:41], t_wsc[:, 0:1], wait=(ev_mm,))
        ev = tvop(lambda e, nrow_t=nrow_t, ncols=ncols: e.tensor_scalar(
            pbank[5][0:1, 41:42], t_wsc[:, 0:1], 1.0 / (nrow_t * 128 * ncols), None, ALU.mult),
            pbank[5][0:1, 41:42], t_wsc[:, 1:2], wait=(ev,))
        ev = tvop(lambda e: e.tensor_scalar(pbank[5][0:1, 42:43], t_wsc[:, 1:2], EPS_Q, None, ALU.max),
                  pbank[5][0:1, 42:43], t_wsc[:, 2:3], wait=(ev,))
        ev_r = tvop(lambda e: e.reciprocal(pbank[5][0:1, 43:44], t_wsc[:, 2:3]),
                    pbank[5][0:1, 43:44], t_wsc[:, 3:4], wait=(ev,))
        ev_m1 = S_.op("t", lambda e: e.matmul(pbank[2][:, 0:1], t_ones_r[:], t_wsc[:, 2:3],
                                              start=True, stop=True), wait=(ev_r,), inc="t")
        ev_c1 = S_.op("s", lambda e, dst_w=dst_w: e.copy(dst_w, pbank[2][:, 0:1]),
                      wait=(ev_m1,), inc="s")
        ev_m2 = S_.op("t", lambda e: e.matmul(pbank[2][:, 1:2], t_ones_r[:], t_wsc[:, 3:4],
                                              start=True, stop=True), wait=(ev_c1,), inc="t")
        return S_.op("v", lambda e, dst_r=dst_r: e.tensor_copy(dst_r, pbank[2][:, 1:2]),
                     wait=(ev_m2,), inc="v")

    S_.barrier(["t", "s", "v", "g", "y"])
    _ew = weight_mean("wupt", 16, F2, t_wupb[:], t_rupb[:])
    ev = tvop(lambda e: e.tensor_scalar(pbank[5][:, 16:24], t_q1m[:, 0:8], t_wupb[:, 0:1],
                                        C1 / QBITS, ALU.mult, ALU.mult),
              pbank[5][:, 16:24], t_st2[:, 16:24], wait=(_ew,))
    tvop(lambda e: e.tensor_tensor(pbank[5][:, 24:32], t_st2[:, 16:24], t_st2[:, 16:24],
                                   ALU.mult),
         pbank[5][:, 24:32], t_d2[:, 0:8], wait=(ev,))
    S_.barrier(["t", "s", "v", "g"])
    al.release("w_stg0", "w_stg1", "w_abs0", "w_abs1")

    t_wqsl = [SBA(f"u_wq{i}", [128, 16 * 512], BF16) for i in range(2)]
    t_fstg = [SBA(f"u_fs{i}", [128, 512], F32) for i in range(2)]
    t_qstg = SBA("u_qs", [128, 512], F32)
    t_h = SBA("u_h", [128, 32 * 1024], BF16)
    t_rl2 = SBA("u_rl2", [128, 512], BF16)
    rot_wsl = Rot(2)
    rot_fst = Rot(2)
    rot_qst = Rot(1)
    rot_hb = Rot(2)
    rot_rlb = Rot(2)
    for slab in range(8):
        wsl = t_wqsl[slab % 2]
        ev_q3 = None
        for dc in range(16):
            ev_ld = S_.op("y", lambda e, dc=dc, slab=slab: e.dma_start(
                out=t_fstg[dc % 2][:], in_=din["wupt"][dc * 128:(dc + 1) * 128,
                                                       slab * 512:(slab + 1) * 512]),
                wait=rot_fst.wait(dc) + (rot_wsl.wait(slab) if dc == 0 else ()),
                inc="ydma", dma=True)
            ev_q1 = S_.op("s", lambda e, dc=dc: e.activation(
                t_qstg[:], t_fstg[dc % 2][:], AF.Copy, scale=t_rupb[:]),
                wait=(ev_ld,) + rot_qst.wait(0), inc="s")
            rot_fst.set(dc, ev_q1)
            _e = S_.op("g", lambda e: e.tensor_scalar(t_qstg[:], t_qstg[:], MROUND, MROUND,
                                                      ALU.add, ALU.subtract), wait=(ev_q1,), inc="g")
            ev_q3 = S_.op("g", lambda e, wsl=wsl, dc=dc: e.tensor_scalar(
                wsl[:, dc * 512:(dc + 1) * 512], t_qstg[:], 1.0, -1.0, ALU.min, ALU.max),
                wait=(_e,), inc="g")
            rot_qst.set(0, ev_q3)
        ev_last_mm = None
        for f2c in range(4):
            for tc in range(2):
                gi = slab * 8 + f2c * 2 + tc
                bank = gi % 2
                ev_mm = None
                for dc in range(16):
                    w = (rot_hb.wait(gi) if dc == 0 else ())
                    if dc == 0 and f2c == 0 and tc == 0:
                        w = w + (ev_q3,)
                    ev_mm = S_.op("t", lambda e, bank=bank, dc=dc, f2c=f2c, tc=tc, wsl=wsl: e.matmul(
                        pbank[bank][:],
                        wsl[:, dc * 512 + f2c * 128: dc * 512 + (f2c + 1) * 128],
                        t_xq1[:, dc * NQ + tc * 512: dc * NQ + (tc + 1) * 512],
                        start=(dc == 0), stop=(dc == 15)),
                        wait=w, inc="t" if dc == 15 else None)
                ev_last_mm = ev_mm
                fcg = slab * 4 + f2c
                rl = t_rl if gi % 2 == 0 else t_rl2
                ev_rl = S_.op("s", lambda e, bank=bank, rl=rl: e.activation(
                    rl[:], pbank[bank][:], AF.Relu), wait=(ev_mm,) + rot_rlb.wait(gi), inc="s")
                ev_h = S_.op("v", lambda e, fcg=fcg, tc=tc, rl=rl: e.tensor_tensor(
                    t_h[:, fcg * 1024 + tc * 512: fcg * 1024 + (tc + 1) * 512],
                    rl[:], rl[:], ALU.mult), wait=(ev_rl,), inc="v")
                rot_hb.set(gi, ev_h)
                rot_rlb.set(gi, ev_h)
        rot_wsl.set(slab, ev_last_mm)
    S_.barrier(["t", "s", "v", "y", "g"])
    al.release("f_xq1", "u_wq0", "u_wq1", "u_fs0", "u_fs1", "u_qs", "u_rl2")

    if dbg == "h":
        o = dbg_out("d_h", [128, 32 * 1024])
        for _eng in ("v", "s", "t", "y"):
            if S_.last[_eng]:
                S_.wait("g", S_.last[_eng])
        o3 = dbg_out("d_wsc", [128, 1])
        o4 = dbg_out("d_wscrow", [1, 8])
        o5 = dbg_out("d_colsum", [1, 512])
        S_.op("g", lambda e, o5=o5: e.dma_start(out=o5, in_=t_B[0:1, 0:512]), inc="o", dma=True)
        S_.op("g", lambda e, o=o: e.dma_start(out=o, in_=t_h[:]), inc="o", dma=True)
        S_.op("g", lambda e, o3=o3: e.dma_start(out=o3, in_=t_rupb[:]), inc="o", dma=True)
        S_.op("g", lambda e, o4=o4: e.dma_start(out=o4, in_=t_wsc[:]), inc="o", dma=True)
        S_.wait("g", S_.ev("o"))
        return _finish(nc, ctx, S_, ddbg)

    t_pong = SBA("x_pong", [128, 32 * 1024], BF16)
    rot_f2 = Rot(2)
    for fc in range(32):
        for half in range(2):
            i6 = fc * 2 + half
            bank = i6 % 2
            ev_mm = S_.op("t", lambda e, fc=fc, half=half, bank=bank: e.matmul(
                pbank[bank][:], t_h128bf[:],
                t_h[:, fc * 1024 + half * 512: fc * 1024 + (half + 1) * 512],
                start=True, stop=True), wait=rot_f2.wait(i6), inc="t")
            ev_e = S_.op("s", lambda e, fc=fc, half=half, bank=bank: e.copy(
                t_h[:, fc * 1024 + half * 512: fc * 1024 + (half + 1) * 512], pbank[bank][:]),
                wait=(ev_mm,), inc="s")
            rot_f2.set(i6, ev_e)
    S_.barrier(["t", "s", "v"])
    bufs2 = [t_h, t_pong]
    _e = None
    for si, stt in enumerate([16, 8, 4, 2, 1]):
        src_v = bufs2[si % 2][:].rearrange("p (g two i t) -> p g two i t", two=2, i=stt, t=1024)
        dst_v = bufs2[(si + 1) % 2][:].rearrange("p (g two i t) -> p g two i t", two=2, i=stt, t=1024)
        S_.op("v", lambda e, s=src_v, d=dst_v: e.tensor_tensor(
            d[:, :, 0], s[:, :, 0], s[:, :, 1], ALU.add), wait=(_e,) if _e else ())
        _e = S_.op("v", lambda e, s=src_v, d=dst_v: e.tensor_tensor(
            d[:, :, 1], s[:, :, 0], s[:, :, 1], ALU.subtract), inc="v")
    t_F = t_pong
    t_xq2 = t_h
    t_trbf2 = t_st[:, 0:512].bitcast(BF16)
    _e = S_.op("v", lambda e: e.tensor_tensor(t_trbf[:], t_F[:, 0:1024], t_F[:, 1024:2048], ALU.max),
               inc="v")
    for c in range(2, 32):
        _e = S_.op("v", lambda e, c=c: e.tensor_tensor(
            t_trbf[:], t_trbf[:], t_F[:, c * 1024:(c + 1) * 1024], ALU.max), wait=(_e,), inc="v")
    _e = S_.op("v", lambda e: e.tensor_tensor(t_trbf2, t_F[:, 0:1024], t_F[:, 1024:2048], ALU.min),
               wait=(_e,), inc="v")
    for c in range(2, 32):
        _e = S_.op("v", lambda e, c=c: e.tensor_tensor(
            t_trbf2, t_trbf2, t_F[:, c * 1024:(c + 1) * 1024], ALU.min), wait=(_e,), inc="v")
    _e = S_.op("v", lambda e: e.tensor_scalar(t_trbf2, t_trbf2, -1.0, None, ALU.mult),
               wait=(_e,), inc="v")
    ev_tr2 = S_.op("v", lambda e: e.tensor_tensor(t_trbf[:], t_trbf[:], t_trbf2, ALU.max),
                   wait=(_e,), inc="v")
    ev_tp = None
    for j in range(8):
        ev_tp = S_.op("t", lambda e, j=j: e.transpose(
            pbank[6][:].bitcast(BF16)[:, j * 128:(j + 1) * 128],
            t_trbf[:, j * 128:(j + 1) * 128], t_idbf[:]),
            wait=(ev_tr2,) if j == 0 else (), inc="t" if j == 7 else None)
    ev_q2m = None
    for j in range(8):
        ev_q2m = S_.op("v", lambda e, j=j: e.reduce_max(
            pbank[5][:, j:j + 1], pbank[6][:].bitcast(BF16)[:, j * 128:(j + 1) * 128], axis=AX.X),
            wait=(ev_tp,) if j == 0 else (), inc="v" if j == 7 else None)
    ev_q2m = vg()
    ev_q2m = S_.op("s", lambda e: e.copy(t_q2m[:, 0:8], pbank[5][:, 0:8]),
                   wait=(ev_q2m,), inc="s")
    ev = tvop(lambda e: e.tensor_tensor(pbank[5][:, 8:16], t_q2m[:, 0:8], t_d2[:, 0:8], ALU.mult),
              pbank[5][:, 8:16], t_st2[:, 24:32], wait=(ev_q2m,))
    ev = tvop(lambda e: e.tensor_scalar(pbank[5][:, 16:24], t_st2[:, 24:32], C2, EPS_Q,
                                        ALU.mult, ALU.max),
              pbank[5][:, 16:24], t_q2m[:, 0:8], wait=(ev,))
    ev = tvop(lambda e: e.reciprocal(pbank[5][:, 8:16], t_q2m[:, 0:8]),
              pbank[5][:, 8:16], t_st2[:, 24:32], wait=(ev,))
    ev = tvop(lambda e: e.tensor_tensor(pbank[5][:, 16:24], t_st2[:, 24:32], t_d2[:, 0:8],
                                        ALU.mult),
              pbank[5][:, 16:24], t_st2[:, 24:32], wait=(ev,))
    ev_s2 = tvop(lambda e: e.tensor_scalar(pbank[5][:, 8:16], t_st2[:, 24:32],
                                           C2 * QBITS, None, ALU.mult),
                 pbank[5][:, 8:16], t_st2[:, 8:16], wait=(ev,))
    if dbg == "z2":
        o8 = dbg_out("d_q2m_postE", [128, 16])
        S_.wait("g", ev_s2)
        S_.op("g", lambda e, o8=o8: e.dma_start(out=o8, in_=t_q2m[:]), inc="gdma", dma=True)
    ev_tp2 = S_.op("t", lambda e: e.transpose(
        pbank[7][0:8, 0:128], t_st2[:, 8:16], t_idf[:]), wait=(ev_s2,), inc="t")
    ev_rT = S_.op("v", lambda e: e.tensor_copy(t_rowT[:], pbank[7][0:8, 0:128]),
                  wait=(ev_tp2,), inc="v")
    ev_b2 = [None, None]
    for half in range(2):
        for j in range(4):
            jj = half * 4 + j
            ev_b2[half] = S_.op("t", lambda e, jj=jj, half=half, j=j: e.matmul(
                pbank[6 + half][:, j * 128:(j + 1) * 128],
                t_sel8[:, jj * 128:(jj + 1) * 128], t_rowT[:],
                start=True, stop=True),
                wait=(ev_rT,) if jj == 0 else (), inc="t" if j == 3 else None)
    ev_B2 = None
    for half in range(2):
        ev_B2 = S_.op("v", lambda e, half=half: e.tensor_copy(
            t_B[:, half * 512:(half + 1) * 512], pbank[6 + half][:]),
            wait=(ev_b2[half],), inc="v" if half == 1 else None)
    ev_x2 = ev_B2
    for fc in range(32):
        ev_x2 = S_.op("v", lambda e, fc=fc: e.tensor_tensor(
            t_st[:, 0:1024], t_F[:, fc * 1024:(fc + 1) * 1024], t_B[:, 0:1024], ALU.mult),
            wait=(ev_x2,), inc="v")
        ev_x2 = S_.op("v", lambda e, fc=fc: e.tensor_scalar(
            t_xq2[:, fc * 1024:(fc + 1) * 1024], t_st[:, 0:1024], MROUND, MROUND,
            ALU.add, ALU.subtract), wait=(ev_x2,), inc="v")
    S_.barrier(["t", "s", "v", "g", "y"])
    al.release("x_pong")

    if dbg == "xq2":
        o = dbg_out("d_xq2", [128, 32 * 1024])
        o2 = dbg_out("d_q2m", [128, 16])
        for _eng in ("v", "s", "t", "y"):
            if S_.last[_eng]:
                S_.wait("g", S_.last[_eng])
        S_.op("g", lambda e, o=o: e.dma_start(out=o, in_=t_xq2[:]), inc="o", dma=True)
        S_.op("g", lambda e, o2=o2: e.dma_start(out=o2, in_=t_q2m[:]), inc="o", dma=True)
        S_.wait("g", S_.ev("o"))
        return _finish(nc, ctx, S_, ddbg)

    if dbg == "z2":
        o7 = dbg_out("d_q2m_pre", [128, 16])
        S_.op("y", lambda e, o7=o7: e.dma_start(out=o7, in_=t_q2m[:]), inc="ydma", dma=True)

    t_wstg = [SBA(f"w2_stg{i}", [128, F2], BF16) for i in range(2)]
    t_wabs = [SBA(f"w2_abs{i}", [128, F2], BF16) for i in range(2)]

    def weight_mean2(dname, nrow_t, ncols, dst_w, dst_r):
        rot_s = Rot(2)
        rot_a = Rot(2)
        nmm = ncols // 512
        ev_mm = None
        for r in range(nrow_t):
            stg = t_wstg[r % 2]
            ev_ld = S_.op("g", lambda e, stg=stg, r=r: e.dma_start(
                out=stg[:, 0:ncols], in_=din[dname][r * 128:(r + 1) * 128, :]),
                wait=rot_s.wait(r), inc="gdma", dma=True)
            wab = t_wabs[r % 2]
            ev_ab = S_.op("s", lambda e, stg=stg, ncols=ncols, wab=wab: e.activation(
                wab[:, 0:ncols], stg[:, 0:ncols], AF.Abs),
                wait=(ev_ld,) + rot_a.wait(r), inc="s")
            rot_s.set(r, ev_ab)
            for n in range(nmm):
                ev_mm = S_.op("t", lambda e, n=n, r=r, nrow_t=nrow_t, nmm=nmm, wab=wab: e.matmul(
                    pbank[2][0:1, :], t_ones_c[:], wab[:, n * 512:(n + 1) * 512],
                    start=(r == 0 and n == 0), stop=(r == nrow_t - 1 and n == nmm - 1)),
                    wait=(ev_ab,) if n == 0 else (), inc="t" if n == nmm - 1 else None)
            rot_a.set(r, ev_mm)
        ev = tvop(lambda e: e.reduce_sum(pbank[5][0:1, 40:41], pbank[2][0:1, :], axis=AX.X),
                  pbank[5][0:1, 40:41], t_wsc[:, 0:1], wait=(ev_mm,))
        ev = tvop(lambda e, nrow_t=nrow_t, ncols=ncols: e.tensor_scalar(
            pbank[5][0:1, 41:42], t_wsc[:, 0:1], 1.0 / (nrow_t * 128 * ncols), None, ALU.mult),
            pbank[5][0:1, 41:42], t_wsc[:, 1:2], wait=(ev,))
        ev = tvop(lambda e: e.tensor_scalar(pbank[5][0:1, 42:43], t_wsc[:, 1:2], EPS_Q, None, ALU.max),
                  pbank[5][0:1, 42:43], t_wsc[:, 2:3], wait=(ev,))
        ev_r = tvop(lambda e: e.reciprocal(pbank[5][0:1, 43:44], t_wsc[:, 2:3]),
                    pbank[5][0:1, 43:44], t_wsc[:, 3:4], wait=(ev,))
        ev_m1 = S_.op("t", lambda e: e.matmul(pbank[2][:, 0:1], t_ones_r[:], t_wsc[:, 2:3],
                                              start=True, stop=True), wait=(ev_r,), inc="t")
        ev_c1 = S_.op("s", lambda e, dst_w=dst_w: e.copy(dst_w, pbank[2][:, 0:1]),
                      wait=(ev_m1,), inc="s")
        ev_m2 = S_.op("t", lambda e: e.matmul(pbank[2][:, 1:2], t_ones_r[:], t_wsc[:, 3:4],
                                              start=True, stop=True), wait=(ev_c1,), inc="t")
        return S_.op("v", lambda e, dst_r=dst_r: e.tensor_copy(dst_r, pbank[2][:, 1:2]),
                     wait=(ev_m2,), inc="v")

    _ew = weight_mean2("wdnt", 32, 2048, t_wdnb[:], t_rdnb[:])
    tvop(lambda e: e.tensor_scalar(pbank[5][:, 24:32], t_q2m[:, 0:8], t_wdnb[:, 0:1],
                                   1.0 / QBITS, ALU.mult, ALU.mult),
         pbank[5][:, 24:32], t_dq2[:, 0:8], wait=(_ew,))
    S_.barrier(["t", "s", "v", "g"])
    al.release("w2_stg0", "w2_stg1", "w2_abs0", "w2_abs1")

    t_wdsl = [SBA(f"d_wd{i}", [128, 32 * 512], BF16) for i in range(2)]
    t_fstg2 = [SBA(f"d_fs{i}", [128, 512], F32) for i in range(2)]
    t_qstg2 = SBA("d_qs", [128, 512], F32)
    t_ystg = [SBA(f"d_ys{i}", [128, 512], F32) for i in range(2)]
    rot_wsl2 = Rot(2)
    rot_fst2 = Rot(2)
    rot_qst2 = Rot(1)
    rot_yst2 = Rot(2)
    rot_db = Rot(2)
    for fsl in range(4):
        wsl = t_wdsl[fsl % 2]
        ev_q3 = None
        for fc in range(32):
            ev_ld = S_.op("y", lambda e, fc=fc, fsl=fsl: e.dma_start(
                out=t_fstg2[fc % 2][:], in_=din["wdnt"][fc * 128:(fc + 1) * 128,
                                                        fsl * 512:(fsl + 1) * 512]),
                wait=rot_fst2.wait(fc) + (rot_wsl2.wait(fsl) if fc == 0 else ()),
                inc="ydma", dma=True)
            ev_q1 = S_.op("s", lambda e, fc=fc: e.activation(
                t_qstg2[:], t_fstg2[fc % 2][:], AF.Copy, scale=t_rdnb[:]),
                wait=(ev_ld,) + rot_qst2.wait(0), inc="s")
            rot_fst2.set(fc, ev_q1)
            _e = S_.op("g", lambda e: e.tensor_scalar(t_qstg2[:], t_qstg2[:], MROUND, MROUND,
                                                      ALU.add, ALU.subtract), wait=(ev_q1,), inc="g")
            ev_q3 = S_.op("g", lambda e, wsl=wsl, fc=fc: e.tensor_scalar(
                wsl[:, fc * 512:(fc + 1) * 512], t_qstg2[:], 1.0, -1.0, ALU.min, ALU.max),
                wait=(_e,), inc="g")
            rot_qst2.set(0, ev_q3)
        ev_last = None
        for tt in range(8):
            ti = fsl * 8 + tt
            ys = t_ystg[tt % 2]
            ev_y = S_.op("y", lambda e, ys=ys, tt=tt, fsl=fsl: e.dma_start(
                out=ys[:], in_=d_yln[tt * 128:(tt + 1) * 128, fsl * 512:(fsl + 1) * 512]),
                wait=rot_yst2.wait(tt), inc="ydma", dma=True)
            bank = ti % 2
            ev_mm = None
            for fc in range(32):
                w = (rot_db.wait(ti) if fc == 0 else ())
                if fc == 0 and tt == 0:
                    w = w + (ev_q3,)
                ev_mm = S_.op("t", lambda e, bank=bank, fc=fc, tt=tt, wsl=wsl: e.matmul(
                    pbank[bank][:],
                    t_xq2[:, fc * 1024 + tt * 128: fc * 1024 + tt * 128 + 128],
                    wsl[:, fc * 512:(fc + 1) * 512],
                    start=(fc == 0), stop=(fc == 31)),
                    wait=w, inc="t" if fc == 31 else None)
            ev_st = S_.op("v", lambda e, bank=bank, ys=ys, tt=tt: e.scalar_tensor_tensor(
                ys[:], pbank[bank][:], t_dq2[:, tt:tt + 1], ys[:], ALU.mult, ALU.add),
                wait=(ev_mm, ev_y), inc="v")
            rot_db.set(ti, ev_st)
            ev_wb = S_.op("y", lambda e, ys=ys, tt=tt, fsl=fsl: e.dma_start(
                out=d_yln[tt * 128:(tt + 1) * 128, fsl * 512:(fsl + 1) * 512], in_=ys[:]),
                wait=(ev_st,), inc="ydma", dma=True)
            rot_yst2.set(tt, ev_wb)
            ev_last = ev_wb
        rot_wsl2.set(fsl, ev_last)
    S_.barrier(["t", "s", "v", "y", "g"])

    if dbg == "z2":
        o = dbg_out("d_z2", [NQ, D])
        o2 = dbg_out("d_dq2", [128, 16])
        o5 = dbg_out("d_q2m", [128, 16])
        o6 = dbg_out("d_s2", [128, 128])
        S_.op("y", lambda e, o5=o5: e.dma_start(out=o5, in_=t_q2m[:]), inc="o", dma=True)
        S_.op("y", lambda e, o6=o6: e.dma_start(out=o6, in_=t_st2[:]), inc="o", dma=True)
        S_.op("y", lambda e, o=o: e.dma_start(out=o, in_=d_yln), inc="o", dma=True)
        S_.op("y", lambda e, o2=o2: e.dma_start(out=o2, in_=t_dq2[:]), inc="o", dma=True)
        S_.wait("y", S_.ev("o"))
        return _finish(nc, ctx, S_, ddbg)

    rot_o = Rot(2)
    for tt in range(8):
        ys = t_yst[tt % 2]
        ev_ld = S_.op("y", lambda e, ys=ys, tt=tt: e.dma_start(
            out=ys[:], in_=d_yln[tt * 128:(tt + 1) * 128, :]),
            wait=rot_o.wait(tt), inc="ydma", dma=True)
        ev_ln = ln_inplace(ys, ev_ld)
        ev_o = S_.op("y", lambda e, ys=ys, tt=tt: e.dma_start(
            out=dout[tt * 128:(tt + 1) * 128, :], in_=ys[:]), wait=(ev_ln,), inc="o", dma=True)
        rot_o.set(tt, ev_o)
    S_.wait("y", S_.ev("o"))
    return _finish(nc, ctx, S_, ddbg)


def _finish(nc, ctx, S_, ddbg):
    with nc.Block() as blk:
        @blk.tensor
        def _(e):
            S_.replay("t", e)

        @blk.vector
        def _(e):
            S_.replay("v", e)

        @blk.scalar
        def _(e):
            S_.replay("s", e)

        @blk.gpsimd
        def _(e):
            S_.replay("g", e)

        @blk.sync
        def _(e):
            S_.replay("y", e)
    ctx.close()
    return nc, ddbg


def _host_inputs(inputs, core):
    b, stripe = core // 2, core % 2
    qb0, qb1 = _core_qrows(stripe)
    qrows = np.concatenate([qb0, qb1])
    x = np.ascontiguousarray(inputs["L0_attn_hidden_states"][b])
    xq = np.ascontiguousarray(x[qrows])
    pe, po = _col_perm_rope_q()
    ke, ko = _col_perm_rope_k()
    cosq, sinq = _rope_tables(qrows)
    cosk, sink = _rope_tables(np.arange(S))
    wo_rows = np.concatenate([np.arange(h * 192, h * 192 + 128) for h in range(H)])
    return {
        "x": x, "xq": xq,
        "wq": inputs["L0_attn_Wq"],
        "wqe": np.ascontiguousarray(inputs["L0_attn_Wqrope"][:, pe]),
        "wqo": np.ascontiguousarray(inputs["L0_attn_Wqrope"][:, po]),
        "wkvc": inputs["L0_attn_Wkvc"],
        "wkdec": inputs["L0_attn_Wkdec"],
        "wvdec": inputs["L0_attn_Wvdec"],
        "wke": np.ascontiguousarray(inputs["L0_attn_Wkrope"][:, ke]),
        "wko": np.ascontiguousarray(inputs["L0_attn_Wkrope"][:, ko]),
        "wo": np.ascontiguousarray(inputs["L0_attn_Wo"][wo_rows]),
        "wupt": np.ascontiguousarray(inputs["L0_ffn_w_up"].T),
        "wdnt": np.ascontiguousarray(inputs["L0_ffn_w_down"].T),
        "cosq": cosq, "sinq": sinq, "cosk": cosk, "sink": sink,
        "mask": np.asarray(_core_masks(stripe)),
    }


def kernel(**inputs):
    nc, _ = build_program(dbg=None)
    in_maps = [_host_inputs(inputs, c) for c in range(8)]
    res = run_bass_kernel_spmd(nc, in_maps, core_ids=list(range(8)))
    out = np.zeros((B, S, D), dtype=np.float32)
    for c in range(8):
        b, stripe = c // 2, c % 2
        qb0, qb1 = _core_qrows(stripe)
        o = res.results[c]["out"]
        out[b, qb0] = o[:512]
        out[b, qb1] = o[512:]
    return out


# version 09
# speedup vs baseline: 1.6978x; 1.6978x over previous
import numpy as np
import ml_dtypes
from contextlib import ExitStack
import concourse.bass as bass
import concourse.mybir as mybir
from concourse.bass_utils import run_bass_kernel_spmd

F32 = mybir.dt.float32
BF16 = mybir.dt.bfloat16
AF = mybir.ActivationFunctionType
ALU = mybir.AluOpType
AX = mybir.AxisListType
bfd = ml_dtypes.bfloat16

B, S, D = 4, 2048, 2048
H, KV, HD, RD = 16, 4, 128, 64
FULL = HD + RD
NQ = 1024
F2 = 4096
MROUND = float(np.float32(1.5 * 2.0 ** 23))
EPS_LN = 1e-5
EPS_Q = 1e-5
QBITS = 127.0
SCALE_ATT = 1.0 / float(np.sqrt(FULL))
C1 = 1.0 / float(np.sqrt(D))
C2 = 1.0 / float(np.sqrt(F2))


def _hadamard(n):
    h = np.array([[1.0]], dtype=np.float64)
    while h.shape[0] < n:
        h = np.block([[h, h], [h, -h]])
    return h.astype(np.float32)


def _rope_tables(positions):
    fr = (1.0 / 10000.0 ** (np.arange(0, RD, 2, dtype=np.float64) / RD))
    ang = np.outer(fr, positions.astype(np.float64))
    ang = np.tile(ang, (4, 1))
    return np.cos(ang).astype(np.float32), np.sin(ang).astype(np.float32)


def _core_qrows(stripe):
    if stripe == 0:
        return np.arange(0, 512), np.arange(1536, 2048)
    return np.arange(512, 1024), np.arange(1024, 1536)


def _core_masks(stripe):
    qb0, qb1 = _core_qrows(stripe)
    m = np.zeros((128, 16 * 512), dtype=np.float32)
    for i in range(8):
        kpos = i * 128 + np.arange(128)
        m[:, i * 512:(i + 1) * 512] = (kpos[:, None] <= qb0[None, :])
    for i in range(8):
        kpos = (8 + i) * 128 + np.arange(128)
        m[:, (8 + i) * 512:(9 + i) * 512] = (kpos[:, None] <= qb1[None, :])
    return m.astype(bfd)


def _col_perm_rope_q():
    even, odd = [], []
    for g in range(4):
        for p in range(128):
            h = (p // 32) * 4 + g
            f = p % 32
            even.append(h * RD + 2 * f)
            odd.append(h * RD + 2 * f + 1)
    return np.array(even), np.array(odd)


def _col_perm_rope_k():
    even = [(p // 32) * RD + 2 * (p % 32) for p in range(128)]
    odd = [c + 1 for c in even]
    return np.array(even), np.array(odd)


class Alloc:
    def __init__(self, nc):
        self.nc = nc
        base = (nc.sbuf_base + 31) & ~31
        self.free = [(base, nc.sbuf_top - base)]
        self.live = {}

    def alloc(self, name, shape, dt):
        esz = 2 if dt == BF16 else 4
        nbytes = (int(np.prod(shape[1:])) * esz + 31) & ~31
        cands = [(sz, off, i) for i, (off, sz) in enumerate(self.free) if sz >= nbytes]
        cands.sort()
        for sz, off, i in cands[:1]:
            if True:
                t = self.nc.alloc_sbuf_tensor_at(name, list(shape), dt, offset=off)
                self.live[name] = (off, nbytes)
                if sz == nbytes:
                    self.free.pop(i)
                else:
                    self.free[i] = (off + nbytes, sz - nbytes)
                return t
        raise MemoryError(f"SBUF alloc fail {name} {nbytes}B free={self.free}")

    def release(self, *names):
        for name in names:
            off, nbytes = self.live.pop(name)
            self.free.append((off, nbytes))
        self.free.sort()
        merged = []
        for o, s in self.free:
            if merged and merged[-1][0] + merged[-1][1] == o:
                merged[-1][1] += s
            else:
                merged.append([o, s])
        self.free = [tuple(x) for x in merged]


class Sched:
    ENGS = ("t", "v", "s", "g", "y")

    def __init__(self):
        self.q = {e: [] for e in self.ENGS}
        self.cnt = {}
        self.sems = {}
        self.dirty = {e: False for e in self.ENGS}
        self.last = {e: None for e in self.ENGS}

    def add_sem(self, name, sem):
        self.sems[name] = sem
        self.cnt[name] = 0

    def ev(self, sname):
        return (sname, self.cnt[sname])

    def op(self, eng, fn, wait=(), inc=None, dma=False):
        incs = []
        wait = tuple(w for w in wait if w is not None)
        if inc:
            for sname in ([inc] if isinstance(inc, str) else inc):
                self.cnt[sname] += 16 if dma else 1
                incs.append((sname, 16 if dma else 1))
        self.q[eng].append((wait, fn, tuple(incs)))
        if incs:
            self.dirty[eng] = False
            nm = incs[-1][0]
            self.last[eng] = (nm, self.cnt[nm])
            return (nm, self.cnt[nm])
        self.dirty[eng] = True
        return None

    def barrier(self, engs=None):
        engs = engs or [e for e in self.ENGS if self.q[e]]
        evs = []
        for e in engs:
            if self.dirty[e]:
                raise RuntimeError(f"barrier: engine {e} has trailing ops without inc")
            if self.last[e] is not None:
                evs.append(self.last[e])
        for e in engs:
            self.q[e].append((tuple(evs), None, ()))

    def wait(self, eng, *evs):
        self.q[eng].append((tuple(e for e in evs if e), None, ()))

    def replay(self, key, engine):
        for waits, fn, incs in self.q[key]:
            for (sname, val) in waits:
                engine.wait_ge(self.sems[sname], val)
            if fn is not None:
                ins = fn(engine)
                for (sname, step) in incs:
                    ins = ins.then_inc(self.sems[sname], step)


class Rot:
    def __init__(self, n):
        self.n = n
        self.ev = [None] * n

    def wait(self, i):
        e = self.ev[i % self.n]
        return (e,) if e is not None else ()

    def set(self, i, e):
        self.ev[i % self.n] = e


def build_program(dbg=None):
    nc = bass.Bass("TRN2", target_bir_lowering=False, debug=False)
    ctx = ExitStack()

    din = {}
    for name, shp, dt in [
        ("x", [S, D], F32), ("xq", [NQ, D], F32),
        ("wq", [D, H * HD], F32), ("wqe", [D, 512], F32), ("wqo", [D, 512], F32),
        ("wkvc", [D, 512], F32), ("wkdec", [512, 512], F32), ("wvdec", [512, 512], F32),
        ("wke", [512, 128], F32), ("wko", [512, 128], F32),
        ("wo", [2048, D], F32),
        ("wupt", [D, F2], F32), ("wdnt", [F2, D], F32),
        ("cosq", [128, NQ], F32), ("sinq", [128, NQ], F32),
        ("cosk", [128, S], F32), ("sink", [128, S], F32),
        ("mask", [128, 16 * 512], BF16),
    ]:
        din[name] = nc.dram_tensor(name, shp, dt, kind="ExternalInput").ap()
    dout = nc.dram_tensor("out", [NQ, D], F32, kind="ExternalOutput").ap()
    d_yln = nc.dram_tensor("yln_scratch", [NQ, D], F32, kind="Internal").ap()
    ddbg = {}

    def dbg_out(name, shp, dt=F32):
        ddbg[name] = nc.dram_tensor(name, shp, dt, kind="ExternalOutput").ap()
        return ddbg[name]

    h128_bf = nc.inline_tensor(_hadamard(128).astype(bfd), name="h128bf").ap()
    h128_f = nc.inline_tensor(_hadamard(128), name="h128f").ap()
    id_bf = nc.inline_tensor(np.eye(128, dtype=bfd), name="idbf").ap()
    id_f = nc.inline_tensor(np.eye(128, dtype=np.float32), name="idf").ap()
    ones_c_bf = nc.inline_tensor(np.ones((128, 1), dtype=bfd), name="onescbf").ap()
    ones_r_f = nc.inline_tensor(np.ones((1, 128), dtype=np.float32), name="onesrf").ap()
    sel8_np = np.zeros((8, 8 * 128), dtype=np.float32)
    for _j in range(8):
        sel8_np[_j, _j * 128:(_j + 1) * 128] = 1.0
    sel8_c = nc.inline_tensor(sel8_np, name="sel8c").ap()

    al = Alloc(nc)
    SBA = al.alloc
    pbank = [ctx.enter_context(nc.psum_tensor(f"pb{i}", [128, 512], F32)) for i in range(8)]

    S_ = Sched()
    for sname in ("ydma", "gdma", "sdma", "t", "v", "s", "g", "o"):
        S_.add_sem(sname, ctx.enter_context(nc.semaphore(sname)))

    t_cosq = SBA("m_cosq", [128, NQ], F32)
    t_sinq = SBA("m_sinq", [128, NQ], F32)
    t_h128bf = SBA("m_h128bf", [128, 128], BF16)
    t_h128f = SBA("m_h128f", [128, 128], F32)
    t_idbf = SBA("m_idbf", [128, 128], BF16)
    t_idf = SBA("m_idf", [128, 128], F32)
    t_ones_c = SBA("m_onescbf", [128, 1], BF16)
    t_ones_r = SBA("m_onesrf", [1, 128], F32)
    t_st = SBA("m_stats", [128, 1024], F32)
    t_st2 = SBA("m_stats2", [128, 128], F32)
    t_trbf = SBA("m_trbf", [128, 1024], BF16)
    t_q1m = SBA("m_q1m", [128, 16], F32)
    t_d2 = SBA("m_d2", [128, 16], F32)
    t_q2m = SBA("m_q2m", [128, 16], F32)
    t_dq2 = SBA("m_dq2", [128, 16], F32)
    t_wupb = SBA("m_wupb", [128, 1], F32)
    t_rupb = SBA("m_rupb", [128, 1], F32)
    t_wdnb = SBA("m_wdnb", [128, 1], F32)
    t_rdnb = SBA("m_rdnb", [128, 1], F32)
    t_wsc = SBA("m_wsc", [1, 8], F32)
    t_row = [SBA(f"m_row{i}", [1, 512], F32) for i in range(2)]
    t_B = SBA("m_B", [128, 1024], F32)
    t_rbs = [SBA(f"m_rbs{i}", [128, 512], F32) for i in range(2)]
    t_rl = SBA("m_rl", [128, 512], BF16)
    t_sel8 = SBA("m_sel8", [8, 8 * 128], F32)
    t_rowT = SBA("m_rowT", [8, 128], F32)
    t_gapa = SBA("m_gapa", [128, 8], F32)
    t_gapb = SBA("m_gapb", [128, 8], F32)

    def vg():
        return S_.op("v", lambda e: e.tensor_copy(t_gapa[:], t_gapb[:]), inc="v")

    def tvop(vfn, psum_ap, shadow, wait=()):
        S_.op("v", vfn, wait=wait, inc="v")
        ev = vg()
        return S_.op("s", lambda e, s=shadow, p=psum_ap: e.copy(s, p), wait=(ev,), inc="s")

    for dst, src in [(t_cosq[:], din["cosq"]), (t_sinq[:], din["sinq"]),
                     (t_h128bf[:], h128_bf), (t_h128f[:], h128_f), (t_idbf[:], id_bf),
                     (t_idf[:], id_f), (t_ones_c[:], ones_c_bf), (t_ones_r[:], ones_r_f),
                     (t_sel8[:], sel8_c)]:
        S_.op("s", lambda e, d=dst, s=src: e.dma_start(out=d, in_=s), inc="sdma", dma=True)
    ev_perm = S_.ev("sdma")
    for _e in ("t", "v", "g", "y"):
        S_.wait(_e, ev_perm)

    def ln_inplace(yw, ev_in):
        ch = [ev_in]

        def vq(fn, extra=()):
            S_.op("v", fn, wait=(ch[0],) + tuple(extra), inc="v")
            ch[0] = vg()
            return ch[0]

        for seg in range(4):
            vq(lambda e, seg=seg: e.bn_stats(
                t_st[:, seg * 6:(seg + 1) * 6], yw[:, seg * 512:(seg + 1) * 512]))
        stv = t_st[:, 0:24].rearrange("p (g c) -> p g c", c=3)
        vq(lambda e: e.reduce_sum(t_st[:, 24:25], stv[:, :, 1], axis=AX.X))
        vq(lambda e: e.tensor_scalar(t_st[:, 25:26], t_st[:, 24:25], 1.0 / 8.0, None, ALU.mult))
        vq(lambda e: e.reduce_sum(t_st[:, 26:27], stv[:, :, 2], axis=AX.X))
        vq(lambda e: e.tensor_tensor(t_st2[:, 0:8], stv[:, :, 1], stv[:, :, 1], ALU.mult))
        vq(lambda e: e.reduce_sum(t_st[:, 27:28], t_st2[:, 0:8], axis=AX.X))
        vq(lambda e: e.tensor_scalar(t_st[:, 27:28], t_st[:, 27:28], 256.0, None, ALU.mult))
        vq(lambda e: e.tensor_add(t_st[:, 26:27], t_st[:, 26:27], t_st[:, 27:28]))
        vq(lambda e: e.tensor_scalar(t_st[:, 26:27], t_st[:, 26:27], 1.0 / 2048.0, None, ALU.mult))
        vq(lambda e: e.tensor_tensor(t_st[:, 28:29], t_st[:, 25:26], t_st[:, 25:26], ALU.mult))
        vq(lambda e: e.tensor_sub(t_st[:, 26:27], t_st[:, 26:27], t_st[:, 28:29]))
        vq(lambda e: e.tensor_scalar(t_st[:, 26:27], t_st[:, 26:27], EPS_LN, None, ALU.add))
        ev_v = vq(lambda e: e.reciprocal(t_st[:, 29:30], t_st[:, 26:27]))
        ev_sq = S_.op("s", lambda e: e.activation(t_st[:, 30:31], t_st[:, 29:30], AF.Sqrt),
                      wait=(ev_v,), inc="s")
        ch[0] = ev_sq
        vq(lambda e: e.tensor_tensor(t_st[:, 31:32], t_st[:, 30:31], t_st[:, 30:31], ALU.mult))
        vq(lambda e: e.tensor_mul(t_st[:, 31:32], t_st[:, 31:32], t_st[:, 26:27]))
        vq(lambda e: e.tensor_scalar(t_st[:, 31:32], t_st[:, 31:32], -1.0, 3.0, ALU.mult, ALU.add))
        vq(lambda e: e.tensor_mul(t_st[:, 31:32], t_st[:, 31:32], t_st[:, 30:31]))
        vq(lambda e: e.tensor_scalar(t_st[:, 32:33], t_st[:, 31:32], 0.5, None, ALU.mult))
        vq(lambda e: e.tensor_mul(t_st[:, 33:34], t_st[:, 25:26], t_st[:, 32:33]))
        vq(lambda e: e.tensor_scalar(t_st[:, 33:34], t_st[:, 33:34], -1.0, None, ALU.mult))
        return vq(lambda e: e.scalar_tensor_tensor(
            yw[:], yw[:], t_st[:, 32:33], t_st[:, 33:34].broadcast_to([128, 2048]),
            ALU.mult, ALU.add))

    def bcast_col(src11, dst_col, bank):
        ev_mm = S_.op("t", lambda e: e.matmul(pbank[bank][:, 0:1], t_ones_r[:], src11,
                                              start=True, stop=True), inc="t")
        return S_.op("v", lambda e: e.tensor_copy(dst_col, pbank[bank][:, 0:1]),
                     wait=(ev_mm,), inc="v")

    t_xT = SBA("a_xT", [128, 16 * 2048], BF16)
    t_stage = [SBA(f"a_stg{i}", [128, 2048], BF16) for i in range(2)]
    t_wkvc = SBA("a_wkvc", [128, 16 * 512], BF16)
    t_cT = SBA("a_cT", [128, 4 * 2048], BF16)

    ev_wkvc = S_.op("g", lambda e: e.dma_start(
        out=t_wkvc[:].rearrange("p (dc c) -> p dc c", dc=16),
        in_=din["wkvc"].rearrange("(dc p) c -> p dc c", p=128)), inc="gdma", dma=True)

    def stream_transpose(src_dram, nrow_tiles, t_dst, chunk_stride):
        rot_stage = Rot(2)
        rot_bank = Rot(4)
        last = None
        for r in range(nrow_tiles):
            st = t_stage[r % 2]
            ev_ld = S_.op("g", lambda e, st=st, r=r: e.dma_start(
                out=st[:], in_=src_dram[r * 128:(r + 1) * 128, :]),
                wait=rot_stage.wait(r), inc="gdma", dma=True)
            for cg in range(4):
                bank = 2 * (r % 2) + (cg % 2)
                ev_t = None
                for ci in range(4):
                    c = cg * 4 + ci
                    w2 = ((ev_ld,) if (cg == 0 and ci == 0) else ())
                    if ci == 0:
                        w2 = w2 + rot_bank.wait(bank)
                    ev_t = S_.op("t", lambda e, st=st, c=c, bank=bank, ci=ci: e.transpose(
                        pbank[bank][:].bitcast(BF16)[:, ci * 128:(ci + 1) * 128],
                        st[:, c * 128:(c + 1) * 128], t_idbf[:]),
                        wait=w2, inc="t" if ci == 3 else None)
                dst = t_dst[:].rearrange("p (dc t) -> p dc t", t=chunk_stride)[
                    :, cg * 4:(cg + 1) * 4, r * 128:(r + 1) * 128]
                ev_e = S_.op("v", lambda e, dst=dst, bank=bank: e.tensor_copy(
                    dst, pbank[bank][:].bitcast(BF16)[:, 0:512]), wait=(ev_t,), inc="v")
                rot_bank.set(bank, ev_e)
                last = ev_e
            rot_stage.set(r, last)
        return last

    ev_xT = stream_transpose(din["x"], 16, t_xT, 2048)
    S_.barrier(["t", "v", "g", "s"])

    rot_c = Rot(2)
    for tcol in range(4):
        for cf in range(4):
            it = tcol * 4 + cf
            bank = 4 + it % 2
            ev_mm = None
            for dc in range(16):
                w = rot_c.wait(it) if dc == 0 else ()
                ev_mm = S_.op("t", lambda e, bank=bank, dc=dc, cf=cf, tcol=tcol: e.matmul(
                    pbank[bank][:],
                    t_wkvc[:, dc * 512 + cf * 128: dc * 512 + (cf + 1) * 128],
                    t_xT[:, dc * 2048 + tcol * 512: dc * 2048 + (tcol + 1) * 512],
                    start=(dc == 0), stop=(dc == 15)),
                    wait=w, inc="t" if dc == 15 else None)
            ev_ev = S_.op("s", lambda e, bank=bank, cf=cf, tcol=tcol: e.copy(
                t_cT[:, cf * 2048 + tcol * 512: cf * 2048 + (tcol + 1) * 512], pbank[bank][:]),
                wait=(ev_mm,), inc="s")
            rot_c.set(it, ev_ev)
    S_.barrier(["t", "s", "v", "g"])
    al.release("a_xT", "a_wkvc")

    t_kn = SBA("a_kn", [128, KV * 2048], BF16)
    t_vtm = SBA("a_vtm", [128, 16 * 512], BF16)
    t_ke = SBA("a_ke", [128, 2048], BF16)
    t_ko = SBA("a_ko", [128, 2048], BF16)
    t_wkdec = SBA("a_wkdec", [128, 4 * 512], BF16)
    t_wvdec = SBA("a_wvdec", [128, 4 * 512], BF16)
    t_wke = SBA("a_wke", [128, 4 * 128], BF16)
    t_wko = SBA("a_wko", [128, 4 * 128], BF16)
    t_cosk = SBA("a_cosk", [128, S], F32)
    t_sink = SBA("a_sink", [128, S], F32)
    t_ker = SBA("a_ker", [128, 2048], F32)
    t_kor = SBA("a_kor", [128, 2048], F32)
    t_rs1 = SBA("a_rs1", [128, 2048], F32)
    t_rs2 = SBA("a_rs2", [128, 2048], F32)

    for dst, src in [(t_cosk[:], din["cosk"]), (t_sink[:], din["sink"])]:
        S_.op("s", lambda e, d=dst, s=src: e.dma_start(out=d, in_=s), inc="sdma", dma=True)
    for t_w, dname in [(t_wkdec, "wkdec"), (t_wvdec, "wvdec"), (t_wke, "wke"), (t_wko, "wko")]:
        S_.op("g", lambda e, t_w=t_w, dname=dname: e.dma_start(
            out=t_w[:].rearrange("p (cc c) -> p cc c", cc=4),
            in_=din[dname].rearrange("(cc p) c -> p cc c", p=128)), inc="gdma", dma=True)
    ev_kw = S_.ev("gdma")
    ev_kcs = S_.ev("sdma")

    rot_k = Rot(4)
    it = 0
    for kv in range(KV):
        for kcol in range(4):
            bank = it % 4
            ev_mm = None
            for cc in range(4):
                w = rot_k.wait(it) if cc == 0 else ()
                if it == 0 and cc == 0:
                    w = w + (ev_kw,)
                ev_mm = S_.op("t", lambda e, bank=bank, cc=cc, kv=kv, kcol=kcol: e.matmul(
                    pbank[bank][:],
                    t_wkdec[:, cc * 512 + kv * 128: cc * 512 + (kv + 1) * 128],
                    t_cT[:, cc * 2048 + kcol * 512: cc * 2048 + (kcol + 1) * 512],
                    start=(cc == 0), stop=(cc == 3)),
                    wait=w, inc="t" if cc == 3 else None)
            ev_ev = S_.op("s", lambda e, bank=bank, kv=kv, kcol=kcol: e.copy(
                t_kn[:, kv * 2048 + kcol * 512: kv * 2048 + (kcol + 1) * 512], pbank[bank][:]),
                wait=(ev_mm,), inc="s")
            rot_k.set(it, ev_ev)
            it += 1
    for kt in range(16):
        bank = it % 4
        ev_mm = None
        for cc in range(4):
            w = rot_k.wait(it) if cc == 0 else ()
            ev_mm = S_.op("t", lambda e, bank=bank, cc=cc, kt=kt: e.matmul(
                pbank[bank][:],
                t_cT[:, cc * 2048 + kt * 128: cc * 2048 + kt * 128 + 128],
                t_wvdec[:, cc * 512:(cc + 1) * 512],
                start=(cc == 0), stop=(cc == 3)),
                wait=w, inc="t" if cc == 3 else None)
        ev_ev = S_.op("s", lambda e, bank=bank, kt=kt: e.copy(
            t_vtm[:, kt * 512:(kt + 1) * 512], pbank[bank][:]), wait=(ev_mm,), inc="s")
        rot_k.set(it, ev_ev)
        it += 1
    for t_dst, t_w in [(t_ker, t_wke), (t_kor, t_wko)]:
        for kcol in range(4):
            bank = it % 4
            ev_mm = None
            for cc in range(4):
                w = rot_k.wait(it) if cc == 0 else ()
                ev_mm = S_.op("t", lambda e, bank=bank, cc=cc, kcol=kcol, t_w=t_w: e.matmul(
                    pbank[bank][:],
                    t_w[:, cc * 128:(cc + 1) * 128],
                    t_cT[:, cc * 2048 + kcol * 512: cc * 2048 + (kcol + 1) * 512],
                    start=(cc == 0), stop=(cc == 3)),
                    wait=w, inc="t" if cc == 3 else None)
            ev_ev = S_.op("s", lambda e, bank=bank, kcol=kcol, t_dst=t_dst: e.copy(
                t_dst[:, kcol * 512:(kcol + 1) * 512], pbank[bank][:]), wait=(ev_mm,), inc="s")
            rot_k.set(it, ev_ev)
            it += 1
    S_.barrier(["t", "s", "v", "g"])
    S_.wait("v", ev_kcs)
    _e = S_.op("v", lambda e: e.tensor_mul(t_rs1[:], t_ker[:], t_cosk[:]), inc="v")
    _e = S_.op("v", lambda e: e.tensor_mul(t_rs2[:], t_kor[:], t_sink[:]), wait=(_e,), inc="v")
    _e = S_.op("v", lambda e: e.tensor_sub(t_ke[:], t_rs1[:], t_rs2[:]), wait=(_e,), inc="v")
    _e = S_.op("v", lambda e: e.tensor_mul(t_rs1[:], t_ker[:], t_sink[:]), wait=(_e,), inc="v")
    _e = S_.op("v", lambda e: e.tensor_mul(t_rs2[:], t_kor[:], t_cosk[:]), wait=(_e,), inc="v")
    _e = S_.op("v", lambda e: e.tensor_add(t_ko[:], t_rs1[:], t_rs2[:]), wait=(_e,), inc="v")
    t_keo = [SBA(f"a_keo{i}", [128, 2048], BF16) for i in range(2)]
    for g4 in range(4):
        for side, tsrc in enumerate((t_ke, t_ko)):
            S_.op("g", lambda e, g4=g4, side=side, tsrc=tsrc: e.dma_start(
                out=t_keo[g4 // 2][(g4 % 2) * 64 + side * 32:(g4 % 2) * 64 + side * 32 + 32, :],
                in_=tsrc[g4 * 32:(g4 + 1) * 32, :]), wait=(_e,), inc="gdma", dma=True)
    S_.barrier(["t", "s", "v", "g"])
    al.release("a_wkdec", "a_wvdec", "a_wke", "a_wko", "a_cosk", "a_sink",
               "a_ker", "a_kor", "a_rs1", "a_rs2", "a_cT", "a_ke", "a_ko")

    t_xqT = SBA("a_xqT", [128, 16 * NQ], BF16)
    t_qn = SBA("a_qn", [128, H * NQ], BF16)
    t_qe = SBA("a_qe", [128, 4 * NQ], BF16)
    t_qo = SBA("a_qo", [128, 4 * NQ], BF16)
    stream_transpose(din["xq"], 8, t_xqT, NQ)
    S_.barrier(["t", "v", "g"])
    al.release("a_stg0", "a_stg1")

    t_wqs = [SBA(f"a_wqs{i}", [128, 16 * 512], BF16) for i in range(2)]
    rot_q = Rot(4)
    it = 0
    ev_slab_used = [None, None]
    for slab in range(4):
        sb = t_wqs[slab % 2]
        ev_wq = S_.op("g", lambda e, sb=sb, slab=slab: e.dma_start(
            out=sb[:].rearrange("p (dc c) -> p dc c", dc=16),
            in_=din["wq"][:, slab * 512:(slab + 1) * 512].rearrange("(dc p) c -> p dc c", p=128)),
            wait=(ev_slab_used[slab % 2],) if ev_slab_used[slab % 2] else (), inc="gdma", dma=True)
        last_mm = None
        for hh in range(4):
            h = slab * 4 + hh
            for qcol in range(2):
                bank = it % 4
                ev_mm = None
                for dc in range(16):
                    w = rot_q.wait(it) if dc == 0 else ()
                    if dc == 0 and hh == 0 and qcol == 0:
                        w = w + (ev_wq,)
                    ev_mm = S_.op("t", lambda e, bank=bank, dc=dc, hh=hh, qcol=qcol, sb=sb: e.matmul(
                        pbank[bank][:],
                        sb[:, dc * 512 + hh * 128: dc * 512 + (hh + 1) * 128],
                        t_xqT[:, dc * NQ + qcol * 512: dc * NQ + (qcol + 1) * 512],
                        start=(dc == 0), stop=(dc == 15)),
                        wait=w, inc="t" if dc == 15 else None)
                last_mm = ev_mm
                ev_ev = S_.op("s", lambda e, bank=bank, h=h, qcol=qcol: e.copy(
                    t_qn[:, h * NQ + qcol * 512: h * NQ + (qcol + 1) * 512], pbank[bank][:]),
                    wait=(ev_mm,), inc="s")
                rot_q.set(it, ev_ev)
                it += 1
        ev_slab_used[slab % 2] = last_mm
    al.release("a_wqs0", "a_wqs1")
    t_wqe = SBA("a_wqe", [128, 16 * 512], BF16)
    t_wqo = SBA("a_wqo", [128, 16 * 512], BF16)
    t_re = t_B[:, 0:512]
    t_ro = t_B[:, 512:1024]
    t_m1 = t_st[:, 0:512]
    t_m2 = t_st[:, 512:1024]
    ev_we = S_.op("g", lambda e: e.dma_start(
        out=t_wqe[:].rearrange("p (dc c) -> p dc c", dc=16),
        in_=din["wqe"].rearrange("(dc p) c -> p dc c", p=128)),
        wait=(ev_slab_used[0],), inc="gdma", dma=True)
    ev_wo_ = S_.op("g", lambda e: e.dma_start(
        out=t_wqo[:].rearrange("p (dc c) -> p dc c", dc=16),
        in_=din["wqo"].rearrange("(dc p) c -> p dc c", p=128)),
        wait=(ev_slab_used[1],), inc="gdma", dma=True)
    ev_cmb = None
    for g in range(4):
        for qcol in range(2):
            evs = []
            for side, (t_w, t_raw, evw) in enumerate([(t_wqe, t_re, ev_we), (t_wqo, t_ro, ev_wo_)]):
                bank = (it + side) % 4
                ev_mm = None
                for dc in range(16):
                    w = rot_q.wait(it + side) if dc == 0 else ()
                    if dc == 0 and g == 0 and qcol == 0:
                        w = w + (evw,)
                    ev_mm = S_.op("t", lambda e, bank=bank, dc=dc, g=g, qcol=qcol, t_w=t_w: e.matmul(
                        pbank[bank][:],
                        t_w[:, dc * 512 + g * 128: dc * 512 + (g + 1) * 128],
                        t_xqT[:, dc * NQ + qcol * 512: dc * NQ + (qcol + 1) * 512],
                        start=(dc == 0), stop=(dc == 15)),
                        wait=w, inc="t" if dc == 15 else None)
                ev_ev = S_.op("s", lambda e, bank=bank, t_raw=t_raw: e.copy(
                    t_raw, pbank[bank][:]),
                    wait=(ev_mm,) + ((ev_cmb,) if ev_cmb else ()), inc="s")
                evs.append(ev_ev)
                rot_q.set(it + side, ev_ev)
            cqs = t_cosq[:, qcol * 512:(qcol + 1) * 512]
            sqs = t_sinq[:, qcol * 512:(qcol + 1) * 512]
            qeo = t_qe[:, g * NQ + qcol * 512: g * NQ + (qcol + 1) * 512]
            qoo = t_qo[:, g * NQ + qcol * 512: g * NQ + (qcol + 1) * 512]
            _e = S_.op("v", lambda e, cqs=cqs: e.tensor_tensor(t_m1, t_re, cqs, ALU.mult),
                       wait=(evs[0], evs[1]), inc="v")
            _e = S_.op("v", lambda e, sqs=sqs: e.tensor_tensor(t_m2, t_ro, sqs, ALU.mult),
                       wait=(_e,), inc="v")
            _e = S_.op("v", lambda e, qeo=qeo: e.tensor_sub(qeo, t_m1, t_m2), wait=(_e,), inc="v")
            _e = S_.op("v", lambda e, sqs=sqs: e.tensor_tensor(t_m1, t_re, sqs, ALU.mult),
                       wait=(_e,), inc="v")
            _e = S_.op("v", lambda e, cqs=cqs: e.tensor_tensor(t_m2, t_ro, cqs, ALU.mult),
                       wait=(_e,), inc="v")
            ev_cmb = S_.op("v", lambda e, qoo=qoo: e.tensor_add(qoo, t_m1, t_m2),
                           wait=(_e,), inc="v")
            it += 2
    t_qeo = [SBA(f"a_qeo{i}", [128, 4 * NQ], BF16) for i in range(2)]
    for g4 in range(4):
        for side, tsrc in enumerate((t_qe, t_qo)):
            S_.op("g", lambda e, g4=g4, side=side, tsrc=tsrc: e.dma_start(
                out=t_qeo[g4 // 2][(g4 % 2) * 64 + side * 32:(g4 % 2) * 64 + side * 32 + 32, :],
                in_=tsrc[g4 * 32:(g4 + 1) * 32, :]), wait=(ev_cmb,), inc="gdma", dma=True)
    S_.barrier(["t", "s", "v", "g"])
    al.release("a_xqT", "a_wqe", "a_wqo", "a_qe", "a_qo")

    if dbg == "proj":
        for nm, t in [("d_kn", t_kn), ("d_vtm", t_vtm), ("d_ke", t_keo[0]), ("d_ko", t_keo[1]),
                      ("d_qn", t_qn), ("d_qe", t_qeo[0]), ("d_qo", t_qeo[1])]:
            o = dbg_out(nm, list(t[:].shape))
            S_.op("g", lambda e, o=o, t=t: e.dma_start(out=o, in_=t[:]), inc="o", dma=True)
        S_.wait("g", S_.ev("o"))
        return _finish(nc, ctx, S_, ddbg)

    t_mask = SBA("m_mask", [128, 16 * 512], BF16)
    ev_mask = S_.op("s", lambda e: e.dma_start(out=t_mask[:], in_=din["mask"]),
                    inc="sdma", dma=True)
    S_.wait("v", ev_mask)
    t_pt = [SBA(f"b_pt{i}", [128, 512], BF16) for i in range(8)]
    t_ao = SBA("b_ao", [128, H * NQ], BF16)
    rot_sb = Rot(4)
    rot_pt = Rot(8)
    rot_ob = Rot(2)
    rot_lb = Rot(2)
    gs = 0
    itt = 0
    LAG = 3
    pending = []

    def flush_pending(upto):
        while pending and pending[0][0] <= upto:
            _, fn = pending.pop(0)
            fn()

    for h in range(H):
        kv = h // 4
        g = h // 4
        gq = h % 4
        for qb in range(2):
            nslots = 8 if qb == 0 else 16
            ob = 4 + itt % 2
            lb = 6 + itt % 2
            state = {"l": None, "av": None}
            for j in range(nslots):
                kt = j
                bank = gs % 4
                S_.op("t", lambda e, bank=bank, kv=kv, kt=kt, h=h, qb=qb: e.matmul(
                    pbank[bank][:],
                    t_kn[:, kv * 2048 + kt * 128: kv * 2048 + (kt + 1) * 128],
                    t_qn[:, h * NQ + qb * 512: h * NQ + (qb + 1) * 512],
                    start=True, stop=False), wait=rot_sb.wait(gs))
                ev_mm = S_.op("t", lambda e, bank=bank, g=g, kt=kt, gq=gq, qb=qb: e.matmul(
                    pbank[bank][:],
                    t_keo[g // 2][(g % 2) * 64:(g % 2) * 64 + 64, kt * 128:(kt + 1) * 128],
                    t_qeo[g // 2][(g % 2) * 64:(g % 2) * 64 + 64,
                                  gq * NQ + qb * 512: gq * NQ + (qb + 1) * 512],
                    start=False, stop=True, tile_position=((g % 2) * 64, 0)), inc="t")
                pt = t_pt[gs % 8]
                ev_exp = S_.op("s", lambda e, pt=pt, bank=bank: e.activation(
                    pt[:], pbank[bank][:], AF.Exp, scale=SCALE_ATT),
                    wait=(ev_mm,) + rot_pt.wait(gs), inc="s")
                rot_sb.set(gs, ev_exp)
                ev_dep = ev_exp
                if qb == 0 or j >= 8:
                    ev_dep = S_.op("v", lambda e, pt=pt, j=j: e.tensor_mul(
                        pt[:], pt[:], t_mask[:, j * 512:(j + 1) * 512]),
                        wait=(ev_exp,), inc="v")

                def emit_lav(j=j, pt=pt, ev_dep=ev_dep, lb=lb, ob=ob, kt=kt, kv=kv,
                             nslots=nslots, itt=itt, gs=gs, state=state):
                    w = (ev_dep,)
                    if j == 0:
                        w = w + rot_lb.wait(itt) + rot_ob.wait(itt)
                    state["l"] = S_.op("t", lambda e: e.matmul(
                        pbank[lb][0:1, :], t_ones_c[:], pt[:],
                        start=(j == 0), stop=(j == nslots - 1)),
                        wait=w, inc="t" if j == nslots - 1 else None)
                    ev_av = S_.op("t", lambda e: e.matmul(
                        pbank[ob][:], t_vtm[:, kt * 512 + kv * 128: kt * 512 + (kv + 1) * 128],
                        pt[:], start=(j == 0), stop=(j == nslots - 1)), inc="t")
                    rot_pt.set(gs, ev_av)
                    state["av"] = ev_av

                pending.append((gs, emit_lav))
                flush_pending(gs - LAG)
                gs += 1
            def emit_tail(lb=lb, ob=ob, h=h, qb=qb, itt=itt, state=state):
                ev_r = S_.op("v", lambda e: e.reciprocal(
                    t_row[itt % 2][:], pbank[lb][0:1, :]),
                    wait=(state["l"], state["av"]), inc="v")
                ev_rb = S_.op("t", lambda e: e.matmul(
                    pbank[lb][:], t_ones_r[:], t_row[itt % 2][:], start=True, stop=True),
                    wait=(ev_r,), inc="t")
                ev_rbs = S_.op("v", lambda e: e.tensor_copy(
                    t_rbs[itt % 2][:], pbank[lb][:]), wait=(ev_rb,), inc="v")
                ev_on = S_.op("v", lambda e: e.tensor_mul(
                    t_ao[:, h * NQ + qb * 512: h * NQ + (qb + 1) * 512],
                    pbank[ob][:], t_rbs[itt % 2][:]), wait=(ev_rbs,), inc="v")
                rot_ob.set(itt, ev_on)
                rot_lb.set(itt, ev_on)

            pending.append((gs - 1 + 4, emit_tail))
            itt += 1
    flush_pending(10 ** 9)
    S_.barrier(["t", "s", "v", "g"])
    al.release("a_kn", "a_vtm", "a_keo0", "a_keo1", "a_qn", "a_qeo0", "a_qeo1",
               "b_pt0", "b_pt1", "b_pt2", "b_pt3", "b_pt4", "b_pt5", "b_pt6", "b_pt7")

    if dbg == "attn":
        o = dbg_out("d_ao", [128, H * NQ])
        for _eng in ("v", "s", "t", "y"):
            if S_.last[_eng]:
                S_.wait("g", S_.last[_eng])
        S_.op("g", lambda e, o=o: e.dma_start(out=o, in_=t_ao[:]), inc="o", dma=True)
        S_.wait("g", S_.ev("o"))
        return _finish(nc, ctx, S_, ddbg)

    t_wosb2 = [SBA(f"c_wo{i}", [128, 8 * 2048], BF16) for i in range(2)]
    t_ywin = [SBA(f"c_yw{i}", [128, 2048], F32) for i in range(2)]
    t_xqst = [SBA(f"c_xq{i}", [128, 2048], F32) for i in range(2)]
    ev_wo = None
    for i in range(2):
        ev_wo = S_.op("g", lambda e, i=i: e.dma_start(
            out=t_wosb2[i][:].rearrange("p (h f) -> p h f", h=8),
            in_=din["wo"][i * 8 * 128:(i + 1) * 8 * 128, :].rearrange(
                "(h p) f -> p h f", p=128)), inc="gdma", dma=True)

    rot_y = Rot(2)
    rot_xs = Rot(2)
    for qt in range(8):
        ev_x = S_.op("s", lambda e, qt=qt: e.dma_start(
            out=t_xqst[qt % 2][:], in_=din["xq"][qt * 128:(qt + 1) * 128, :]),
            wait=rot_xs.wait(qt), inc="sdma", dma=True)
        bank0 = 4 * (qt % 2)
        ev_fs = [None] * 4
        for hh in range(16):
            for fs in range(4):
                w = ()
                if hh == 0:
                    w = rot_y.wait(qt)
                    if qt == 0 and fs == 0:
                        w = w + (ev_wo,)
                ev_fs[fs] = S_.op("t", lambda e, bank0=bank0, fs=fs, hh=hh, qt=qt: e.matmul(
                    pbank[bank0 + fs][:],
                    t_ao[:, hh * NQ + qt * 128: hh * NQ + qt * 128 + 128],
                    t_wosb2[hh // 8][:, (hh % 8) * 2048 + fs * 512:
                                     (hh % 8) * 2048 + (fs + 1) * 512],
                    start=(hh == 0), stop=(hh == 15)),
                    wait=w, inc="t" if hh == 15 else None)
        yw = t_ywin[qt % 2]
        ev_z = None
        for fs in range(4):
            ev_z = S_.op("v", lambda e, bank0=bank0, fs=fs, yw=yw, qt=qt: e.tensor_add(
                yw[:, fs * 512:(fs + 1) * 512], pbank[bank0 + fs][:],
                t_xqst[qt % 2][:, fs * 512:(fs + 1) * 512]),
                wait=(ev_fs[fs], ev_x) if fs == 0 else (ev_fs[fs],),
                inc="v" if fs == 3 else None)
        rot_xs.set(qt, ev_z)
        ev_ln = ln_inplace(yw, ev_z)
        ev_sp = S_.op("y", lambda e, yw=yw, qt=qt: e.dma_start(
            out=d_yln[qt * 128:(qt + 1) * 128, :], in_=yw[:]), wait=(ev_ln,), inc="ydma", dma=True)
        rot_y.set(qt, ev_sp)
    S_.barrier(["t", "s", "v", "g", "y"])
    al.release("c_wo0", "c_wo1", "c_xq0", "c_xq1", "b_ao", "c_yw0", "c_yw1")

    if dbg == "wo":
        o = dbg_out("d_yln", [NQ, D])
        S_.op("g", lambda e, o=o: e.dma_start(out=o, in_=d_yln), inc="o", dma=True)
        S_.wait("g", S_.ev("o"))
        return _finish(nc, ctx, S_, ddbg)

    al.release("m_mask", "m_rbs0", "m_rbs1", "m_row0", "m_row1", "m_cosq", "m_sinq")
    t_yfm = SBA("f_yfm", [128, 16 * 512], F32)
    t_B2f = SBA("f_B2f", [128, 16 * 512], F32)
    t_xq1 = SBA("f_xq1", [128, 16 * NQ], BF16)
    t_yst = [SBA(f"f_yst{i}", [128, 2048], F32) for i in range(2)]
    rot_ys = Rot(2)
    rot_bank4b = Rot(4)
    for ch in range(2):
        last_e = None
        for rt in range(4):
            qt = ch * 4 + rt
            ys = t_yst[rt % 2]
            ev_ld = S_.op("y", lambda e, ys=ys, qt=qt: e.dma_start(
                out=ys[:], in_=d_yln[qt * 128:(qt + 1) * 128, :]),
                wait=rot_ys.wait(rt), inc="ydma", dma=True)
            for cg in range(4):
                bank = 2 * (rt % 2) + (cg % 2)
                ev_t = None
                for ci in range(4):
                    c = cg * 4 + ci
                    w2 = ((ev_ld,) if (cg == 0 and ci == 0) else ())
                    if ci == 0:
                        w2 = w2 + rot_bank4b.wait(bank)
                    ev_t = S_.op("t", lambda e, ys=ys, c=c, bank=bank, ci=ci: e.transpose(
                        pbank[bank][:, ci * 128:(ci + 1) * 128],
                        ys[:, c * 128:(c + 1) * 128], t_idf[:]),
                        wait=w2, inc="t" if ci == 3 else None)
                dst = t_yfm[:].rearrange("p (c t) -> p c t", c=16)[
                    :, cg * 4:(cg + 1) * 4, rt * 128:(rt + 1) * 128]
                last_e = S_.op("v", lambda e, dst=dst, bank=bank: e.tensor_copy(
                    dst, pbank[bank][:]), wait=(ev_t,), inc="v")
                rot_bank4b.set(bank, last_e)
            rot_ys.set(rt, last_e)
        S_.barrier(["t", "v", "y", "s"])
        if dbg == "yfm" and ch == 0:
            o = dbg_out("d_yfm", [128, 16 * 512])
            S_.wait("g", S_.last["v"])
            S_.op("g", lambda e, o=o: e.dma_start(out=o, in_=t_yfm[:]), inc="o", dma=True)
            S_.wait("g", S_.ev("o"))
            return _finish(nc, ctx, S_, ddbg)
        rot_f = Rot(2)
        for c in range(16):
            bank = c % 2
            ev_mm = S_.op("t", lambda e, c=c, bank=bank: e.matmul(
                pbank[bank][:], t_h128f[:], t_yfm[:, c * 512:(c + 1) * 512],
                start=True, stop=True), wait=rot_f.wait(c), inc="t")
            ev_e = S_.op("s", lambda e, c=c, bank=bank: e.copy(
                t_yfm[:, c * 512:(c + 1) * 512], pbank[bank][:]), wait=(ev_mm,), inc="s")
            rot_f.set(c, ev_e)
        S_.barrier(["t", "s", "v"])
        if dbg == "yfA" and ch == 0:
            o = dbg_out("d_yfm", [128, 16 * 512])
            S_.wait("g", S_.last["s"])
            S_.op("g", lambda e, o=o: e.dma_start(out=o, in_=t_yfm[:]), inc="o", dma=True)
            S_.wait("g", S_.ev("o"))
            return _finish(nc, ctx, S_, ddbg)
        bufs = [t_yfm, t_B2f]
        _e = None
        for si, stt in enumerate([8, 4, 2, 1]):
            src_v = bufs[si % 2][:].rearrange("p (g two i t) -> p g two i t", two=2, i=stt, t=512)
            dst_v = bufs[(si + 1) % 2][:].rearrange("p (g two i t) -> p g two i t", two=2, i=stt, t=512)
            S_.op("v", lambda e, s=src_v, d=dst_v: e.tensor_tensor(
                d[:, :, 0], s[:, :, 0], s[:, :, 1], ALU.add), wait=(_e,) if _e else ())
            _e = S_.op("v", lambda e, s=src_v, d=dst_v: e.tensor_tensor(
                d[:, :, 1], s[:, :, 0], s[:, :, 1], ALU.subtract), inc="v")
        if dbg == "xf1" and ch == 0:
            o = dbg_out("d_yfm", [128, 16 * 512])
            S_.wait("g", S_.last["v"])
            S_.op("g", lambda e, o=o: e.dma_start(out=o, in_=t_yfm[:]), inc="o", dma=True)
            S_.wait("g", S_.ev("o"))
            return _finish(nc, ctx, S_, ddbg)
        _e = S_.op("v", lambda e: e.tensor_tensor(t_st[:, 0:512], t_yfm[:, 0:512],
                                                  t_yfm[:, 512:1024], ALU.max), inc="v")
        for c in range(2, 16):
            _e = S_.op("v", lambda e, c=c: e.tensor_tensor(
                t_st[:, 0:512], t_st[:, 0:512], t_yfm[:, c * 512:(c + 1) * 512], ALU.max),
                wait=(_e,), inc="v")
        _e = S_.op("v", lambda e: e.tensor_tensor(t_st[:, 512:1024], t_yfm[:, 0:512],
                                                  t_yfm[:, 512:1024], ALU.min), wait=(_e,), inc="v")
        for c in range(2, 16):
            _e = S_.op("v", lambda e, c=c: e.tensor_tensor(
                t_st[:, 512:1024], t_st[:, 512:1024], t_yfm[:, c * 512:(c + 1) * 512], ALU.min),
                wait=(_e,), inc="v")
        _e = S_.op("v", lambda e: e.tensor_scalar(t_st[:, 512:1024], t_st[:, 512:1024],
                                                  -1.0, None, ALU.mult), wait=(_e,), inc="v")
        ev_tree = S_.op("v", lambda e: e.tensor_tensor(
            t_st[:, 0:512], t_st[:, 0:512], t_st[:, 512:1024], ALU.max), wait=(_e,), inc="v")
        ev_tp = None
        for jcol in range(4):
            ev_tp = S_.op("t", lambda e, jcol=jcol: e.transpose(
                pbank[6][:, jcol * 128:(jcol + 1) * 128],
                t_st[:, jcol * 128:(jcol + 1) * 128], t_idf[:]),
                wait=(ev_tree,) if jcol == 0 else (), inc="t" if jcol == 3 else None)
        ev_q1m = None
        for jcol in range(4):
            ev_q1m = S_.op("v", lambda e, jcol=jcol: e.reduce_max(
                pbank[5][:, jcol: jcol + 1],
                pbank[6][:, jcol * 128:(jcol + 1) * 128], axis=AX.X),
                wait=(ev_tp,) if jcol == 0 else (), inc="v" if jcol == 3 else None)
        ev_q1m = vg()
        ev_q1m = S_.op("s", lambda e, ch=ch: e.copy(
            t_q1m[:, ch * 4:(ch + 1) * 4], pbank[5][:, 0:4]), wait=(ev_q1m,), inc="s")
        if dbg == "q1r" and ch == 0:
            o = dbg_out("d_q1m", [128, 16])
            o2 = dbg_out("d_pb6", [128, 512])
            ev_c6 = S_.op("v", lambda e: e.tensor_copy(t_B[:, 0:512], pbank[6][:]),
                          wait=(ev_q1m,), inc="v")
            S_.wait("g", ev_c6)
            S_.op("g", lambda e, o=o: e.dma_start(out=o, in_=t_q1m[:]), inc="o", dma=True)
            S_.op("g", lambda e, o2=o2: e.dma_start(out=o2, in_=t_B[:, 0:512]), inc="o", dma=True)
            S_.wait("g", S_.ev("o"))
            return _finish(nc, ctx, S_, ddbg)
        ev = tvop(lambda e, ch=ch: e.tensor_scalar(
            pbank[5][:, 8:12], t_q1m[:, ch * 4:(ch + 1) * 4], EPS_Q / C1, None, ALU.max),
            pbank[5][:, 8:12], t_q1m[:, ch * 4:(ch + 1) * 4], wait=(ev_q1m,))
        ev = tvop(lambda e, ch=ch: e.reciprocal(pbank[5][:, 12:16], t_q1m[:, ch * 4:(ch + 1) * 4]),
                  pbank[5][:, 12:16], t_st2[:, 0:4], wait=(ev,))
        ev_s1 = tvop(lambda e: e.tensor_scalar(pbank[5][:, 8:12], t_st2[:, 0:4],
                                               QBITS, None, ALU.mult),
                     pbank[5][:, 8:12], t_st2[:, 0:4], wait=(ev,))
        ev_tp2 = S_.op("t", lambda e: e.transpose(
            pbank[7][0:4, 0:128], t_st2[:, 0:4], t_idf[:]), wait=(ev_s1,), inc="t")
        ev_rT = S_.op("v", lambda e: e.tensor_copy(t_rowT[0:4, :], pbank[7][0:4, 0:128]),
                      wait=(ev_tp2,), inc="v")
        ev_b1 = None
        for j in range(4):
            ev_b1 = S_.op("t", lambda e, j=j: e.matmul(
                pbank[4][:, j * 128:(j + 1) * 128], t_sel8[0:4, j * 128:(j + 1) * 128],
                t_rowT[0:4, :], start=True, stop=True),
                wait=(ev_rT,) if j == 0 else (), inc="t" if j == 3 else None)
        ev_B1 = S_.op("v", lambda e: e.tensor_copy(t_B[:, 0:512], pbank[4][:]),
                      wait=(ev_b1,), inc="v")
        if dbg == "q1s" and ch == 0:
            o = dbg_out("d_B", [128, 1024])
            o2 = dbg_out("d_q1m", [128, 16])
            o3 = dbg_out("d_s1", [128, 128])
            o4 = dbg_out("d_pmax", [128, 1024])
            S_.wait("g", ev_B1)
            S_.op("g", lambda e, o=o: e.dma_start(out=o, in_=t_B[:]), inc="o", dma=True)
            S_.op("g", lambda e, o2=o2: e.dma_start(out=o2, in_=t_q1m[:]), inc="o", dma=True)
            S_.op("g", lambda e, o3=o3: e.dma_start(out=o3, in_=t_st2[:]), inc="o", dma=True)
            S_.op("g", lambda e, o4=o4: e.dma_start(out=o4, in_=t_st[:]), inc="o", dma=True)
            S_.wait("g", S_.ev("o"))
            return _finish(nc, ctx, S_, ddbg)
        ev_x1 = ev_B1
        for c in range(16):
            ev_x1 = S_.op("v", lambda e, c=c: e.tensor_tensor(
                t_st[:, 0:512], t_yfm[:, c * 512:(c + 1) * 512], t_B[:, 0:512], ALU.mult),
                wait=(ev_x1,), inc="v")
            ev_x1 = S_.op("v", lambda e, c=c, ch=ch: e.tensor_scalar(
                t_xq1[:, c * NQ + ch * 512: c * NQ + (ch + 1) * 512],
                t_st[:, 0:512], MROUND, MROUND, ALU.add, ALU.subtract),
                wait=(ev_x1,), inc="v")
        S_.barrier(["t", "s", "v", "y"])
    al.release("f_yfm", "f_B2f")

    if dbg == "xq1":
        o = dbg_out("d_xq1", [128, 16 * NQ])
        o2 = dbg_out("d_q1m", [128, 16])
        for _eng in ("v", "s", "t", "y"):
            if S_.last[_eng]:
                S_.wait("g", S_.last[_eng])
        S_.op("g", lambda e, o=o: e.dma_start(out=o, in_=t_xq1[:]), inc="o", dma=True)
        S_.op("g", lambda e, o2=o2: e.dma_start(out=o2, in_=t_q1m[:]), inc="o", dma=True)
        S_.wait("g", S_.ev("o"))
        return _finish(nc, ctx, S_, ddbg)

    t_wstg = [SBA(f"w_stg{i}", [128, F2], BF16) for i in range(2)]
    t_wabs = [SBA(f"w_abs{i}", [128, F2], BF16) for i in range(2)]

    def weight_mean(dname, nrow_t, ncols, dst_w, dst_r):
        rot_s = Rot(2)
        rot_a = Rot(2)
        nmm = ncols // 512
        ev_mm = None
        for r in range(nrow_t):
            stg = t_wstg[r % 2]
            ev_ld = S_.op("g", lambda e, stg=stg, r=r: e.dma_start(
                out=stg[:, 0:ncols], in_=din[dname][r * 128:(r + 1) * 128, :]),
                wait=rot_s.wait(r), inc="gdma", dma=True)
            wab = t_wabs[r % 2]
            ev_ab = S_.op("s", lambda e, stg=stg, ncols=ncols, wab=wab: e.activation(
                wab[:, 0:ncols], stg[:, 0:ncols], AF.Abs),
                wait=(ev_ld,) + rot_a.wait(r), inc="s")
            rot_s.set(r, ev_ab)
            for n in range(nmm):
                ev_mm = S_.op("t", lambda e, n=n, r=r, nrow_t=nrow_t, nmm=nmm, wab=wab: e.matmul(
                    pbank[2][0:1, :], t_ones_c[:], wab[:, n * 512:(n + 1) * 512],
                    start=(r == 0 and n == 0), stop=(r == nrow_t - 1 and n == nmm - 1)),
                    wait=(ev_ab,) if n == 0 else (), inc="t" if n == nmm - 1 else None)
            rot_a.set(r, ev_mm)
        ev = tvop(lambda e: e.reduce_sum(pbank[5][0:1, 40:41], pbank[2][0:1, :], axis=AX.X),
                  pbank[5][0:1, 40:41], t_wsc[:, 0:1], wait=(ev_mm,))
        ev = tvop(lambda e, nrow_t=nrow_t, ncols=ncols: e.tensor_scalar(
            pbank[5][0:1, 41:42], t_wsc[:, 0:1], 1.0 / (nrow_t * 128 * ncols), None, ALU.mult),
            pbank[5][0:1, 41:42], t_wsc[:, 1:2], wait=(ev,))
        ev = tvop(lambda e: e.tensor_scalar(pbank[5][0:1, 42:43], t_wsc[:, 1:2], EPS_Q, None, ALU.max),
                  pbank[5][0:1, 42:43], t_wsc[:, 2:3], wait=(ev,))
        ev_r = tvop(lambda e: e.reciprocal(pbank[5][0:1, 43:44], t_wsc[:, 2:3]),
                    pbank[5][0:1, 43:44], t_wsc[:, 3:4], wait=(ev,))
        ev_m1 = S_.op("t", lambda e: e.matmul(pbank[2][:, 0:1], t_ones_r[:], t_wsc[:, 2:3],
                                              start=True, stop=True), wait=(ev_r,), inc="t")
        ev_c1 = S_.op("s", lambda e, dst_w=dst_w: e.copy(dst_w, pbank[2][:, 0:1]),
                      wait=(ev_m1,), inc="s")
        ev_m2 = S_.op("t", lambda e: e.matmul(pbank[2][:, 1:2], t_ones_r[:], t_wsc[:, 3:4],
                                              start=True, stop=True), wait=(ev_c1,), inc="t")
        return S_.op("v", lambda e, dst_r=dst_r: e.tensor_copy(dst_r, pbank[2][:, 1:2]),
                     wait=(ev_m2,), inc="v")

    S_.barrier(["t", "s", "v", "g", "y"])
    _ew = weight_mean("wupt", 16, F2, t_wupb[:], t_rupb[:])
    ev = tvop(lambda e: e.tensor_scalar(pbank[5][:, 16:24], t_q1m[:, 0:8], t_wupb[:, 0:1],
                                        C1 / QBITS, ALU.mult, ALU.mult),
              pbank[5][:, 16:24], t_st2[:, 16:24], wait=(_ew,))
    tvop(lambda e: e.tensor_tensor(pbank[5][:, 24:32], t_st2[:, 16:24], t_st2[:, 16:24],
                                   ALU.mult),
         pbank[5][:, 24:32], t_d2[:, 0:8], wait=(ev,))
    S_.barrier(["t", "s", "v", "g"])
    al.release("w_stg0", "w_stg1", "w_abs0", "w_abs1")

    t_wqsl = [SBA(f"u_wq{i}", [128, 16 * 512], BF16) for i in range(2)]
    t_fstg = [SBA(f"u_fs{i}", [128, 512], F32) for i in range(2)]
    t_qstg = SBA("u_qs", [128, 512], F32)
    t_h = SBA("u_h", [128, 32 * 1024], BF16)
    t_rl2 = SBA("u_rl2", [128, 512], BF16)
    t_rl3 = SBA("u_rl3", [128, 512], BF16)
    t_rl4 = SBA("u_rl4", [128, 512], BF16)
    t_rls = [t_rl, t_rl2, t_rl3, t_rl4]
    rot_wsl = Rot(2)
    rot_fst = Rot(2)
    rot_qst = Rot(1)
    rot_hb = Rot(4)
    rot_rlb = Rot(4)
    for slab in range(8):
        wsl = t_wqsl[slab % 2]
        ev_q3 = None
        for dc in range(16):
            ev_ld = S_.op("y", lambda e, dc=dc, slab=slab: e.dma_start(
                out=t_fstg[dc % 2][:], in_=din["wupt"][dc * 128:(dc + 1) * 128,
                                                       slab * 512:(slab + 1) * 512]),
                wait=rot_fst.wait(dc) + (rot_wsl.wait(slab) if dc == 0 else ()),
                inc="ydma", dma=True)
            ev_q1 = S_.op("s", lambda e, dc=dc: e.activation(
                t_qstg[:], t_fstg[dc % 2][:], AF.Copy, scale=t_rupb[:]),
                wait=(ev_ld,) + rot_qst.wait(0), inc="s")
            rot_fst.set(dc, ev_q1)
            _e = S_.op("v", lambda e: e.tensor_scalar(t_qstg[:], t_qstg[:], MROUND, MROUND,
                                                      ALU.add, ALU.subtract), wait=(ev_q1,), inc="v")
            ev_q3 = S_.op("v", lambda e, wsl=wsl, dc=dc: e.tensor_scalar(
                wsl[:, dc * 512:(dc + 1) * 512], t_qstg[:], 1.0, -1.0, ALU.min, ALU.max),
                wait=(_e,), inc="v")
            rot_qst.set(0, ev_q3)
        ev_last_mm = None
        for f2c in range(4):
            for tc in range(2):
                gi = slab * 8 + f2c * 2 + tc
                bank = gi % 4
                ev_mm = None
                for dc in range(16):
                    w = (rot_hb.wait(gi) if dc == 0 else ())
                    if dc == 0 and f2c == 0 and tc == 0:
                        w = w + (ev_q3,)
                    ev_mm = S_.op("t", lambda e, bank=bank, dc=dc, f2c=f2c, tc=tc, wsl=wsl: e.matmul(
                        pbank[bank][:],
                        wsl[:, dc * 512 + f2c * 128: dc * 512 + (f2c + 1) * 128],
                        t_xq1[:, dc * NQ + tc * 512: dc * NQ + (tc + 1) * 512],
                        start=(dc == 0), stop=(dc == 15)),
                        wait=w, inc="t" if dc == 15 else None)
                ev_last_mm = ev_mm
                fcg = slab * 4 + f2c
                rl = t_rls[gi % 4]
                ev_rl = S_.op("s", lambda e, bank=bank, rl=rl: e.activation(
                    rl[:], pbank[bank][:], AF.Relu), wait=(ev_mm,) + rot_rlb.wait(gi), inc="s")
                ev_h = S_.op("v", lambda e, fcg=fcg, tc=tc, rl=rl: e.tensor_tensor(
                    t_h[:, fcg * 1024 + tc * 512: fcg * 1024 + (tc + 1) * 512],
                    rl[:], rl[:], ALU.mult), wait=(ev_rl,), inc="v")
                rot_hb.set(gi, ev_h)
                rot_rlb.set(gi, ev_h)
        rot_wsl.set(slab, ev_last_mm)
    S_.barrier(["t", "s", "v", "y", "g"])
    al.release("f_xq1", "u_wq0", "u_wq1", "u_fs0", "u_fs1", "u_qs", "u_rl2",
               "u_rl3", "u_rl4")

    if dbg == "h":
        o = dbg_out("d_h", [128, 32 * 1024])
        for _eng in ("v", "s", "t", "y"):
            if S_.last[_eng]:
                S_.wait("g", S_.last[_eng])
        o3 = dbg_out("d_wsc", [128, 1])
        o4 = dbg_out("d_wscrow", [1, 8])
        o5 = dbg_out("d_colsum", [1, 512])
        S_.op("g", lambda e, o5=o5: e.dma_start(out=o5, in_=t_B[0:1, 0:512]), inc="o", dma=True)
        S_.op("g", lambda e, o=o: e.dma_start(out=o, in_=t_h[:]), inc="o", dma=True)
        S_.op("g", lambda e, o3=o3: e.dma_start(out=o3, in_=t_rupb[:]), inc="o", dma=True)
        S_.op("g", lambda e, o4=o4: e.dma_start(out=o4, in_=t_wsc[:]), inc="o", dma=True)
        S_.wait("g", S_.ev("o"))
        return _finish(nc, ctx, S_, ddbg)

    t_pong = SBA("x_pong", [128, 32 * 1024], BF16)
    rot_f2 = Rot(2)
    for fc in range(32):
        for half in range(2):
            i6 = fc * 2 + half
            bank = i6 % 2
            ev_mm = S_.op("t", lambda e, fc=fc, half=half, bank=bank: e.matmul(
                pbank[bank][:], t_h128bf[:],
                t_h[:, fc * 1024 + half * 512: fc * 1024 + (half + 1) * 512],
                start=True, stop=True), wait=rot_f2.wait(i6), inc="t")
            ev_e = S_.op("s", lambda e, fc=fc, half=half, bank=bank: e.copy(
                t_h[:, fc * 1024 + half * 512: fc * 1024 + (half + 1) * 512], pbank[bank][:]),
                wait=(ev_mm,), inc="s")
            rot_f2.set(i6, ev_e)
    S_.barrier(["t", "s", "v"])
    bufs2 = [t_h, t_pong]
    _e = None
    for si, stt in enumerate([16, 8, 4, 2, 1]):
        src_v = bufs2[si % 2][:].rearrange("p (g two i t) -> p g two i t", two=2, i=stt, t=1024)
        dst_v = bufs2[(si + 1) % 2][:].rearrange("p (g two i t) -> p g two i t", two=2, i=stt, t=1024)
        S_.op("v", lambda e, s=src_v, d=dst_v: e.tensor_tensor(
            d[:, :, 0], s[:, :, 0], s[:, :, 1], ALU.add), wait=(_e,) if _e else ())
        _e = S_.op("v", lambda e, s=src_v, d=dst_v: e.tensor_tensor(
            d[:, :, 1], s[:, :, 0], s[:, :, 1], ALU.subtract), inc="v")
    t_F = t_pong
    t_xq2 = t_h
    t_trbf2 = t_st[:, 0:512].bitcast(BF16)
    _e = S_.op("v", lambda e: e.tensor_tensor(t_trbf[:], t_F[:, 0:1024], t_F[:, 1024:2048], ALU.max),
               inc="v")
    for c in range(2, 32):
        _e = S_.op("v", lambda e, c=c: e.tensor_tensor(
            t_trbf[:], t_trbf[:], t_F[:, c * 1024:(c + 1) * 1024], ALU.max), wait=(_e,), inc="v")
    _e = S_.op("v", lambda e: e.tensor_tensor(t_trbf2, t_F[:, 0:1024], t_F[:, 1024:2048], ALU.min),
               wait=(_e,), inc="v")
    for c in range(2, 32):
        _e = S_.op("v", lambda e, c=c: e.tensor_tensor(
            t_trbf2, t_trbf2, t_F[:, c * 1024:(c + 1) * 1024], ALU.min), wait=(_e,), inc="v")
    _e = S_.op("v", lambda e: e.tensor_scalar(t_trbf2, t_trbf2, -1.0, None, ALU.mult),
               wait=(_e,), inc="v")
    ev_tr2 = S_.op("v", lambda e: e.tensor_tensor(t_trbf[:], t_trbf[:], t_trbf2, ALU.max),
                   wait=(_e,), inc="v")
    ev_tp = None
    for j in range(8):
        ev_tp = S_.op("t", lambda e, j=j: e.transpose(
            pbank[6][:].bitcast(BF16)[:, j * 128:(j + 1) * 128],
            t_trbf[:, j * 128:(j + 1) * 128], t_idbf[:]),
            wait=(ev_tr2,) if j == 0 else (), inc="t" if j == 7 else None)
    ev_q2m = None
    for j in range(8):
        ev_q2m = S_.op("v", lambda e, j=j: e.reduce_max(
            pbank[5][:, j:j + 1], pbank[6][:].bitcast(BF16)[:, j * 128:(j + 1) * 128], axis=AX.X),
            wait=(ev_tp,) if j == 0 else (), inc="v" if j == 7 else None)
    ev_q2m = vg()
    ev_q2m = S_.op("s", lambda e: e.copy(t_q2m[:, 0:8], pbank[5][:, 0:8]),
                   wait=(ev_q2m,), inc="s")
    ev = tvop(lambda e: e.tensor_tensor(pbank[5][:, 8:16], t_q2m[:, 0:8], t_d2[:, 0:8], ALU.mult),
              pbank[5][:, 8:16], t_st2[:, 24:32], wait=(ev_q2m,))
    ev = tvop(lambda e: e.tensor_scalar(pbank[5][:, 16:24], t_st2[:, 24:32], C2, EPS_Q,
                                        ALU.mult, ALU.max),
              pbank[5][:, 16:24], t_q2m[:, 0:8], wait=(ev,))
    ev = tvop(lambda e: e.reciprocal(pbank[5][:, 8:16], t_q2m[:, 0:8]),
              pbank[5][:, 8:16], t_st2[:, 24:32], wait=(ev,))
    ev = tvop(lambda e: e.tensor_tensor(pbank[5][:, 16:24], t_st2[:, 24:32], t_d2[:, 0:8],
                                        ALU.mult),
              pbank[5][:, 16:24], t_st2[:, 24:32], wait=(ev,))
    ev_s2 = tvop(lambda e: e.tensor_scalar(pbank[5][:, 8:16], t_st2[:, 24:32],
                                           C2 * QBITS, None, ALU.mult),
                 pbank[5][:, 8:16], t_st2[:, 8:16], wait=(ev,))
    if dbg == "z2":
        o8 = dbg_out("d_q2m_postE", [128, 16])
        S_.wait("g", ev_s2)
        S_.op("g", lambda e, o8=o8: e.dma_start(out=o8, in_=t_q2m[:]), inc="gdma", dma=True)
    ev_tp2 = S_.op("t", lambda e: e.transpose(
        pbank[7][0:8, 0:128], t_st2[:, 8:16], t_idf[:]), wait=(ev_s2,), inc="t")
    ev_rT = S_.op("v", lambda e: e.tensor_copy(t_rowT[:], pbank[7][0:8, 0:128]),
                  wait=(ev_tp2,), inc="v")
    ev_b2 = [None, None]
    for half in range(2):
        for j in range(4):
            jj = half * 4 + j
            ev_b2[half] = S_.op("t", lambda e, jj=jj, half=half, j=j: e.matmul(
                pbank[6 + half][:, j * 128:(j + 1) * 128],
                t_sel8[:, jj * 128:(jj + 1) * 128], t_rowT[:],
                start=True, stop=True),
                wait=(ev_rT,) if jj == 0 else (), inc="t" if j == 3 else None)
    ev_B2 = None
    for half in range(2):
        ev_B2 = S_.op("v", lambda e, half=half: e.tensor_copy(
            t_B[:, half * 512:(half + 1) * 512], pbank[6 + half][:]),
            wait=(ev_b2[half],), inc="v" if half == 1 else None)
    ev_x2 = ev_B2
    for fc in range(32):
        ev_x2 = S_.op("v", lambda e, fc=fc: e.tensor_tensor(
            t_st[:, 0:1024], t_F[:, fc * 1024:(fc + 1) * 1024], t_B[:, 0:1024], ALU.mult),
            wait=(ev_x2,), inc="v")
        ev_x2 = S_.op("v", lambda e, fc=fc: e.tensor_scalar(
            t_xq2[:, fc * 1024:(fc + 1) * 1024], t_st[:, 0:1024], MROUND, MROUND,
            ALU.add, ALU.subtract), wait=(ev_x2,), inc="v")
    S_.barrier(["t", "s", "v", "g", "y"])
    al.release("x_pong")

    if dbg == "xq2":
        o = dbg_out("d_xq2", [128, 32 * 1024])
        o2 = dbg_out("d_q2m", [128, 16])
        for _eng in ("v", "s", "t", "y"):
            if S_.last[_eng]:
                S_.wait("g", S_.last[_eng])
        S_.op("g", lambda e, o=o: e.dma_start(out=o, in_=t_xq2[:]), inc="o", dma=True)
        S_.op("g", lambda e, o2=o2: e.dma_start(out=o2, in_=t_q2m[:]), inc="o", dma=True)
        S_.wait("g", S_.ev("o"))
        return _finish(nc, ctx, S_, ddbg)

    if dbg == "z2":
        o7 = dbg_out("d_q2m_pre", [128, 16])
        S_.op("y", lambda e, o7=o7: e.dma_start(out=o7, in_=t_q2m[:]), inc="ydma", dma=True)

    t_wstg = [SBA(f"w2_stg{i}", [128, F2], BF16) for i in range(2)]
    t_wabs = [SBA(f"w2_abs{i}", [128, F2], BF16) for i in range(2)]

    def weight_mean2(dname, nrow_t, ncols, dst_w, dst_r):
        rot_s = Rot(2)
        rot_a = Rot(2)
        nmm = ncols // 512
        ev_mm = None
        for r in range(nrow_t):
            stg = t_wstg[r % 2]
            ev_ld = S_.op("g", lambda e, stg=stg, r=r: e.dma_start(
                out=stg[:, 0:ncols], in_=din[dname][r * 128:(r + 1) * 128, :]),
                wait=rot_s.wait(r), inc="gdma", dma=True)
            wab = t_wabs[r % 2]
            ev_ab = S_.op("s", lambda e, stg=stg, ncols=ncols, wab=wab: e.activation(
                wab[:, 0:ncols], stg[:, 0:ncols], AF.Abs),
                wait=(ev_ld,) + rot_a.wait(r), inc="s")
            rot_s.set(r, ev_ab)
            for n in range(nmm):
                ev_mm = S_.op("t", lambda e, n=n, r=r, nrow_t=nrow_t, nmm=nmm, wab=wab: e.matmul(
                    pbank[2][0:1, :], t_ones_c[:], wab[:, n * 512:(n + 1) * 512],
                    start=(r == 0 and n == 0), stop=(r == nrow_t - 1 and n == nmm - 1)),
                    wait=(ev_ab,) if n == 0 else (), inc="t" if n == nmm - 1 else None)
            rot_a.set(r, ev_mm)
        ev = tvop(lambda e: e.reduce_sum(pbank[5][0:1, 40:41], pbank[2][0:1, :], axis=AX.X),
                  pbank[5][0:1, 40:41], t_wsc[:, 0:1], wait=(ev_mm,))
        ev = tvop(lambda e, nrow_t=nrow_t, ncols=ncols: e.tensor_scalar(
            pbank[5][0:1, 41:42], t_wsc[:, 0:1], 1.0 / (nrow_t * 128 * ncols), None, ALU.mult),
            pbank[5][0:1, 41:42], t_wsc[:, 1:2], wait=(ev,))
        ev = tvop(lambda e: e.tensor_scalar(pbank[5][0:1, 42:43], t_wsc[:, 1:2], EPS_Q, None, ALU.max),
                  pbank[5][0:1, 42:43], t_wsc[:, 2:3], wait=(ev,))
        ev_r = tvop(lambda e: e.reciprocal(pbank[5][0:1, 43:44], t_wsc[:, 2:3]),
                    pbank[5][0:1, 43:44], t_wsc[:, 3:4], wait=(ev,))
        ev_m1 = S_.op("t", lambda e: e.matmul(pbank[2][:, 0:1], t_ones_r[:], t_wsc[:, 2:3],
                                              start=True, stop=True), wait=(ev_r,), inc="t")
        ev_c1 = S_.op("s", lambda e, dst_w=dst_w: e.copy(dst_w, pbank[2][:, 0:1]),
                      wait=(ev_m1,), inc="s")
        ev_m2 = S_.op("t", lambda e: e.matmul(pbank[2][:, 1:2], t_ones_r[:], t_wsc[:, 3:4],
                                              start=True, stop=True), wait=(ev_c1,), inc="t")
        return S_.op("v", lambda e, dst_r=dst_r: e.tensor_copy(dst_r, pbank[2][:, 1:2]),
                     wait=(ev_m2,), inc="v")

    _ew = weight_mean2("wdnt", 32, 2048, t_wdnb[:], t_rdnb[:])
    tvop(lambda e: e.tensor_scalar(pbank[5][:, 24:32], t_q2m[:, 0:8], t_wdnb[:, 0:1],
                                   1.0 / QBITS, ALU.mult, ALU.mult),
         pbank[5][:, 24:32], t_dq2[:, 0:8], wait=(_ew,))
    S_.barrier(["t", "s", "v", "g"])
    al.release("w2_stg0", "w2_stg1", "w2_abs0", "w2_abs1")

    t_wdsl = [SBA(f"d_wd{i}", [128, 32 * 512], BF16) for i in range(2)]
    t_fstg2 = [SBA(f"d_fs{i}", [128, 512], F32) for i in range(2)]
    t_qstg2 = SBA("d_qs", [128, 512], F32)
    t_ystg = [SBA(f"d_ys{i}", [128, 512], F32) for i in range(2)]
    rot_wsl2 = Rot(2)
    rot_fst2 = Rot(2)
    rot_qst2 = Rot(1)
    rot_yst2 = Rot(2)
    rot_db = Rot(4)
    for fsl in range(4):
        wsl = t_wdsl[fsl % 2]
        ev_q3 = None
        for fc in range(32):
            ev_ld = S_.op("y", lambda e, fc=fc, fsl=fsl: e.dma_start(
                out=t_fstg2[fc % 2][:], in_=din["wdnt"][fc * 128:(fc + 1) * 128,
                                                        fsl * 512:(fsl + 1) * 512]),
                wait=rot_fst2.wait(fc) + (rot_wsl2.wait(fsl) if fc == 0 else ()),
                inc="ydma", dma=True)
            ev_q1 = S_.op("s", lambda e, fc=fc: e.activation(
                t_qstg2[:], t_fstg2[fc % 2][:], AF.Copy, scale=t_rdnb[:]),
                wait=(ev_ld,) + rot_qst2.wait(0), inc="s")
            rot_fst2.set(fc, ev_q1)
            _e = S_.op("v", lambda e: e.tensor_scalar(t_qstg2[:], t_qstg2[:], MROUND, MROUND,
                                                      ALU.add, ALU.subtract), wait=(ev_q1,), inc="v")
            ev_q3 = S_.op("v", lambda e, wsl=wsl, fc=fc: e.tensor_scalar(
                wsl[:, fc * 512:(fc + 1) * 512], t_qstg2[:], 1.0, -1.0, ALU.min, ALU.max),
                wait=(_e,), inc="v")
            rot_qst2.set(0, ev_q3)
        ev_last = None
        for tt in range(8):
            ti = fsl * 8 + tt
            ys = t_ystg[tt % 2]
            ev_y = S_.op("y", lambda e, ys=ys, tt=tt, fsl=fsl: e.dma_start(
                out=ys[:], in_=d_yln[tt * 128:(tt + 1) * 128, fsl * 512:(fsl + 1) * 512]),
                wait=rot_yst2.wait(tt), inc="ydma", dma=True)
            bank = ti % 4
            ev_mm = None
            for fc in range(32):
                w = (rot_db.wait(ti) if fc == 0 else ())
                if fc == 0 and tt == 0:
                    w = w + (ev_q3,)
                ev_mm = S_.op("t", lambda e, bank=bank, fc=fc, tt=tt, wsl=wsl: e.matmul(
                    pbank[bank][:],
                    t_xq2[:, fc * 1024 + tt * 128: fc * 1024 + tt * 128 + 128],
                    wsl[:, fc * 512:(fc + 1) * 512],
                    start=(fc == 0), stop=(fc == 31)),
                    wait=w, inc="t" if fc == 31 else None)
            ev_st = S_.op("v", lambda e, bank=bank, ys=ys, tt=tt: e.scalar_tensor_tensor(
                ys[:], pbank[bank][:], t_dq2[:, tt:tt + 1], ys[:], ALU.mult, ALU.add),
                wait=(ev_mm, ev_y), inc="v")
            rot_db.set(ti, ev_st)
            ev_wb = S_.op("y", lambda e, ys=ys, tt=tt, fsl=fsl: e.dma_start(
                out=d_yln[tt * 128:(tt + 1) * 128, fsl * 512:(fsl + 1) * 512], in_=ys[:]),
                wait=(ev_st,), inc="ydma", dma=True)
            rot_yst2.set(tt, ev_wb)
            ev_last = ev_wb
        rot_wsl2.set(fsl, ev_last)
    S_.barrier(["t", "s", "v", "y", "g"])

    if dbg == "z2":
        o = dbg_out("d_z2", [NQ, D])
        o2 = dbg_out("d_dq2", [128, 16])
        o5 = dbg_out("d_q2m", [128, 16])
        o6 = dbg_out("d_s2", [128, 128])
        S_.op("y", lambda e, o5=o5: e.dma_start(out=o5, in_=t_q2m[:]), inc="o", dma=True)
        S_.op("y", lambda e, o6=o6: e.dma_start(out=o6, in_=t_st2[:]), inc="o", dma=True)
        S_.op("y", lambda e, o=o: e.dma_start(out=o, in_=d_yln), inc="o", dma=True)
        S_.op("y", lambda e, o2=o2: e.dma_start(out=o2, in_=t_dq2[:]), inc="o", dma=True)
        S_.wait("y", S_.ev("o"))
        return _finish(nc, ctx, S_, ddbg)

    al.release("u_h", "d_wd0", "d_wd1", "d_fs0", "d_fs1", "d_qs", "d_ys0", "d_ys1")
    t_z = [SBA(f"z_t{t}", [128, 2048], F32) for t in range(8)]
    qnames = ["y", "g", "s"]
    qinc = {"y": "ydma", "g": "gdma", "s": "sdma"}
    ev_zld = []
    for t in range(8):
        q = qnames[t % 3]
        ev_zld.append(S_.op(q, lambda e, t=t: e.dma_start(
            out=t_z[t][:], in_=d_yln[t * 128:(t + 1) * 128, :]), inc=qinc[q], dma=True))
    for t in range(8):
        for seg in range(4):
            S_.op("v", lambda e, t=t, seg=seg: e.bn_stats(
                t_st[:, t * 24 + seg * 6: t * 24 + (seg + 1) * 6],
                t_z[t][:, seg * 512:(seg + 1) * 512]), wait=(ev_zld[t],))
    ch8 = [vg()]

    def vq8(fn, extra=()):
        S_.op("v", fn, wait=(ch8[0],) + tuple(extra), inc="v")
        ch8[0] = vg()
        return ch8[0]

    stv = t_st[:, 0:192].rearrange("p (t g c) -> p t g c", g=8, c=3)
    M1, VA = t_st[:, 192:200], t_st[:, 200:208]
    MS, R0 = t_st[:, 272:280], t_st[:, 280:288]
    SQ, NW = t_st[:, 288:296], t_st[:, 296:304]
    INV, BIA = t_st[:, 304:312], t_st[:, 312:320]
    SCR8 = t_st[:, 336:400].rearrange("p (t g) -> p t g", g=8)
    r81 = lambda ap: ap.rearrange("p (t o) -> p t o", o=1)
    vq8(lambda e: e.reduce_sum(r81(M1), stv[:, :, :, 1], axis=AX.X))
    vq8(lambda e: e.tensor_scalar(M1, M1, 1.0 / 8.0, None, ALU.mult))
    vq8(lambda e: e.reduce_sum(r81(VA), stv[:, :, :, 2], axis=AX.X))
    vq8(lambda e: e.tensor_tensor(SCR8, stv[:, :, :, 1], stv[:, :, :, 1], ALU.mult))
    vq8(lambda e: e.reduce_sum(r81(MS), SCR8, axis=AX.X))
    vq8(lambda e: e.tensor_scalar(MS, MS, 256.0, None, ALU.mult))
    vq8(lambda e: e.tensor_add(VA, VA, MS))
    vq8(lambda e: e.tensor_scalar(VA, VA, 1.0 / 2048.0, None, ALU.mult))
    vq8(lambda e: e.tensor_tensor(MS, M1, M1, ALU.mult))
    vq8(lambda e: e.tensor_sub(VA, VA, MS))
    vq8(lambda e: e.tensor_scalar(VA, VA, EPS_LN, None, ALU.add))
    ev_v8 = vq8(lambda e: e.reciprocal(R0, VA))
    ev_sq8 = S_.op("s", lambda e: e.activation(SQ, R0, AF.Sqrt), wait=(ev_v8,), inc="s")
    ch8[0] = ev_sq8
    vq8(lambda e: e.tensor_tensor(NW, SQ, SQ, ALU.mult))
    vq8(lambda e: e.tensor_mul(NW, NW, VA))
    vq8(lambda e: e.tensor_scalar(NW, NW, -1.0, 3.0, ALU.mult, ALU.add))
    vq8(lambda e: e.tensor_mul(NW, NW, SQ))
    vq8(lambda e: e.tensor_scalar(INV, NW, 0.5, None, ALU.mult))
    vq8(lambda e: e.tensor_mul(BIA, M1, INV))
    ev_b8 = vq8(lambda e: e.tensor_scalar(BIA, BIA, -1.0, None, ALU.mult))
    for t in range(8):
        ev_n = S_.op("v", lambda e, t=t: e.scalar_tensor_tensor(
            t_z[t][:], t_z[t][:], INV[:, t:t + 1], BIA[:, t:t + 1].broadcast_to([128, 2048]),
            ALU.mult, ALU.add), wait=(ev_b8,) if t == 0 else (), inc="v")
        q = qnames[t % 3]
        S_.op(q, lambda e, t=t: e.dma_start(
            out=dout[t * 128:(t + 1) * 128, :], in_=t_z[t][:]), wait=(ev_n,), inc="o", dma=True)
    S_.wait("y", S_.ev("o"))
    return _finish(nc, ctx, S_, ddbg)


def _finish(nc, ctx, S_, ddbg):
    with nc.Block() as blk:
        @blk.tensor
        def _(e):
            S_.replay("t", e)

        @blk.vector
        def _(e):
            S_.replay("v", e)

        @blk.scalar
        def _(e):
            S_.replay("s", e)

        @blk.gpsimd
        def _(e):
            S_.replay("g", e)

        @blk.sync
        def _(e):
            S_.replay("y", e)
    ctx.close()
    return nc, ddbg


def _host_inputs(inputs, core):
    b, stripe = core // 2, core % 2
    qb0, qb1 = _core_qrows(stripe)
    qrows = np.concatenate([qb0, qb1])
    x = np.ascontiguousarray(inputs["L0_attn_hidden_states"][b])
    xq = np.ascontiguousarray(x[qrows])
    pe, po = _col_perm_rope_q()
    ke, ko = _col_perm_rope_k()
    cosq, sinq = _rope_tables(qrows)
    cosk, sink = _rope_tables(np.arange(S))
    wo_rows = np.concatenate([np.arange(h * 192, h * 192 + 128) for h in range(H)])
    return {
        "x": x, "xq": xq,
        "wq": inputs["L0_attn_Wq"],
        "wqe": np.ascontiguousarray(inputs["L0_attn_Wqrope"][:, pe]),
        "wqo": np.ascontiguousarray(inputs["L0_attn_Wqrope"][:, po]),
        "wkvc": inputs["L0_attn_Wkvc"],
        "wkdec": inputs["L0_attn_Wkdec"],
        "wvdec": inputs["L0_attn_Wvdec"],
        "wke": np.ascontiguousarray(inputs["L0_attn_Wkrope"][:, ke]),
        "wko": np.ascontiguousarray(inputs["L0_attn_Wkrope"][:, ko]),
        "wo": np.ascontiguousarray(inputs["L0_attn_Wo"][wo_rows]),
        "wupt": np.ascontiguousarray(inputs["L0_ffn_w_up"].T),
        "wdnt": np.ascontiguousarray(inputs["L0_ffn_w_down"].T),
        "cosq": cosq, "sinq": sinq, "cosk": cosk, "sink": sink,
        "mask": np.asarray(_core_masks(stripe)),
    }


def kernel(**inputs):
    nc, _ = build_program(dbg=None)
    in_maps = [_host_inputs(inputs, c) for c in range(8)]
    res = run_bass_kernel_spmd(nc, in_maps, core_ids=list(range(8)))
    out = np.zeros((B, S, D), dtype=np.float32)
    for c in range(8):
        b, stripe = c // 2, c % 2
        qb0, qb1 = _core_qrows(stripe)
        o = res.results[c]["out"]
        out[b, qb0] = o[:512]
        out[b, qb1] = o[512:]
    return out


# version 12 (final)
# speedup vs baseline: 1.7554x; 1.0339x over previous
import numpy as np
import ml_dtypes
from contextlib import ExitStack
import concourse.bass as bass
import concourse.mybir as mybir
from concourse.bass_utils import run_bass_kernel_spmd

F32 = mybir.dt.float32
BF16 = mybir.dt.bfloat16
AF = mybir.ActivationFunctionType
ALU = mybir.AluOpType
AX = mybir.AxisListType
bfd = ml_dtypes.bfloat16

B, S, D = 4, 2048, 2048
H, KV, HD, RD = 16, 4, 128, 64
FULL = HD + RD
NQ = 1024
F2 = 4096
MROUND = float(np.float32(1.5 * 2.0 ** 23))
EPS_LN = 1e-5
EPS_Q = 1e-5
QBITS = 127.0
SCALE_ATT = 1.0 / float(np.sqrt(FULL))
C1 = 1.0 / float(np.sqrt(D))
C2 = 1.0 / float(np.sqrt(F2))


def _hadamard(n):
    h = np.array([[1.0]], dtype=np.float64)
    while h.shape[0] < n:
        h = np.block([[h, h], [h, -h]])
    return h.astype(np.float32)


def _rope_tables(positions):
    fr = (1.0 / 10000.0 ** (np.arange(0, RD, 2, dtype=np.float64) / RD))
    ang = np.outer(fr, positions.astype(np.float64))
    ang = np.tile(ang, (4, 1))
    return np.cos(ang).astype(np.float32), np.sin(ang).astype(np.float32)


def _core_qrows(stripe):
    if stripe == 0:
        return np.arange(0, 512), np.arange(1536, 2048)
    return np.arange(512, 1024), np.arange(1024, 1536)


def _core_masks(stripe):
    qb0, qb1 = _core_qrows(stripe)
    m = np.zeros((128, 16 * 512), dtype=np.float32)
    for i in range(8):
        kpos = i * 128 + np.arange(128)
        m[:, i * 512:(i + 1) * 512] = (kpos[:, None] <= qb0[None, :])
    for i in range(8):
        kpos = (8 + i) * 128 + np.arange(128)
        m[:, (8 + i) * 512:(9 + i) * 512] = (kpos[:, None] <= qb1[None, :])
    return m.astype(bfd)


def _col_perm_rope_q():
    even, odd = [], []
    for g in range(4):
        for p in range(128):
            h = (p // 32) * 4 + g
            f = p % 32
            even.append(h * RD + 2 * f)
            odd.append(h * RD + 2 * f + 1)
    return np.array(even), np.array(odd)


def _col_perm_rope_k():
    even = [(p // 32) * RD + 2 * (p % 32) for p in range(128)]
    odd = [c + 1 for c in even]
    return np.array(even), np.array(odd)


class Alloc:
    def __init__(self, nc):
        self.nc = nc
        base = (nc.sbuf_base + 31) & ~31
        self.free = [(base, nc.sbuf_top - base)]
        self.live = {}

    def alloc(self, name, shape, dt):
        esz = 2 if dt == BF16 else 4
        nbytes = (int(np.prod(shape[1:])) * esz + 31) & ~31
        cands = [(sz, off, i) for i, (off, sz) in enumerate(self.free) if sz >= nbytes]
        cands.sort()
        for sz, off, i in cands[:1]:
            if True:
                t = self.nc.alloc_sbuf_tensor_at(name, list(shape), dt, offset=off)
                self.live[name] = (off, nbytes)
                if sz == nbytes:
                    self.free.pop(i)
                else:
                    self.free[i] = (off + nbytes, sz - nbytes)
                return t
        raise MemoryError(f"SBUF alloc fail {name} {nbytes}B free={self.free}")

    def release(self, *names):
        for name in names:
            off, nbytes = self.live.pop(name)
            self.free.append((off, nbytes))
        self.free.sort()
        merged = []
        for o, s in self.free:
            if merged and merged[-1][0] + merged[-1][1] == o:
                merged[-1][1] += s
            else:
                merged.append([o, s])
        self.free = [tuple(x) for x in merged]


class Sched:
    ENGS = ("t", "v", "s", "g", "y")

    def __init__(self):
        self.q = {e: [] for e in self.ENGS}
        self.cnt = {}
        self.sems = {}
        self.dirty = {e: False for e in self.ENGS}
        self.last = {e: None for e in self.ENGS}

    def add_sem(self, name, sem):
        self.sems[name] = sem
        self.cnt[name] = 0

    def ev(self, sname):
        return (sname, self.cnt[sname])

    def op(self, eng, fn, wait=(), inc=None, dma=False):
        incs = []
        wait = tuple(w for w in wait if w is not None)
        if inc:
            for sname in ([inc] if isinstance(inc, str) else inc):
                self.cnt[sname] += 16 if dma else 1
                incs.append((sname, 16 if dma else 1))
        self.q[eng].append((wait, fn, tuple(incs)))
        if incs:
            self.dirty[eng] = False
            nm = incs[-1][0]
            self.last[eng] = (nm, self.cnt[nm])
            return (nm, self.cnt[nm])
        self.dirty[eng] = True
        return None

    def barrier(self, engs=None):
        engs = engs or [e for e in self.ENGS if self.q[e]]
        evs = []
        for e in engs:
            if self.dirty[e]:
                raise RuntimeError(f"barrier: engine {e} has trailing ops without inc")
            if self.last[e] is not None:
                evs.append(self.last[e])
        for e in engs:
            self.q[e].append((tuple(evs), None, ()))

    def wait(self, eng, *evs):
        self.q[eng].append((tuple(e for e in evs if e), None, ()))

    def replay(self, key, engine):
        for waits, fn, incs in self.q[key]:
            for (sname, val) in waits:
                engine.wait_ge(self.sems[sname], val)
            if fn is not None:
                ins = fn(engine)
                for (sname, step) in incs:
                    ins = ins.then_inc(self.sems[sname], step)


class Rot:
    def __init__(self, n):
        self.n = n
        self.ev = [None] * n

    def wait(self, i):
        e = self.ev[i % self.n]
        return (e,) if e is not None else ()

    def set(self, i, e):
        self.ev[i % self.n] = e


def build_program(dbg=None):
    nc = bass.Bass("TRN2", target_bir_lowering=False, debug=False)
    ctx = ExitStack()

    din = {}
    for name, shp, dt in [
        ("x", [S, D], F32), ("xq", [NQ, D], F32),
        ("wq", [D, H * HD], F32), ("wqe", [D, 512], F32), ("wqo", [D, 512], F32),
        ("wkvc", [D, 512], F32), ("wkdec", [512, 512], F32), ("wvdec", [512, 512], F32),
        ("wke", [512, 128], F32), ("wko", [512, 128], F32),
        ("wo", [2048, D], F32),
        ("wupt", [D, F2], F32), ("wdnt", [F2, D], F32),
        ("cosq", [128, NQ], F32), ("sinq", [128, NQ], F32),
        ("cosk", [128, S], F32), ("sink", [128, S], F32),
        ("mask", [128, 16 * 512], BF16),
    ]:
        din[name] = nc.dram_tensor(name, shp, dt, kind="ExternalInput").ap()
    dout = nc.dram_tensor("out", [NQ, D], F32, kind="ExternalOutput").ap()
    d_yln = nc.dram_tensor("yln_scratch", [NQ, D], F32, kind="Internal").ap()
    ddbg = {}

    def dbg_out(name, shp, dt=F32):
        ddbg[name] = nc.dram_tensor(name, shp, dt, kind="ExternalOutput").ap()
        return ddbg[name]

    h128_bf = nc.inline_tensor(_hadamard(128).astype(bfd), name="h128bf").ap()
    h128_f = nc.inline_tensor(_hadamard(128), name="h128f").ap()
    id_bf = nc.inline_tensor(np.eye(128, dtype=bfd), name="idbf").ap()
    id_f = nc.inline_tensor(np.eye(128, dtype=np.float32), name="idf").ap()
    ones_c_bf = nc.inline_tensor(np.ones((128, 1), dtype=bfd), name="onescbf").ap()
    ones_r_f = nc.inline_tensor(np.ones((1, 128), dtype=np.float32), name="onesrf").ap()
    sel8_np = np.zeros((8, 8 * 128), dtype=np.float32)
    for _j in range(8):
        sel8_np[_j, _j * 128:(_j + 1) * 128] = 1.0
    sel8_c = nc.inline_tensor(sel8_np, name="sel8c").ap()

    al = Alloc(nc)
    SBA = al.alloc
    pbank = [ctx.enter_context(nc.psum_tensor(f"pb{i}", [128, 512], F32)) for i in range(8)]

    S_ = Sched()
    for sname in ("ydma", "gdma", "sdma", "t", "v", "s", "g", "o"):
        S_.add_sem(sname, ctx.enter_context(nc.semaphore(sname)))

    t_cosq = SBA("m_cosq", [128, NQ], F32)
    t_sinq = SBA("m_sinq", [128, NQ], F32)
    t_h128bf = SBA("m_h128bf", [128, 128], BF16)
    t_h128f = SBA("m_h128f", [128, 128], F32)
    t_idbf = SBA("m_idbf", [128, 128], BF16)
    t_idf = SBA("m_idf", [128, 128], F32)
    t_ones_c = SBA("m_onescbf", [128, 1], BF16)
    t_ones_r = SBA("m_onesrf", [1, 128], F32)
    t_st = SBA("m_stats", [128, 1024], F32)
    t_st2 = SBA("m_stats2", [128, 128], F32)
    t_trbf = SBA("m_trbf", [128, 1024], BF16)
    t_q1m = SBA("m_q1m", [128, 16], F32)
    t_d2 = SBA("m_d2", [128, 16], F32)
    t_q2m = SBA("m_q2m", [128, 16], F32)
    t_dq2 = SBA("m_dq2", [128, 16], F32)
    t_wupb = SBA("m_wupb", [128, 1], F32)
    t_rupb = SBA("m_rupb", [128, 1], F32)
    t_wdnb = SBA("m_wdnb", [128, 1], F32)
    t_rdnb = SBA("m_rdnb", [128, 1], F32)
    t_wsc = SBA("m_wsc", [1, 8], F32)
    t_row = [SBA(f"m_row{i}", [1, 512], F32) for i in range(2)]
    t_B = SBA("m_B", [128, 1024], F32)
    t_rbs = [SBA(f"m_rbs{i}", [128, 512], F32) for i in range(2)]
    t_rl = SBA("m_rl", [128, 512], BF16)
    t_sel8 = SBA("m_sel8", [8, 8 * 128], F32)
    t_rowT = SBA("m_rowT", [8, 128], F32)
    t_gapa = SBA("m_gapa", [128, 8], F32)
    t_gapb = SBA("m_gapb", [128, 8], F32)

    def vg():
        return S_.op("v", lambda e: e.tensor_copy(t_gapa[:], t_gapb[:]), inc="v")

    def tvop(vfn, psum_ap, shadow, wait=()):
        S_.op("v", vfn, wait=wait, inc="v")
        ev = vg()
        return S_.op("s", lambda e, s=shadow, p=psum_ap: e.copy(s, p), wait=(ev,), inc="s")

    for dst, src in [(t_cosq[:], din["cosq"]), (t_sinq[:], din["sinq"]),
                     (t_h128bf[:], h128_bf), (t_h128f[:], h128_f), (t_idbf[:], id_bf),
                     (t_idf[:], id_f), (t_ones_c[:], ones_c_bf), (t_ones_r[:], ones_r_f),
                     (t_sel8[:], sel8_c)]:
        S_.op("s", lambda e, d=dst, s=src: e.dma_start(out=d, in_=s), inc="sdma", dma=True)
    ev_perm = S_.ev("sdma")
    for _e in ("t", "v", "g", "y"):
        S_.wait(_e, ev_perm)

    def ln_inplace(yw, ev_in):
        ch = [ev_in]

        def vq(fn, extra=()):
            S_.op("v", fn, wait=(ch[0],) + tuple(extra), inc="v")
            ch[0] = vg()
            return ch[0]

        for seg in range(4):
            vq(lambda e, seg=seg: e.bn_stats(
                t_st[:, seg * 6:(seg + 1) * 6], yw[:, seg * 512:(seg + 1) * 512]))
        stv = t_st[:, 0:24].rearrange("p (g c) -> p g c", c=3)
        vq(lambda e: e.reduce_sum(t_st[:, 24:25], stv[:, :, 1], axis=AX.X))
        vq(lambda e: e.tensor_scalar(t_st[:, 25:26], t_st[:, 24:25], 1.0 / 8.0, None, ALU.mult))
        vq(lambda e: e.reduce_sum(t_st[:, 26:27], stv[:, :, 2], axis=AX.X))
        vq(lambda e: e.tensor_tensor(t_st2[:, 0:8], stv[:, :, 1], stv[:, :, 1], ALU.mult))
        vq(lambda e: e.reduce_sum(t_st[:, 27:28], t_st2[:, 0:8], axis=AX.X))
        vq(lambda e: e.tensor_scalar(t_st[:, 27:28], t_st[:, 27:28], 256.0, None, ALU.mult))
        vq(lambda e: e.tensor_add(t_st[:, 26:27], t_st[:, 26:27], t_st[:, 27:28]))
        vq(lambda e: e.tensor_scalar(t_st[:, 26:27], t_st[:, 26:27], 1.0 / 2048.0, None, ALU.mult))
        vq(lambda e: e.tensor_tensor(t_st[:, 28:29], t_st[:, 25:26], t_st[:, 25:26], ALU.mult))
        vq(lambda e: e.tensor_sub(t_st[:, 26:27], t_st[:, 26:27], t_st[:, 28:29]))
        vq(lambda e: e.tensor_scalar(t_st[:, 26:27], t_st[:, 26:27], EPS_LN, None, ALU.add))
        ev_v = vq(lambda e: e.reciprocal(t_st[:, 29:30], t_st[:, 26:27]))
        ev_sq = S_.op("s", lambda e: e.activation(t_st[:, 30:31], t_st[:, 29:30], AF.Sqrt),
                      wait=(ev_v,), inc="s")
        ch[0] = ev_sq
        vq(lambda e: e.tensor_tensor(t_st[:, 31:32], t_st[:, 30:31], t_st[:, 30:31], ALU.mult))
        vq(lambda e: e.tensor_mul(t_st[:, 31:32], t_st[:, 31:32], t_st[:, 26:27]))
        vq(lambda e: e.tensor_scalar(t_st[:, 31:32], t_st[:, 31:32], -1.0, 3.0, ALU.mult, ALU.add))
        vq(lambda e: e.tensor_mul(t_st[:, 31:32], t_st[:, 31:32], t_st[:, 30:31]))
        vq(lambda e: e.tensor_scalar(t_st[:, 32:33], t_st[:, 31:32], 0.5, None, ALU.mult))
        vq(lambda e: e.tensor_mul(t_st[:, 33:34], t_st[:, 25:26], t_st[:, 32:33]))
        vq(lambda e: e.tensor_scalar(t_st[:, 33:34], t_st[:, 33:34], -1.0, None, ALU.mult))
        return vq(lambda e: e.scalar_tensor_tensor(
            yw[:], yw[:], t_st[:, 32:33], t_st[:, 33:34].broadcast_to([128, 2048]),
            ALU.mult, ALU.add))

    def bcast_col(src11, dst_col, bank):
        ev_mm = S_.op("t", lambda e: e.matmul(pbank[bank][:, 0:1], t_ones_r[:], src11,
                                              start=True, stop=True), inc="t")
        return S_.op("v", lambda e: e.tensor_copy(dst_col, pbank[bank][:, 0:1]),
                     wait=(ev_mm,), inc="v")

    t_xT = SBA("a_xT", [128, 16 * 2048], BF16)
    t_stage = [SBA(f"a_stg{i}", [128, 2048], BF16) for i in range(4)]
    t_wkvc = SBA("a_wkvc", [128, 16 * 512], BF16)
    t_cT = SBA("a_cT", [128, 4 * 2048], BF16)

    ev_wkvc = S_.op("g", lambda e: e.dma_start(
        out=t_wkvc[:].rearrange("p (dc c) -> p dc c", dc=16),
        in_=din["wkvc"].rearrange("(dc p) c -> p dc c", p=128)), inc="gdma", dma=True)

    def stream_transpose(src_dram, nrow_tiles, t_dst, chunk_stride):
        rot_stage = Rot(4)
        rot_bank = Rot(4)
        last = None
        for r in range(nrow_tiles):
            st = t_stage[r % 4]
            ev_ld = S_.op("g", lambda e, st=st, r=r: e.dma_start(
                out=st[:], in_=src_dram[r * 128:(r + 1) * 128, :]),
                wait=rot_stage.wait(r), inc="gdma", dma=True)
            for cg in range(4):
                bank = 2 * (r % 2) + (cg % 2)
                ev_t = None
                for ci in range(4):
                    c = cg * 4 + ci
                    w2 = ((ev_ld,) if (cg == 0 and ci == 0) else ())
                    if ci == 0:
                        w2 = w2 + rot_bank.wait(bank)
                    ev_t = S_.op("t", lambda e, st=st, c=c, bank=bank, ci=ci: e.transpose(
                        pbank[bank][:].bitcast(BF16)[:, ci * 128:(ci + 1) * 128],
                        st[:, c * 128:(c + 1) * 128], t_idbf[:]),
                        wait=w2, inc="t" if ci == 3 else None)
                dst = t_dst[:].rearrange("p (dc t) -> p dc t", t=chunk_stride)[
                    :, cg * 4:(cg + 1) * 4, r * 128:(r + 1) * 128]
                ev_e = S_.op("v", lambda e, dst=dst, bank=bank: e.tensor_copy(
                    dst, pbank[bank][:].bitcast(BF16)[:, 0:512]), wait=(ev_t,), inc="v")
                rot_bank.set(bank, ev_e)
                last = ev_e
            rot_stage.set(r, last)
        return last

    ev_xT = stream_transpose(din["x"], 16, t_xT, 2048)
    S_.barrier(["t", "v", "g", "s"])

    rot_c = Rot(2)
    for tcol in range(4):
        for cf in range(4):
            it = tcol * 4 + cf
            bank = 4 + it % 2
            ev_mm = None
            for dc in range(16):
                w = rot_c.wait(it) if dc == 0 else ()
                ev_mm = S_.op("t", lambda e, bank=bank, dc=dc, cf=cf, tcol=tcol: e.matmul(
                    pbank[bank][:],
                    t_wkvc[:, dc * 512 + cf * 128: dc * 512 + (cf + 1) * 128],
                    t_xT[:, dc * 2048 + tcol * 512: dc * 2048 + (tcol + 1) * 512],
                    start=(dc == 0), stop=(dc == 15)),
                    wait=w, inc="t" if dc == 15 else None)
            ev_ev = S_.op("s", lambda e, bank=bank, cf=cf, tcol=tcol: e.copy(
                t_cT[:, cf * 2048 + tcol * 512: cf * 2048 + (tcol + 1) * 512], pbank[bank][:]),
                wait=(ev_mm,), inc="s")
            rot_c.set(it, ev_ev)
    S_.barrier(["t", "s", "v", "g"])
    al.release("a_xT", "a_wkvc")

    t_kn = SBA("a_kn", [128, KV * 2048], BF16)
    t_vtm = SBA("a_vtm", [128, 16 * 512], BF16)
    t_ke = SBA("a_ke", [128, 2048], BF16)
    t_ko = SBA("a_ko", [128, 2048], BF16)
    t_wkdec = SBA("a_wkdec", [128, 4 * 512], BF16)
    t_wvdec = SBA("a_wvdec", [128, 4 * 512], BF16)
    t_wke = SBA("a_wke", [128, 4 * 128], BF16)
    t_wko = SBA("a_wko", [128, 4 * 128], BF16)
    t_cosk = SBA("a_cosk", [128, S], F32)
    t_sink = SBA("a_sink", [128, S], F32)
    t_ker = SBA("a_ker", [128, 2048], F32)
    t_kor = SBA("a_kor", [128, 2048], F32)
    t_rs1 = SBA("a_rs1", [128, 2048], F32)
    t_rs2 = SBA("a_rs2", [128, 2048], F32)

    for dst, src in [(t_cosk[:], din["cosk"]), (t_sink[:], din["sink"])]:
        S_.op("s", lambda e, d=dst, s=src: e.dma_start(out=d, in_=s), inc="sdma", dma=True)
    for t_w, dname in [(t_wkdec, "wkdec"), (t_wvdec, "wvdec"), (t_wke, "wke"), (t_wko, "wko")]:
        S_.op("g", lambda e, t_w=t_w, dname=dname: e.dma_start(
            out=t_w[:].rearrange("p (cc c) -> p cc c", cc=4),
            in_=din[dname].rearrange("(cc p) c -> p cc c", p=128)), inc="gdma", dma=True)
    ev_kw = S_.ev("gdma")
    ev_kcs = S_.ev("sdma")

    rot_k = Rot(4)
    it = 0
    for kv in range(KV):
        for kcol in range(4):
            bank = it % 4
            ev_mm = None
            for cc in range(4):
                w = rot_k.wait(it) if cc == 0 else ()
                if it == 0 and cc == 0:
                    w = w + (ev_kw,)
                ev_mm = S_.op("t", lambda e, bank=bank, cc=cc, kv=kv, kcol=kcol: e.matmul(
                    pbank[bank][:],
                    t_wkdec[:, cc * 512 + kv * 128: cc * 512 + (kv + 1) * 128],
                    t_cT[:, cc * 2048 + kcol * 512: cc * 2048 + (kcol + 1) * 512],
                    start=(cc == 0), stop=(cc == 3)),
                    wait=w, inc="t" if cc == 3 else None)
            ev_ev = S_.op("s", lambda e, bank=bank, kv=kv, kcol=kcol: e.copy(
                t_kn[:, kv * 2048 + kcol * 512: kv * 2048 + (kcol + 1) * 512], pbank[bank][:]),
                wait=(ev_mm,), inc="s")
            rot_k.set(it, ev_ev)
            it += 1
    for kt in range(16):
        bank = it % 4
        ev_mm = None
        for cc in range(4):
            w = rot_k.wait(it) if cc == 0 else ()
            ev_mm = S_.op("t", lambda e, bank=bank, cc=cc, kt=kt: e.matmul(
                pbank[bank][:],
                t_cT[:, cc * 2048 + kt * 128: cc * 2048 + kt * 128 + 128],
                t_wvdec[:, cc * 512:(cc + 1) * 512],
                start=(cc == 0), stop=(cc == 3)),
                wait=w, inc="t" if cc == 3 else None)
        ev_ev = S_.op("s", lambda e, bank=bank, kt=kt: e.copy(
            t_vtm[:, kt * 512:(kt + 1) * 512], pbank[bank][:]), wait=(ev_mm,), inc="s")
        rot_k.set(it, ev_ev)
        it += 1
    for t_dst, t_w in [(t_ker, t_wke), (t_kor, t_wko)]:
        for kcol in range(4):
            bank = it % 4
            ev_mm = None
            for cc in range(4):
                w = rot_k.wait(it) if cc == 0 else ()
                ev_mm = S_.op("t", lambda e, bank=bank, cc=cc, kcol=kcol, t_w=t_w: e.matmul(
                    pbank[bank][:],
                    t_w[:, cc * 128:(cc + 1) * 128],
                    t_cT[:, cc * 2048 + kcol * 512: cc * 2048 + (kcol + 1) * 512],
                    start=(cc == 0), stop=(cc == 3)),
                    wait=w, inc="t" if cc == 3 else None)
            ev_ev = S_.op("s", lambda e, bank=bank, kcol=kcol, t_dst=t_dst: e.copy(
                t_dst[:, kcol * 512:(kcol + 1) * 512], pbank[bank][:]), wait=(ev_mm,), inc="s")
            rot_k.set(it, ev_ev)
            it += 1
    S_.barrier(["t", "s", "v", "g"])
    S_.wait("v", ev_kcs)
    _e = S_.op("v", lambda e: e.tensor_mul(t_rs1[:], t_ker[:], t_cosk[:]), inc="v")
    _e = S_.op("v", lambda e: e.tensor_mul(t_rs2[:], t_kor[:], t_sink[:]), wait=(_e,), inc="v")
    _e = S_.op("v", lambda e: e.tensor_sub(t_ke[:], t_rs1[:], t_rs2[:]), wait=(_e,), inc="v")
    _e = S_.op("v", lambda e: e.tensor_mul(t_rs1[:], t_ker[:], t_sink[:]), wait=(_e,), inc="v")
    _e = S_.op("v", lambda e: e.tensor_mul(t_rs2[:], t_kor[:], t_cosk[:]), wait=(_e,), inc="v")
    _e = S_.op("v", lambda e: e.tensor_add(t_ko[:], t_rs1[:], t_rs2[:]), wait=(_e,), inc="v")
    t_keo = [SBA(f"a_keo{i}", [128, 2048], BF16) for i in range(2)]
    for g4 in range(4):
        for side, tsrc in enumerate((t_ke, t_ko)):
            S_.op("g", lambda e, g4=g4, side=side, tsrc=tsrc: e.dma_start(
                out=t_keo[g4 // 2][(g4 % 2) * 64 + side * 32:(g4 % 2) * 64 + side * 32 + 32, :],
                in_=tsrc[g4 * 32:(g4 + 1) * 32, :]), wait=(_e,), inc="gdma", dma=True)
    S_.barrier(["t", "s", "v", "g"])
    al.release("a_wkdec", "a_wvdec", "a_wke", "a_wko", "a_cosk", "a_sink",
               "a_ker", "a_kor", "a_rs1", "a_rs2", "a_cT", "a_ke", "a_ko")

    t_xqT = SBA("a_xqT", [128, 16 * NQ], BF16)
    t_qn = SBA("a_qn", [128, H * NQ], BF16)
    t_qe = SBA("a_qe", [128, 4 * NQ], BF16)
    t_qo = SBA("a_qo", [128, 4 * NQ], BF16)
    stream_transpose(din["xq"], 8, t_xqT, NQ)
    S_.barrier(["t", "v", "g"])
    al.release("a_stg0", "a_stg1", "a_stg2", "a_stg3")

    t_wqs = [SBA(f"a_wqs{i}", [128, 16 * 512], BF16) for i in range(2)]
    rot_q = Rot(4)
    it = 0
    ev_slab_used = [None, None]
    for slab in range(4):
        sb = t_wqs[slab % 2]
        ev_wq = S_.op("g", lambda e, sb=sb, slab=slab: e.dma_start(
            out=sb[:].rearrange("p (dc c) -> p dc c", dc=16),
            in_=din["wq"][:, slab * 512:(slab + 1) * 512].rearrange("(dc p) c -> p dc c", p=128)),
            wait=(ev_slab_used[slab % 2],) if ev_slab_used[slab % 2] else (), inc="gdma", dma=True)
        last_mm = None
        for hh in range(4):
            h = slab * 4 + hh
            for qcol in range(2):
                bank = it % 4
                ev_mm = None
                for dc in range(16):
                    w = rot_q.wait(it) if dc == 0 else ()
                    if dc == 0 and hh == 0 and qcol == 0:
                        w = w + (ev_wq,)
                    ev_mm = S_.op("t", lambda e, bank=bank, dc=dc, hh=hh, qcol=qcol, sb=sb: e.matmul(
                        pbank[bank][:],
                        sb[:, dc * 512 + hh * 128: dc * 512 + (hh + 1) * 128],
                        t_xqT[:, dc * NQ + qcol * 512: dc * NQ + (qcol + 1) * 512],
                        start=(dc == 0), stop=(dc == 15)),
                        wait=w, inc="t" if dc == 15 else None)
                last_mm = ev_mm
                ev_ev = S_.op("s", lambda e, bank=bank, h=h, qcol=qcol: e.copy(
                    t_qn[:, h * NQ + qcol * 512: h * NQ + (qcol + 1) * 512], pbank[bank][:]),
                    wait=(ev_mm,), inc="s")
                rot_q.set(it, ev_ev)
                it += 1
        ev_slab_used[slab % 2] = last_mm
    al.release("a_wqs0", "a_wqs1")
    t_wqe = SBA("a_wqe", [128, 16 * 512], BF16)
    t_wqo = SBA("a_wqo", [128, 16 * 512], BF16)
    t_re = t_B[:, 0:512]
    t_ro = t_B[:, 512:1024]
    t_m1 = t_st[:, 0:512]
    t_m2 = t_st[:, 512:1024]
    ev_we = S_.op("g", lambda e: e.dma_start(
        out=t_wqe[:].rearrange("p (dc c) -> p dc c", dc=16),
        in_=din["wqe"].rearrange("(dc p) c -> p dc c", p=128)),
        wait=(ev_slab_used[0],), inc="gdma", dma=True)
    ev_wo_ = S_.op("g", lambda e: e.dma_start(
        out=t_wqo[:].rearrange("p (dc c) -> p dc c", dc=16),
        in_=din["wqo"].rearrange("(dc p) c -> p dc c", p=128)),
        wait=(ev_slab_used[1],), inc="gdma", dma=True)
    ev_cmb = None
    for g in range(4):
        for qcol in range(2):
            evs = []
            for side, (t_w, t_raw, evw) in enumerate([(t_wqe, t_re, ev_we), (t_wqo, t_ro, ev_wo_)]):
                bank = (it + side) % 4
                ev_mm = None
                for dc in range(16):
                    w = rot_q.wait(it + side) if dc == 0 else ()
                    if dc == 0 and g == 0 and qcol == 0:
                        w = w + (evw,)
                    ev_mm = S_.op("t", lambda e, bank=bank, dc=dc, g=g, qcol=qcol, t_w=t_w: e.matmul(
                        pbank[bank][:],
                        t_w[:, dc * 512 + g * 128: dc * 512 + (g + 1) * 128],
                        t_xqT[:, dc * NQ + qcol * 512: dc * NQ + (qcol + 1) * 512],
                        start=(dc == 0), stop=(dc == 15)),
                        wait=w, inc="t" if dc == 15 else None)
                ev_ev = S_.op("s", lambda e, bank=bank, t_raw=t_raw: e.copy(
                    t_raw, pbank[bank][:]),
                    wait=(ev_mm,) + ((ev_cmb,) if ev_cmb else ()), inc="s")
                evs.append(ev_ev)
                rot_q.set(it + side, ev_ev)
            cqs = t_cosq[:, qcol * 512:(qcol + 1) * 512]
            sqs = t_sinq[:, qcol * 512:(qcol + 1) * 512]
            qeo = t_qe[:, g * NQ + qcol * 512: g * NQ + (qcol + 1) * 512]
            qoo = t_qo[:, g * NQ + qcol * 512: g * NQ + (qcol + 1) * 512]
            _e = S_.op("v", lambda e, cqs=cqs: e.tensor_tensor(t_m1, t_re, cqs, ALU.mult),
                       wait=(evs[0], evs[1]), inc="v")
            _e = S_.op("v", lambda e, sqs=sqs: e.tensor_tensor(t_m2, t_ro, sqs, ALU.mult),
                       wait=(_e,), inc="v")
            _e = S_.op("v", lambda e, qeo=qeo: e.tensor_sub(qeo, t_m1, t_m2), wait=(_e,), inc="v")
            _e = S_.op("v", lambda e, sqs=sqs: e.tensor_tensor(t_m1, t_re, sqs, ALU.mult),
                       wait=(_e,), inc="v")
            _e = S_.op("v", lambda e, cqs=cqs: e.tensor_tensor(t_m2, t_ro, cqs, ALU.mult),
                       wait=(_e,), inc="v")
            ev_cmb = S_.op("v", lambda e, qoo=qoo: e.tensor_add(qoo, t_m1, t_m2),
                           wait=(_e,), inc="v")
            it += 2
    t_qeo = [SBA(f"a_qeo{i}", [128, 4 * NQ], BF16) for i in range(2)]
    for g4 in range(4):
        for side, tsrc in enumerate((t_qe, t_qo)):
            S_.op("g", lambda e, g4=g4, side=side, tsrc=tsrc: e.dma_start(
                out=t_qeo[g4 // 2][(g4 % 2) * 64 + side * 32:(g4 % 2) * 64 + side * 32 + 32, :],
                in_=tsrc[g4 * 32:(g4 + 1) * 32, :]), wait=(ev_cmb,), inc="gdma", dma=True)
    S_.barrier(["t", "s", "v", "g"])
    al.release("a_xqT", "a_wqe", "a_wqo", "a_qe", "a_qo")

    if dbg == "proj":
        for nm, t in [("d_kn", t_kn), ("d_vtm", t_vtm), ("d_ke", t_keo[0]), ("d_ko", t_keo[1]),
                      ("d_qn", t_qn), ("d_qe", t_qeo[0]), ("d_qo", t_qeo[1])]:
            o = dbg_out(nm, list(t[:].shape))
            S_.op("g", lambda e, o=o, t=t: e.dma_start(out=o, in_=t[:]), inc="o", dma=True)
        S_.wait("g", S_.ev("o"))
        return _finish(nc, ctx, S_, ddbg)

    t_mask = SBA("m_mask", [128, 16 * 512], BF16)
    ev_mask = S_.op("s", lambda e: e.dma_start(out=t_mask[:], in_=din["mask"]),
                    inc="sdma", dma=True)
    S_.wait("v", ev_mask)
    t_pt = [SBA(f"b_pt{i}", [128, 512], BF16) for i in range(8)]
    t_ao = SBA("b_ao", [128, H * NQ], BF16)
    rot_sb = Rot(4)
    rot_pt = Rot(8)
    rot_ob = Rot(2)
    rot_lb = Rot(2)
    gs = 0
    itt = 0
    LAG = 3
    pending = []

    def flush_pending(upto):
        while pending and pending[0][0] <= upto:
            _, fn = pending.pop(0)
            fn()

    for h in range(H):
        kv = h // 4
        g = h // 4
        gq = h % 4
        for qb in range(2):
            nslots = 8 if qb == 0 else 16
            ob = 4 + itt % 2
            lb = 6 + itt % 2
            state = {"l": None, "av": None}
            for j in range(nslots):
                kt = j
                bank = gs % 4
                S_.op("t", lambda e, bank=bank, kv=kv, kt=kt, h=h, qb=qb: e.matmul(
                    pbank[bank][:],
                    t_kn[:, kv * 2048 + kt * 128: kv * 2048 + (kt + 1) * 128],
                    t_qn[:, h * NQ + qb * 512: h * NQ + (qb + 1) * 512],
                    start=True, stop=False), wait=rot_sb.wait(gs))
                ev_mm = S_.op("t", lambda e, bank=bank, g=g, kt=kt, gq=gq, qb=qb: e.matmul(
                    pbank[bank][:],
                    t_keo[g // 2][(g % 2) * 64:(g % 2) * 64 + 64, kt * 128:(kt + 1) * 128],
                    t_qeo[g // 2][(g % 2) * 64:(g % 2) * 64 + 64,
                                  gq * NQ + qb * 512: gq * NQ + (qb + 1) * 512],
                    start=False, stop=True, tile_position=((g % 2) * 64, 0)), inc="t")
                pt = t_pt[gs % 8]
                ev_exp = S_.op("s", lambda e, pt=pt, bank=bank: e.activation(
                    pt[:], pbank[bank][:], AF.Exp, scale=SCALE_ATT),
                    wait=(ev_mm,) + rot_pt.wait(gs), inc="s")
                rot_sb.set(gs, ev_exp)
                ev_dep = ev_exp
                if qb == 0 or j >= 8:
                    ev_dep = S_.op("v", lambda e, pt=pt, j=j: e.tensor_mul(
                        pt[:], pt[:], t_mask[:, j * 512:(j + 1) * 512]),
                        wait=(ev_exp,), inc="v")

                def emit_lav(j=j, pt=pt, ev_dep=ev_dep, lb=lb, ob=ob, kt=kt, kv=kv,
                             nslots=nslots, itt=itt, gs=gs, state=state):
                    w = (ev_dep,)
                    if j == 0:
                        w = w + rot_lb.wait(itt) + rot_ob.wait(itt)
                    state["l"] = S_.op("t", lambda e: e.matmul(
                        pbank[lb][0:1, :], t_ones_c[:], pt[:],
                        start=(j == 0), stop=(j == nslots - 1)),
                        wait=w, inc="t" if j == nslots - 1 else None)
                    ev_av = S_.op("t", lambda e: e.matmul(
                        pbank[ob][:], t_vtm[:, kt * 512 + kv * 128: kt * 512 + (kv + 1) * 128],
                        pt[:], start=(j == 0), stop=(j == nslots - 1)), inc="t")
                    rot_pt.set(gs, ev_av)
                    state["av"] = ev_av

                pending.append((gs, emit_lav))
                flush_pending(gs - LAG)
                gs += 1
            def emit_tail(lb=lb, ob=ob, h=h, qb=qb, itt=itt, state=state):
                ev_r = S_.op("v", lambda e: e.reciprocal(
                    t_row[itt % 2][:], pbank[lb][0:1, :]),
                    wait=(state["l"], state["av"]), inc="v")
                ev_rb = S_.op("t", lambda e: e.matmul(
                    pbank[lb][:], t_ones_r[:], t_row[itt % 2][:], start=True, stop=True),
                    wait=(ev_r,), inc="t")
                ev_rbs = S_.op("v", lambda e: e.tensor_copy(
                    t_rbs[itt % 2][:], pbank[lb][:]), wait=(ev_rb,), inc="v")
                ev_on = S_.op("v", lambda e: e.tensor_mul(
                    t_ao[:, h * NQ + qb * 512: h * NQ + (qb + 1) * 512],
                    pbank[ob][:], t_rbs[itt % 2][:]), wait=(ev_rbs,), inc="v")
                rot_ob.set(itt, ev_on)
                rot_lb.set(itt, ev_on)

            pending.append((gs - 1 + 4, emit_tail))
            itt += 1
    flush_pending(10 ** 9)
    S_.barrier(["t", "s", "v", "g"])
    al.release("a_kn", "a_vtm", "a_keo0", "a_keo1", "a_qn", "a_qeo0", "a_qeo1",
               "b_pt0", "b_pt1", "b_pt2", "b_pt3", "b_pt4", "b_pt5", "b_pt6", "b_pt7")

    if dbg == "attn":
        o = dbg_out("d_ao", [128, H * NQ])
        for _eng in ("v", "s", "t", "y"):
            if S_.last[_eng]:
                S_.wait("g", S_.last[_eng])
        S_.op("g", lambda e, o=o: e.dma_start(out=o, in_=t_ao[:]), inc="o", dma=True)
        S_.wait("g", S_.ev("o"))
        return _finish(nc, ctx, S_, ddbg)

    t_wosb2 = [SBA(f"c_wo{i}", [128, 8 * 2048], BF16) for i in range(2)]
    t_ywin = [SBA(f"c_yw{i}", [128, 2048], F32) for i in range(2)]
    t_xqst = [SBA(f"c_xq{i}", [128, 2048], F32) for i in range(2)]
    ev_wo = None
    for i in range(2):
        ev_wo = S_.op("g", lambda e, i=i: e.dma_start(
            out=t_wosb2[i][:].rearrange("p (h f) -> p h f", h=8),
            in_=din["wo"][i * 8 * 128:(i + 1) * 8 * 128, :].rearrange(
                "(h p) f -> p h f", p=128)), inc="gdma", dma=True)

    rot_y = Rot(2)
    rot_xs = Rot(2)
    for qt in range(8):
        ev_x = S_.op("s", lambda e, qt=qt: e.dma_start(
            out=t_xqst[qt % 2][:], in_=din["xq"][qt * 128:(qt + 1) * 128, :]),
            wait=rot_xs.wait(qt), inc="sdma", dma=True)
        bank0 = 4 * (qt % 2)
        ev_fs = [None] * 4
        for hh in range(16):
            for fs in range(4):
                w = ()
                if hh == 0:
                    w = rot_y.wait(qt)
                    if qt == 0 and fs == 0:
                        w = w + (ev_wo,)
                ev_fs[fs] = S_.op("t", lambda e, bank0=bank0, fs=fs, hh=hh, qt=qt: e.matmul(
                    pbank[bank0 + fs][:],
                    t_ao[:, hh * NQ + qt * 128: hh * NQ + qt * 128 + 128],
                    t_wosb2[hh // 8][:, (hh % 8) * 2048 + fs * 512:
                                     (hh % 8) * 2048 + (fs + 1) * 512],
                    start=(hh == 0), stop=(hh == 15)),
                    wait=w, inc="t" if hh == 15 else None)
        yw = t_ywin[qt % 2]
        ev_z = None
        for fs in range(4):
            ev_z = S_.op("v", lambda e, bank0=bank0, fs=fs, yw=yw, qt=qt: e.tensor_add(
                yw[:, fs * 512:(fs + 1) * 512], pbank[bank0 + fs][:],
                t_xqst[qt % 2][:, fs * 512:(fs + 1) * 512]),
                wait=(ev_fs[fs], ev_x) if fs == 0 else (ev_fs[fs],),
                inc="v" if fs == 3 else None)
        rot_xs.set(qt, ev_z)
        ev_ln = ln_inplace(yw, ev_z)
        ev_sp = S_.op("y", lambda e, yw=yw, qt=qt: e.dma_start(
            out=d_yln[qt * 128:(qt + 1) * 128, :], in_=yw[:]), wait=(ev_ln,), inc="ydma", dma=True)
        rot_y.set(qt, ev_sp)
    S_.barrier(["t", "s", "v", "g", "y"])
    al.release("c_wo0", "c_wo1", "c_xq0", "c_xq1", "b_ao", "c_yw0", "c_yw1")

    if dbg == "wo":
        o = dbg_out("d_yln", [NQ, D])
        S_.op("g", lambda e, o=o: e.dma_start(out=o, in_=d_yln), inc="o", dma=True)
        S_.wait("g", S_.ev("o"))
        return _finish(nc, ctx, S_, ddbg)

    al.release("m_mask", "m_rbs0", "m_rbs1", "m_row0", "m_row1", "m_cosq", "m_sinq")
    t_yfm = SBA("f_yfm", [128, 16 * 512], F32)
    t_B2f = SBA("f_B2f", [128, 16 * 512], F32)
    t_xq1 = SBA("f_xq1", [128, 16 * NQ], BF16)
    t_yst = [SBA(f"f_yst{i}", [128, 2048], F32) for i in range(2)]
    rot_ys = Rot(2)
    rot_bank4b = Rot(4)
    for ch in range(2):
        last_e = None
        for rt in range(4):
            qt = ch * 4 + rt
            ys = t_yst[rt % 2]
            ev_ld = S_.op("y", lambda e, ys=ys, qt=qt: e.dma_start(
                out=ys[:], in_=d_yln[qt * 128:(qt + 1) * 128, :]),
                wait=rot_ys.wait(rt), inc="ydma", dma=True)
            for cg in range(4):
                bank = 2 * (rt % 2) + (cg % 2)
                ev_t = None
                for ci in range(4):
                    c = cg * 4 + ci
                    w2 = ((ev_ld,) if (cg == 0 and ci == 0) else ())
                    if ci == 0:
                        w2 = w2 + rot_bank4b.wait(bank)
                    ev_t = S_.op("t", lambda e, ys=ys, c=c, bank=bank, ci=ci: e.transpose(
                        pbank[bank][:, ci * 128:(ci + 1) * 128],
                        ys[:, c * 128:(c + 1) * 128], t_idf[:]),
                        wait=w2, inc="t" if ci == 3 else None)
                dst = t_yfm[:].rearrange("p (c t) -> p c t", c=16)[
                    :, cg * 4:(cg + 1) * 4, rt * 128:(rt + 1) * 128]
                last_e = S_.op("v", lambda e, dst=dst, bank=bank: e.tensor_copy(
                    dst, pbank[bank][:]), wait=(ev_t,), inc="v")
                rot_bank4b.set(bank, last_e)
            rot_ys.set(rt, last_e)
        S_.barrier(["t", "v", "y", "s"])
        if dbg == "yfm" and ch == 0:
            o = dbg_out("d_yfm", [128, 16 * 512])
            S_.wait("g", S_.last["v"])
            S_.op("g", lambda e, o=o: e.dma_start(out=o, in_=t_yfm[:]), inc="o", dma=True)
            S_.wait("g", S_.ev("o"))
            return _finish(nc, ctx, S_, ddbg)
        rot_f = Rot(2)
        for c in range(16):
            bank = c % 2
            ev_mm = S_.op("t", lambda e, c=c, bank=bank: e.matmul(
                pbank[bank][:], t_h128f[:], t_yfm[:, c * 512:(c + 1) * 512],
                start=True, stop=True), wait=rot_f.wait(c), inc="t")
            ev_e = S_.op("s", lambda e, c=c, bank=bank: e.copy(
                t_yfm[:, c * 512:(c + 1) * 512], pbank[bank][:]), wait=(ev_mm,), inc="s")
            rot_f.set(c, ev_e)
        S_.barrier(["t", "s", "v"])
        if dbg == "yfA" and ch == 0:
            o = dbg_out("d_yfm", [128, 16 * 512])
            S_.wait("g", S_.last["s"])
            S_.op("g", lambda e, o=o: e.dma_start(out=o, in_=t_yfm[:]), inc="o", dma=True)
            S_.wait("g", S_.ev("o"))
            return _finish(nc, ctx, S_, ddbg)
        bufs = [t_yfm, t_B2f]
        _e = None
        for si, stt in enumerate([8, 4, 2, 1]):
            src_v = bufs[si % 2][:].rearrange("p (g two i t) -> p g two i t", two=2, i=stt, t=512)
            dst_v = bufs[(si + 1) % 2][:].rearrange("p (g two i t) -> p g two i t", two=2, i=stt, t=512)
            S_.op("v", lambda e, s=src_v, d=dst_v: e.tensor_tensor(
                d[:, :, 0], s[:, :, 0], s[:, :, 1], ALU.add), wait=(_e,) if _e else ())
            _e = S_.op("v", lambda e, s=src_v, d=dst_v: e.tensor_tensor(
                d[:, :, 1], s[:, :, 0], s[:, :, 1], ALU.subtract), inc="v")
        if dbg == "xf1" and ch == 0:
            o = dbg_out("d_yfm", [128, 16 * 512])
            S_.wait("g", S_.last["v"])
            S_.op("g", lambda e, o=o: e.dma_start(out=o, in_=t_yfm[:]), inc="o", dma=True)
            S_.wait("g", S_.ev("o"))
            return _finish(nc, ctx, S_, ddbg)
        _e = S_.op("v", lambda e: e.tensor_tensor(t_st[:, 0:512], t_yfm[:, 0:512],
                                                  t_yfm[:, 512:1024], ALU.max), inc="v")
        for c in range(2, 16):
            _e = S_.op("v", lambda e, c=c: e.tensor_tensor(
                t_st[:, 0:512], t_st[:, 0:512], t_yfm[:, c * 512:(c + 1) * 512], ALU.max),
                wait=(_e,), inc="v")
        _e = S_.op("v", lambda e: e.tensor_tensor(t_st[:, 512:1024], t_yfm[:, 0:512],
                                                  t_yfm[:, 512:1024], ALU.min), wait=(_e,), inc="v")
        for c in range(2, 16):
            _e = S_.op("v", lambda e, c=c: e.tensor_tensor(
                t_st[:, 512:1024], t_st[:, 512:1024], t_yfm[:, c * 512:(c + 1) * 512], ALU.min),
                wait=(_e,), inc="v")
        _e = S_.op("v", lambda e: e.tensor_scalar(t_st[:, 512:1024], t_st[:, 512:1024],
                                                  -1.0, None, ALU.mult), wait=(_e,), inc="v")
        ev_tree = S_.op("v", lambda e: e.tensor_tensor(
            t_st[:, 0:512], t_st[:, 0:512], t_st[:, 512:1024], ALU.max), wait=(_e,), inc="v")
        ev_tp = None
        for jcol in range(4):
            ev_tp = S_.op("t", lambda e, jcol=jcol: e.transpose(
                pbank[6][:, jcol * 128:(jcol + 1) * 128],
                t_st[:, jcol * 128:(jcol + 1) * 128], t_idf[:]),
                wait=(ev_tree,) if jcol == 0 else (), inc="t" if jcol == 3 else None)
        ev_q1m = None
        for jcol in range(4):
            ev_q1m = S_.op("v", lambda e, jcol=jcol: e.reduce_max(
                pbank[5][:, jcol: jcol + 1],
                pbank[6][:, jcol * 128:(jcol + 1) * 128], axis=AX.X),
                wait=(ev_tp,) if jcol == 0 else (), inc="v" if jcol == 3 else None)
        ev_q1m = vg()
        ev_q1m = S_.op("s", lambda e, ch=ch: e.copy(
            t_q1m[:, ch * 4:(ch + 1) * 4], pbank[5][:, 0:4]), wait=(ev_q1m,), inc="s")
        if dbg == "q1r" and ch == 0:
            o = dbg_out("d_q1m", [128, 16])
            o2 = dbg_out("d_pb6", [128, 512])
            ev_c6 = S_.op("v", lambda e: e.tensor_copy(t_B[:, 0:512], pbank[6][:]),
                          wait=(ev_q1m,), inc="v")
            S_.wait("g", ev_c6)
            S_.op("g", lambda e, o=o: e.dma_start(out=o, in_=t_q1m[:]), inc="o", dma=True)
            S_.op("g", lambda e, o2=o2: e.dma_start(out=o2, in_=t_B[:, 0:512]), inc="o", dma=True)
            S_.wait("g", S_.ev("o"))
            return _finish(nc, ctx, S_, ddbg)
        ev = tvop(lambda e, ch=ch: e.tensor_scalar(
            pbank[5][:, 8:12], t_q1m[:, ch * 4:(ch + 1) * 4], EPS_Q / C1, None, ALU.max),
            pbank[5][:, 8:12], t_q1m[:, ch * 4:(ch + 1) * 4], wait=(ev_q1m,))
        ev = tvop(lambda e, ch=ch: e.reciprocal(pbank[5][:, 12:16], t_q1m[:, ch * 4:(ch + 1) * 4]),
                  pbank[5][:, 12:16], t_st2[:, 0:4], wait=(ev,))
        ev_s1 = tvop(lambda e: e.tensor_scalar(pbank[5][:, 8:12], t_st2[:, 0:4],
                                               QBITS, None, ALU.mult),
                     pbank[5][:, 8:12], t_st2[:, 0:4], wait=(ev,))
        ev_tp2 = S_.op("t", lambda e: e.transpose(
            pbank[7][0:4, 0:128], t_st2[:, 0:4], t_idf[:]), wait=(ev_s1,), inc="t")
        ev_rT = S_.op("v", lambda e: e.tensor_copy(t_rowT[0:4, :], pbank[7][0:4, 0:128]),
                      wait=(ev_tp2,), inc="v")
        ev_b1 = None
        for j in range(4):
            ev_b1 = S_.op("t", lambda e, j=j: e.matmul(
                pbank[4][:, j * 128:(j + 1) * 128], t_sel8[0:4, j * 128:(j + 1) * 128],
                t_rowT[0:4, :], start=True, stop=True),
                wait=(ev_rT,) if j == 0 else (), inc="t" if j == 3 else None)
        ev_B1 = S_.op("v", lambda e: e.tensor_copy(t_B[:, 0:512], pbank[4][:]),
                      wait=(ev_b1,), inc="v")
        if dbg == "q1s" and ch == 0:
            o = dbg_out("d_B", [128, 1024])
            o2 = dbg_out("d_q1m", [128, 16])
            o3 = dbg_out("d_s1", [128, 128])
            o4 = dbg_out("d_pmax", [128, 1024])
            S_.wait("g", ev_B1)
            S_.op("g", lambda e, o=o: e.dma_start(out=o, in_=t_B[:]), inc="o", dma=True)
            S_.op("g", lambda e, o2=o2: e.dma_start(out=o2, in_=t_q1m[:]), inc="o", dma=True)
            S_.op("g", lambda e, o3=o3: e.dma_start(out=o3, in_=t_st2[:]), inc="o", dma=True)
            S_.op("g", lambda e, o4=o4: e.dma_start(out=o4, in_=t_st[:]), inc="o", dma=True)
            S_.wait("g", S_.ev("o"))
            return _finish(nc, ctx, S_, ddbg)
        ev_x1 = ev_B1
        for c in range(16):
            ev_x1 = S_.op("v", lambda e, c=c: e.tensor_tensor(
                t_st[:, 0:512], t_yfm[:, c * 512:(c + 1) * 512], t_B[:, 0:512], ALU.mult),
                wait=(ev_x1,), inc="v")
            ev_x1 = S_.op("v", lambda e, c=c, ch=ch: e.tensor_scalar(
                t_xq1[:, c * NQ + ch * 512: c * NQ + (ch + 1) * 512],
                t_st[:, 0:512], MROUND, MROUND, ALU.add, ALU.subtract),
                wait=(ev_x1,), inc="v")
        S_.barrier(["t", "s", "v", "y"])
    al.release("f_yfm", "f_B2f")

    if dbg == "xq1":
        o = dbg_out("d_xq1", [128, 16 * NQ])
        o2 = dbg_out("d_q1m", [128, 16])
        for _eng in ("v", "s", "t", "y"):
            if S_.last[_eng]:
                S_.wait("g", S_.last[_eng])
        S_.op("g", lambda e, o=o: e.dma_start(out=o, in_=t_xq1[:]), inc="o", dma=True)
        S_.op("g", lambda e, o2=o2: e.dma_start(out=o2, in_=t_q1m[:]), inc="o", dma=True)
        S_.wait("g", S_.ev("o"))
        return _finish(nc, ctx, S_, ddbg)

    t_wstg = [SBA(f"w_stg{i}", [128, F2], BF16) for i in range(2)]
    t_wabs = [SBA(f"w_abs{i}", [128, F2], BF16) for i in range(2)]

    def weight_mean(dname, nrow_t, ncols, dst_w, dst_r):
        rot_s = Rot(2)
        rot_a = Rot(2)
        nmm = ncols // 512
        ev_mm = None
        for r in range(nrow_t):
            stg = t_wstg[r % 2]
            ev_ld = S_.op("g", lambda e, stg=stg, r=r: e.dma_start(
                out=stg[:, 0:ncols], in_=din[dname][r * 128:(r + 1) * 128, :]),
                wait=rot_s.wait(r), inc="gdma", dma=True)
            wab = t_wabs[r % 2]
            ev_ab = S_.op("s", lambda e, stg=stg, ncols=ncols, wab=wab: e.activation(
                wab[:, 0:ncols], stg[:, 0:ncols], AF.Abs),
                wait=(ev_ld,) + rot_a.wait(r), inc="s")
            rot_s.set(r, ev_ab)
            for n in range(nmm):
                ev_mm = S_.op("t", lambda e, n=n, r=r, nrow_t=nrow_t, nmm=nmm, wab=wab: e.matmul(
                    pbank[2][0:1, :], t_ones_c[:], wab[:, n * 512:(n + 1) * 512],
                    start=(r == 0 and n == 0), stop=(r == nrow_t - 1 and n == nmm - 1)),
                    wait=(ev_ab,) if n == 0 else (), inc="t" if n == nmm - 1 else None)
            rot_a.set(r, ev_mm)
        ev = tvop(lambda e: e.reduce_sum(pbank[5][0:1, 40:41], pbank[2][0:1, :], axis=AX.X),
                  pbank[5][0:1, 40:41], t_wsc[:, 0:1], wait=(ev_mm,))
        ev = tvop(lambda e, nrow_t=nrow_t, ncols=ncols: e.tensor_scalar(
            pbank[5][0:1, 41:42], t_wsc[:, 0:1], 1.0 / (nrow_t * 128 * ncols), None, ALU.mult),
            pbank[5][0:1, 41:42], t_wsc[:, 1:2], wait=(ev,))
        ev = tvop(lambda e: e.tensor_scalar(pbank[5][0:1, 42:43], t_wsc[:, 1:2], EPS_Q, None, ALU.max),
                  pbank[5][0:1, 42:43], t_wsc[:, 2:3], wait=(ev,))
        ev_r = tvop(lambda e: e.reciprocal(pbank[5][0:1, 43:44], t_wsc[:, 2:3]),
                    pbank[5][0:1, 43:44], t_wsc[:, 3:4], wait=(ev,))
        ev_m1 = S_.op("t", lambda e: e.matmul(pbank[2][:, 0:1], t_ones_r[:], t_wsc[:, 2:3],
                                              start=True, stop=True), wait=(ev_r,), inc="t")
        ev_c1 = S_.op("s", lambda e, dst_w=dst_w: e.copy(dst_w, pbank[2][:, 0:1]),
                      wait=(ev_m1,), inc="s")
        ev_m2 = S_.op("t", lambda e: e.matmul(pbank[2][:, 1:2], t_ones_r[:], t_wsc[:, 3:4],
                                              start=True, stop=True), wait=(ev_c1,), inc="t")
        return S_.op("v", lambda e, dst_r=dst_r: e.tensor_copy(dst_r, pbank[2][:, 1:2]),
                     wait=(ev_m2,), inc="v")

    S_.barrier(["t", "s", "v", "g", "y"])
    _ew = weight_mean("wupt", 16, F2, t_wupb[:], t_rupb[:])
    ev = tvop(lambda e: e.tensor_scalar(pbank[5][:, 16:24], t_q1m[:, 0:8], t_wupb[:, 0:1],
                                        C1 / QBITS, ALU.mult, ALU.mult),
              pbank[5][:, 16:24], t_st2[:, 16:24], wait=(_ew,))
    tvop(lambda e: e.tensor_tensor(pbank[5][:, 24:32], t_st2[:, 16:24], t_st2[:, 16:24],
                                   ALU.mult),
         pbank[5][:, 24:32], t_d2[:, 0:8], wait=(ev,))
    S_.barrier(["t", "s", "v", "g"])
    al.release("w_stg0", "w_stg1", "w_abs0", "w_abs1")

    t_wqsl = [SBA(f"u_wq{i}", [128, 16 * 512], BF16) for i in range(2)]
    t_fstg = [SBA(f"u_fs{i}", [128, 512], F32) for i in range(4)]
    t_qstg = SBA("u_qs", [128, 512], F32)
    t_h = SBA("u_h", [128, 32 * 1024], BF16)
    t_rl2 = SBA("u_rl2", [128, 512], BF16)
    rot_wsl = Rot(2)
    rot_fst = Rot(4)
    rot_qst = Rot(1)
    rot_hb = Rot(2)
    rot_rlb = Rot(2)
    for slab in range(8):
        wsl = t_wqsl[slab % 2]
        ev_q3 = None
        for dc in range(16):
            ev_ld = S_.op("y", lambda e, dc=dc, slab=slab: e.dma_start(
                out=t_fstg[dc % 4][:], in_=din["wupt"][dc * 128:(dc + 1) * 128,
                                                       slab * 512:(slab + 1) * 512]),
                wait=rot_fst.wait(dc) + (rot_wsl.wait(slab) if dc == 0 else ()),
                inc="ydma", dma=True)
            ev_q1 = S_.op("s", lambda e, dc=dc: e.activation(
                t_qstg[:], t_fstg[dc % 4][:], AF.Copy, scale=t_rupb[:]),
                wait=(ev_ld,) + rot_qst.wait(0), inc="s")
            rot_fst.set(dc, ev_q1)
            _e = S_.op("v", lambda e: e.tensor_scalar(t_qstg[:], t_qstg[:], MROUND, MROUND,
                                                      ALU.add, ALU.subtract), wait=(ev_q1,), inc="v")
            ev_q3 = S_.op("v", lambda e, wsl=wsl, dc=dc: e.tensor_scalar(
                wsl[:, dc * 512:(dc + 1) * 512], t_qstg[:], 1.0, -1.0, ALU.min, ALU.max),
                wait=(_e,), inc="v")
            rot_qst.set(0, ev_q3)
        ev_last_mm = None
        for f2c in range(4):
            for tc in range(2):
                gi = slab * 8 + f2c * 2 + tc
                bank = gi % 2
                ev_mm = None
                for dc in range(16):
                    w = (rot_hb.wait(gi) if dc == 0 else ())
                    if dc == 0 and f2c == 0 and tc == 0:
                        w = w + (ev_q3,)
                    ev_mm = S_.op("t", lambda e, bank=bank, dc=dc, f2c=f2c, tc=tc, wsl=wsl: e.matmul(
                        pbank[bank][:],
                        wsl[:, dc * 512 + f2c * 128: dc * 512 + (f2c + 1) * 128],
                        t_xq1[:, dc * NQ + tc * 512: dc * NQ + (tc + 1) * 512],
                        start=(dc == 0), stop=(dc == 15)),
                        wait=w, inc="t" if dc == 15 else None)
                ev_last_mm = ev_mm
                fcg = slab * 4 + f2c
                rl = t_rl if gi % 2 == 0 else t_rl2
                ev_rl = S_.op("s", lambda e, bank=bank, rl=rl: e.activation(
                    rl[:], pbank[bank][:], AF.Relu), wait=(ev_mm,) + rot_rlb.wait(gi), inc="s")
                ev_h = S_.op("v", lambda e, fcg=fcg, tc=tc, rl=rl: e.tensor_tensor(
                    t_h[:, fcg * 1024 + tc * 512: fcg * 1024 + (tc + 1) * 512],
                    rl[:], rl[:], ALU.mult), wait=(ev_rl,), inc="v")
                rot_hb.set(gi, ev_h)
                rot_rlb.set(gi, ev_h)
        rot_wsl.set(slab, ev_last_mm)
    S_.barrier(["t", "s", "v", "y", "g"])
    al.release("f_xq1", "u_wq0", "u_wq1", "u_fs0", "u_fs1", "u_fs2", "u_fs3",
               "u_qs", "u_rl2")

    if dbg == "h":
        o = dbg_out("d_h", [128, 32 * 1024])
        for _eng in ("v", "s", "t", "y"):
            if S_.last[_eng]:
                S_.wait("g", S_.last[_eng])
        o3 = dbg_out("d_wsc", [128, 1])
        o4 = dbg_out("d_wscrow", [1, 8])
        o5 = dbg_out("d_colsum", [1, 512])
        S_.op("g", lambda e, o5=o5: e.dma_start(out=o5, in_=t_B[0:1, 0:512]), inc="o", dma=True)
        S_.op("g", lambda e, o=o: e.dma_start(out=o, in_=t_h[:]), inc="o", dma=True)
        S_.op("g", lambda e, o3=o3: e.dma_start(out=o3, in_=t_rupb[:]), inc="o", dma=True)
        S_.op("g", lambda e, o4=o4: e.dma_start(out=o4, in_=t_wsc[:]), inc="o", dma=True)
        S_.wait("g", S_.ev("o"))
        return _finish(nc, ctx, S_, ddbg)

    t_pong = SBA("x_pong", [128, 32 * 1024], BF16)
    rot_f2 = Rot(2)
    for fc in range(32):
        for half in range(2):
            i6 = fc * 2 + half
            bank = i6 % 2
            ev_mm = S_.op("t", lambda e, fc=fc, half=half, bank=bank: e.matmul(
                pbank[bank][:], t_h128bf[:],
                t_h[:, fc * 1024 + half * 512: fc * 1024 + (half + 1) * 512],
                start=True, stop=True), wait=rot_f2.wait(i6), inc="t")
            ev_e = S_.op("s", lambda e, fc=fc, half=half, bank=bank: e.copy(
                t_h[:, fc * 1024 + half * 512: fc * 1024 + (half + 1) * 512], pbank[bank][:]),
                wait=(ev_mm,), inc="s")
            rot_f2.set(i6, ev_e)
    S_.barrier(["t", "s", "v"])
    bufs2 = [t_h, t_pong]
    _e = None
    for si, stt in enumerate([16, 8, 4, 2, 1]):
        src_v = bufs2[si % 2][:].rearrange("p (g two i t) -> p g two i t", two=2, i=stt, t=1024)
        dst_v = bufs2[(si + 1) % 2][:].rearrange("p (g two i t) -> p g two i t", two=2, i=stt, t=1024)
        S_.op("v", lambda e, s=src_v, d=dst_v: e.tensor_tensor(
            d[:, :, 0], s[:, :, 0], s[:, :, 1], ALU.add), wait=(_e,) if _e else ())
        _e = S_.op("v", lambda e, s=src_v, d=dst_v: e.tensor_tensor(
            d[:, :, 1], s[:, :, 0], s[:, :, 1], ALU.subtract), inc="v")
    t_F = t_pong
    t_xq2 = t_h
    t_trbf2 = t_st[:, 0:512].bitcast(BF16)
    _e = S_.op("v", lambda e: e.tensor_tensor(t_trbf[:], t_F[:, 0:1024], t_F[:, 1024:2048], ALU.max),
               inc="v")
    for c in range(2, 32):
        _e = S_.op("v", lambda e, c=c: e.tensor_tensor(
            t_trbf[:], t_trbf[:], t_F[:, c * 1024:(c + 1) * 1024], ALU.max), wait=(_e,), inc="v")
    _e = S_.op("v", lambda e: e.tensor_tensor(t_trbf2, t_F[:, 0:1024], t_F[:, 1024:2048], ALU.min),
               wait=(_e,), inc="v")
    for c in range(2, 32):
        _e = S_.op("v", lambda e, c=c: e.tensor_tensor(
            t_trbf2, t_trbf2, t_F[:, c * 1024:(c + 1) * 1024], ALU.min), wait=(_e,), inc="v")
    _e = S_.op("v", lambda e: e.tensor_scalar(t_trbf2, t_trbf2, -1.0, None, ALU.mult),
               wait=(_e,), inc="v")
    ev_tr2 = S_.op("v", lambda e: e.tensor_tensor(t_trbf[:], t_trbf[:], t_trbf2, ALU.max),
                   wait=(_e,), inc="v")
    ev_tp = None
    for j in range(8):
        ev_tp = S_.op("t", lambda e, j=j: e.transpose(
            pbank[6][:].bitcast(BF16)[:, j * 128:(j + 1) * 128],
            t_trbf[:, j * 128:(j + 1) * 128], t_idbf[:]),
            wait=(ev_tr2,) if j == 0 else (), inc="t" if j == 7 else None)
    ev_q2m = None
    for j in range(8):
        ev_q2m = S_.op("v", lambda e, j=j: e.reduce_max(
            pbank[5][:, j:j + 1], pbank[6][:].bitcast(BF16)[:, j * 128:(j + 1) * 128], axis=AX.X),
            wait=(ev_tp,) if j == 0 else (), inc="v" if j == 7 else None)
    ev_q2m = vg()
    ev_q2m = S_.op("s", lambda e: e.copy(t_q2m[:, 0:8], pbank[5][:, 0:8]),
                   wait=(ev_q2m,), inc="s")
    ev = tvop(lambda e: e.tensor_tensor(pbank[5][:, 8:16], t_q2m[:, 0:8], t_d2[:, 0:8], ALU.mult),
              pbank[5][:, 8:16], t_st2[:, 24:32], wait=(ev_q2m,))
    ev = tvop(lambda e: e.tensor_scalar(pbank[5][:, 16:24], t_st2[:, 24:32], C2, EPS_Q,
                                        ALU.mult, ALU.max),
              pbank[5][:, 16:24], t_q2m[:, 0:8], wait=(ev,))
    ev = tvop(lambda e: e.reciprocal(pbank[5][:, 8:16], t_q2m[:, 0:8]),
              pbank[5][:, 8:16], t_st2[:, 24:32], wait=(ev,))
    ev = tvop(lambda e: e.tensor_tensor(pbank[5][:, 16:24], t_st2[:, 24:32], t_d2[:, 0:8],
                                        ALU.mult),
              pbank[5][:, 16:24], t_st2[:, 24:32], wait=(ev,))
    ev_s2 = tvop(lambda e: e.tensor_scalar(pbank[5][:, 8:16], t_st2[:, 24:32],
                                           C2 * QBITS, None, ALU.mult),
                 pbank[5][:, 8:16], t_st2[:, 8:16], wait=(ev,))
    if dbg == "z2":
        o8 = dbg_out("d_q2m_postE", [128, 16])
        S_.wait("g", ev_s2)
        S_.op("g", lambda e, o8=o8: e.dma_start(out=o8, in_=t_q2m[:]), inc="gdma", dma=True)
    ev_tp2 = S_.op("t", lambda e: e.transpose(
        pbank[7][0:8, 0:128], t_st2[:, 8:16], t_idf[:]), wait=(ev_s2,), inc="t")
    ev_rT = S_.op("v", lambda e: e.tensor_copy(t_rowT[:], pbank[7][0:8, 0:128]),
                  wait=(ev_tp2,), inc="v")
    ev_b2 = [None, None]
    for half in range(2):
        for j in range(4):
            jj = half * 4 + j
            ev_b2[half] = S_.op("t", lambda e, jj=jj, half=half, j=j: e.matmul(
                pbank[6 + half][:, j * 128:(j + 1) * 128],
                t_sel8[:, jj * 128:(jj + 1) * 128], t_rowT[:],
                start=True, stop=True),
                wait=(ev_rT,) if jj == 0 else (), inc="t" if j == 3 else None)
    ev_B2 = None
    for half in range(2):
        ev_B2 = S_.op("v", lambda e, half=half: e.tensor_copy(
            t_B[:, half * 512:(half + 1) * 512], pbank[6 + half][:]),
            wait=(ev_b2[half],), inc="v" if half == 1 else None)
    ev_x2 = ev_B2
    for fc in range(32):
        ev_x2 = S_.op("v", lambda e, fc=fc: e.tensor_tensor(
            t_st[:, 0:1024], t_F[:, fc * 1024:(fc + 1) * 1024], t_B[:, 0:1024], ALU.mult),
            wait=(ev_x2,), inc="v")
        ev_x2 = S_.op("v", lambda e, fc=fc: e.tensor_scalar(
            t_xq2[:, fc * 1024:(fc + 1) * 1024], t_st[:, 0:1024], MROUND, MROUND,
            ALU.add, ALU.subtract), wait=(ev_x2,), inc="v")
    S_.barrier(["t", "s", "v", "g", "y"])
    al.release("x_pong")

    if dbg == "xq2":
        o = dbg_out("d_xq2", [128, 32 * 1024])
        o2 = dbg_out("d_q2m", [128, 16])
        for _eng in ("v", "s", "t", "y"):
            if S_.last[_eng]:
                S_.wait("g", S_.last[_eng])
        S_.op("g", lambda e, o=o: e.dma_start(out=o, in_=t_xq2[:]), inc="o", dma=True)
        S_.op("g", lambda e, o2=o2: e.dma_start(out=o2, in_=t_q2m[:]), inc="o", dma=True)
        S_.wait("g", S_.ev("o"))
        return _finish(nc, ctx, S_, ddbg)

    if dbg == "z2":
        o7 = dbg_out("d_q2m_pre", [128, 16])
        S_.op("y", lambda e, o7=o7: e.dma_start(out=o7, in_=t_q2m[:]), inc="ydma", dma=True)

    t_wstg = [SBA(f"w2_stg{i}", [128, F2], BF16) for i in range(2)]
    t_wabs = [SBA(f"w2_abs{i}", [128, F2], BF16) for i in range(2)]

    def weight_mean2(dname, nrow_t, ncols, dst_w, dst_r):
        rot_s = Rot(2)
        rot_a = Rot(2)
        nmm = ncols // 512
        ev_mm = None
        for r in range(nrow_t):
            stg = t_wstg[r % 2]
            ev_ld = S_.op("g", lambda e, stg=stg, r=r: e.dma_start(
                out=stg[:, 0:ncols], in_=din[dname][r * 128:(r + 1) * 128, :]),
                wait=rot_s.wait(r), inc="gdma", dma=True)
            wab = t_wabs[r % 2]
            ev_ab = S_.op("s", lambda e, stg=stg, ncols=ncols, wab=wab: e.activation(
                wab[:, 0:ncols], stg[:, 0:ncols], AF.Abs),
                wait=(ev_ld,) + rot_a.wait(r), inc="s")
            rot_s.set(r, ev_ab)
            for n in range(nmm):
                ev_mm = S_.op("t", lambda e, n=n, r=r, nrow_t=nrow_t, nmm=nmm, wab=wab: e.matmul(
                    pbank[2][0:1, :], t_ones_c[:], wab[:, n * 512:(n + 1) * 512],
                    start=(r == 0 and n == 0), stop=(r == nrow_t - 1 and n == nmm - 1)),
                    wait=(ev_ab,) if n == 0 else (), inc="t" if n == nmm - 1 else None)
            rot_a.set(r, ev_mm)
        ev = tvop(lambda e: e.reduce_sum(pbank[5][0:1, 40:41], pbank[2][0:1, :], axis=AX.X),
                  pbank[5][0:1, 40:41], t_wsc[:, 0:1], wait=(ev_mm,))
        ev = tvop(lambda e, nrow_t=nrow_t, ncols=ncols: e.tensor_scalar(
            pbank[5][0:1, 41:42], t_wsc[:, 0:1], 1.0 / (nrow_t * 128 * ncols), None, ALU.mult),
            pbank[5][0:1, 41:42], t_wsc[:, 1:2], wait=(ev,))
        ev = tvop(lambda e: e.tensor_scalar(pbank[5][0:1, 42:43], t_wsc[:, 1:2], EPS_Q, None, ALU.max),
                  pbank[5][0:1, 42:43], t_wsc[:, 2:3], wait=(ev,))
        ev_r = tvop(lambda e: e.reciprocal(pbank[5][0:1, 43:44], t_wsc[:, 2:3]),
                    pbank[5][0:1, 43:44], t_wsc[:, 3:4], wait=(ev,))
        ev_m1 = S_.op("t", lambda e: e.matmul(pbank[2][:, 0:1], t_ones_r[:], t_wsc[:, 2:3],
                                              start=True, stop=True), wait=(ev_r,), inc="t")
        ev_c1 = S_.op("s", lambda e, dst_w=dst_w: e.copy(dst_w, pbank[2][:, 0:1]),
                      wait=(ev_m1,), inc="s")
        ev_m2 = S_.op("t", lambda e: e.matmul(pbank[2][:, 1:2], t_ones_r[:], t_wsc[:, 3:4],
                                              start=True, stop=True), wait=(ev_c1,), inc="t")
        return S_.op("v", lambda e, dst_r=dst_r: e.tensor_copy(dst_r, pbank[2][:, 1:2]),
                     wait=(ev_m2,), inc="v")

    _ew = weight_mean2("wdnt", 32, 2048, t_wdnb[:], t_rdnb[:])
    tvop(lambda e: e.tensor_scalar(pbank[5][:, 24:32], t_q2m[:, 0:8], t_wdnb[:, 0:1],
                                   1.0 / QBITS, ALU.mult, ALU.mult),
         pbank[5][:, 24:32], t_dq2[:, 0:8], wait=(_ew,))
    S_.barrier(["t", "s", "v", "g"])
    al.release("w2_stg0", "w2_stg1", "w2_abs0", "w2_abs1")

    t_wdsl = [SBA(f"d_wd{i}", [128, 32 * 512], BF16) for i in range(2)]
    t_fstg2 = [SBA(f"d_fs{i}", [128, 512], F32) for i in range(4)]
    t_qstg2 = SBA("d_qs", [128, 512], F32)
    t_ystg = [SBA(f"d_ys{i}", [128, 512], F32) for i in range(2)]
    rot_wsl2 = Rot(2)
    rot_fst2 = Rot(4)
    rot_qst2 = Rot(1)
    rot_yst2 = Rot(2)
    rot_db = Rot(2)
    for fsl in range(4):
        wsl = t_wdsl[fsl % 2]
        ev_q3 = None
        for fc in range(32):
            ev_ld = S_.op("y", lambda e, fc=fc, fsl=fsl: e.dma_start(
                out=t_fstg2[fc % 4][:], in_=din["wdnt"][fc * 128:(fc + 1) * 128,
                                                        fsl * 512:(fsl + 1) * 512]),
                wait=rot_fst2.wait(fc) + (rot_wsl2.wait(fsl) if fc == 0 else ()),
                inc="ydma", dma=True)
            ev_q1 = S_.op("s", lambda e, fc=fc: e.activation(
                t_qstg2[:], t_fstg2[fc % 4][:], AF.Copy, scale=t_rdnb[:]),
                wait=(ev_ld,) + rot_qst2.wait(0), inc="s")
            rot_fst2.set(fc, ev_q1)
            _e = S_.op("v", lambda e: e.tensor_scalar(t_qstg2[:], t_qstg2[:], MROUND, MROUND,
                                                      ALU.add, ALU.subtract), wait=(ev_q1,), inc="v")
            ev_q3 = S_.op("v", lambda e, wsl=wsl, fc=fc: e.tensor_scalar(
                wsl[:, fc * 512:(fc + 1) * 512], t_qstg2[:], 1.0, -1.0, ALU.min, ALU.max),
                wait=(_e,), inc="v")
            rot_qst2.set(0, ev_q3)
        ev_last = None
        for tt in range(8):
            ti = fsl * 8 + tt
            ys = t_ystg[tt % 2]
            ev_y = S_.op("y", lambda e, ys=ys, tt=tt, fsl=fsl: e.dma_start(
                out=ys[:], in_=d_yln[tt * 128:(tt + 1) * 128, fsl * 512:(fsl + 1) * 512]),
                wait=rot_yst2.wait(tt), inc="ydma", dma=True)
            bank = ti % 2
            ev_mm = None
            for fc in range(32):
                w = (rot_db.wait(ti) if fc == 0 else ())
                if fc == 0 and tt == 0:
                    w = w + (ev_q3,)
                ev_mm = S_.op("t", lambda e, bank=bank, fc=fc, tt=tt, wsl=wsl: e.matmul(
                    pbank[bank][:],
                    t_xq2[:, fc * 1024 + tt * 128: fc * 1024 + tt * 128 + 128],
                    wsl[:, fc * 512:(fc + 1) * 512],
                    start=(fc == 0), stop=(fc == 31)),
                    wait=w, inc="t" if fc == 31 else None)
            ev_st = S_.op("v", lambda e, bank=bank, ys=ys, tt=tt: e.scalar_tensor_tensor(
                ys[:], pbank[bank][:], t_dq2[:, tt:tt + 1], ys[:], ALU.mult, ALU.add),
                wait=(ev_mm, ev_y), inc="v")
            rot_db.set(ti, ev_st)
            ev_wb = S_.op("y", lambda e, ys=ys, tt=tt, fsl=fsl: e.dma_start(
                out=d_yln[tt * 128:(tt + 1) * 128, fsl * 512:(fsl + 1) * 512], in_=ys[:]),
                wait=(ev_st,), inc="ydma", dma=True)
            rot_yst2.set(tt, ev_wb)
            ev_last = ev_wb
        rot_wsl2.set(fsl, ev_last)
    S_.barrier(["t", "s", "v", "y", "g"])

    if dbg == "z2":
        o = dbg_out("d_z2", [NQ, D])
        o2 = dbg_out("d_dq2", [128, 16])
        o5 = dbg_out("d_q2m", [128, 16])
        o6 = dbg_out("d_s2", [128, 128])
        S_.op("y", lambda e, o5=o5: e.dma_start(out=o5, in_=t_q2m[:]), inc="o", dma=True)
        S_.op("y", lambda e, o6=o6: e.dma_start(out=o6, in_=t_st2[:]), inc="o", dma=True)
        S_.op("y", lambda e, o=o: e.dma_start(out=o, in_=d_yln), inc="o", dma=True)
        S_.op("y", lambda e, o2=o2: e.dma_start(out=o2, in_=t_dq2[:]), inc="o", dma=True)
        S_.wait("y", S_.ev("o"))
        return _finish(nc, ctx, S_, ddbg)

    al.release("u_h", "d_wd0", "d_wd1", "d_fs0", "d_fs1", "d_fs2", "d_fs3",
               "d_qs", "d_ys0", "d_ys1")
    t_z = [SBA(f"z_t{t}", [128, 2048], F32) for t in range(8)]
    qnames = ["y", "g", "s"]
    qinc = {"y": "ydma", "g": "gdma", "s": "sdma"}
    ev_zld = []
    for t in range(8):
        q = qnames[t % 3]
        ev_zld.append(S_.op(q, lambda e, t=t: e.dma_start(
            out=t_z[t][:], in_=d_yln[t * 128:(t + 1) * 128, :]), inc=qinc[q], dma=True))
    for t in range(8):
        for seg in range(4):
            S_.op("v", lambda e, t=t, seg=seg: e.bn_stats(
                t_st[:, t * 24 + seg * 6: t * 24 + (seg + 1) * 6],
                t_z[t][:, seg * 512:(seg + 1) * 512]), wait=(ev_zld[t],))
    ch8 = [vg()]

    def vq8(fn, extra=()):
        S_.op("v", fn, wait=(ch8[0],) + tuple(extra), inc="v")
        ch8[0] = vg()
        return ch8[0]

    stv = t_st[:, 0:192].rearrange("p (t g c) -> p t g c", g=8, c=3)
    M1, VA = t_st[:, 192:200], t_st[:, 200:208]
    MS, R0 = t_st[:, 272:280], t_st[:, 280:288]
    SQ, NW = t_st[:, 288:296], t_st[:, 296:304]
    INV, BIA = t_st[:, 304:312], t_st[:, 312:320]
    SCR8 = t_st[:, 336:400].rearrange("p (t g) -> p t g", g=8)
    r81 = lambda ap: ap.rearrange("p (t o) -> p t o", o=1)
    vq8(lambda e: e.reduce_sum(r81(M1), stv[:, :, :, 1], axis=AX.X))
    vq8(lambda e: e.tensor_scalar(M1, M1, 1.0 / 8.0, None, ALU.mult))
    vq8(lambda e: e.reduce_sum(r81(VA), stv[:, :, :, 2], axis=AX.X))
    vq8(lambda e: e.tensor_tensor(SCR8, stv[:, :, :, 1], stv[:, :, :, 1], ALU.mult))
    vq8(lambda e: e.reduce_sum(r81(MS), SCR8, axis=AX.X))
    vq8(lambda e: e.tensor_scalar(MS, MS, 256.0, None, ALU.mult))
    vq8(lambda e: e.tensor_add(VA, VA, MS))
    vq8(lambda e: e.tensor_scalar(VA, VA, 1.0 / 2048.0, None, ALU.mult))
    vq8(lambda e: e.tensor_tensor(MS, M1, M1, ALU.mult))
    vq8(lambda e: e.tensor_sub(VA, VA, MS))
    vq8(lambda e: e.tensor_scalar(VA, VA, EPS_LN, None, ALU.add))
    ev_v8 = vq8(lambda e: e.reciprocal(R0, VA))
    ev_sq8 = S_.op("s", lambda e: e.activation(SQ, R0, AF.Sqrt), wait=(ev_v8,), inc="s")
    ch8[0] = ev_sq8
    vq8(lambda e: e.tensor_tensor(NW, SQ, SQ, ALU.mult))
    vq8(lambda e: e.tensor_mul(NW, NW, VA))
    vq8(lambda e: e.tensor_scalar(NW, NW, -1.0, 3.0, ALU.mult, ALU.add))
    vq8(lambda e: e.tensor_mul(NW, NW, SQ))
    vq8(lambda e: e.tensor_scalar(INV, NW, 0.5, None, ALU.mult))
    vq8(lambda e: e.tensor_mul(BIA, M1, INV))
    ev_b8 = vq8(lambda e: e.tensor_scalar(BIA, BIA, -1.0, None, ALU.mult))
    for t in range(8):
        ev_n = S_.op("v", lambda e, t=t: e.scalar_tensor_tensor(
            t_z[t][:], t_z[t][:], INV[:, t:t + 1], BIA[:, t:t + 1].broadcast_to([128, 2048]),
            ALU.mult, ALU.add), wait=(ev_b8,) if t == 0 else (), inc="v")
        q = qnames[t % 3]
        S_.op(q, lambda e, t=t: e.dma_start(
            out=dout[t * 128:(t + 1) * 128, :], in_=t_z[t][:]), wait=(ev_n,), inc="o", dma=True)
    S_.wait("y", S_.ev("o"))
    return _finish(nc, ctx, S_, ddbg)


def _finish(nc, ctx, S_, ddbg):
    with nc.Block() as blk:
        @blk.tensor
        def _(e):
            S_.replay("t", e)

        @blk.vector
        def _(e):
            S_.replay("v", e)

        @blk.scalar
        def _(e):
            S_.replay("s", e)

        @blk.gpsimd
        def _(e):
            S_.replay("g", e)

        @blk.sync
        def _(e):
            S_.replay("y", e)
    ctx.close()
    return nc, ddbg


def _host_inputs(inputs, core):
    b, stripe = core // 2, core % 2
    qb0, qb1 = _core_qrows(stripe)
    qrows = np.concatenate([qb0, qb1])
    x = np.ascontiguousarray(inputs["L0_attn_hidden_states"][b])
    xq = np.ascontiguousarray(x[qrows])
    pe, po = _col_perm_rope_q()
    ke, ko = _col_perm_rope_k()
    cosq, sinq = _rope_tables(qrows)
    cosk, sink = _rope_tables(np.arange(S))
    wo_rows = np.concatenate([np.arange(h * 192, h * 192 + 128) for h in range(H)])
    return {
        "x": x, "xq": xq,
        "wq": inputs["L0_attn_Wq"],
        "wqe": np.ascontiguousarray(inputs["L0_attn_Wqrope"][:, pe]),
        "wqo": np.ascontiguousarray(inputs["L0_attn_Wqrope"][:, po]),
        "wkvc": inputs["L0_attn_Wkvc"],
        "wkdec": inputs["L0_attn_Wkdec"],
        "wvdec": inputs["L0_attn_Wvdec"],
        "wke": np.ascontiguousarray(inputs["L0_attn_Wkrope"][:, ke]),
        "wko": np.ascontiguousarray(inputs["L0_attn_Wkrope"][:, ko]),
        "wo": np.ascontiguousarray(inputs["L0_attn_Wo"][wo_rows]),
        "wupt": np.ascontiguousarray(inputs["L0_ffn_w_up"].T),
        "wdnt": np.ascontiguousarray(inputs["L0_ffn_w_down"].T),
        "cosq": cosq, "sinq": sinq, "cosk": cosk, "sink": sink,
        "mask": np.asarray(_core_masks(stripe)),
    }


def kernel(**inputs):
    nc, _ = build_program(dbg=None)
    in_maps = [_host_inputs(inputs, c) for c in range(8)]
    res = run_bass_kernel_spmd(nc, in_maps, core_ids=list(range(8)))
    out = np.zeros((B, S, D), dtype=np.float32)
    for c in range(8):
        b, stripe = c // 2, c % 2
        qb0, qb1 = _core_qrows(stripe)
        o = res.results[c]["out"]
        out[b, qb0] = o[:512]
        out[b, qb1] = o[512:]
    return out
